# version 13
# speedup vs baseline: 1.1552x; 1.0320x over previous
import contextlib
import math

import numpy as np

import concourse.bass as bass
import concourse.mybir as mybir
from concourse.alu_op_type import AluOpType as OP
from concourse.bass_utils import run_bass_kernel_spmd

F32 = mybir.dt.float32
F32R = mybir.dt.float32r
BF16 = mybir.dt.bfloat16
I32 = mybir.dt.int32
AF = mybir.ActivationFunctionType

P = 128
DM = 2048
FI = DM // P
TOWN = 1024
NOC = TOWN // P
TPREF = 3072
NPC = TPREF // P
NCH = NPC + NOC
KVH = 4
H = 16
IDIM = 8192
NCB = 16
CBG = 4
EPS_LN = 1e-5
EPS_ATT = 1e-6
PB = 3.141592502593994
TWO_PI = 2 * math.pi
CW1 = 6.28125
CW2 = float(np.float32(TWO_PI - CW1))
INV_DM = 1.0 / DM

DEBUG_TAPS = []


class Ctr:
    def __init__(self, es, nc, name):
        self.sem = es.enter_context(nc.semaphore(name))
        self.v = 0

    def inc(self, ins, n=1):
        ins.then_inc(self.sem, n)
        self.v += n
        return self.v

    def dma(self, ins):
        return self.inc(ins, 16)

    def wait(self, eng, v=None):
        eng.wait_ge(self.sem, self.v if v is None else v)


def build_program():
    nc = bass.Bass("TRN2", target_bir_lowering=False, debug=False)

    d_xt = nc.dram_tensor("xt", [NCH, P, FI, P], F32R, kind="ExternalInput")
    d_pos = nc.dram_tensor("pos", [P, NCH], F32, kind="ExternalInput")
    d_posr = nc.dram_tensor("posrep", [P, NCH * 64], F32, kind="ExternalInput")
    d_mask = nc.dram_tensor("maskc", [P, NCH], BF16, kind="ExternalInput")
    d_ivh = nc.dram_tensor("ivh", [P, 64], F32, kind="ExternalInput")
    d_ivl = nc.dram_tensor("ivl", [P, 64], F32, kind="ExternalInput")
    d_idf = nc.dram_tensor("identf", [P, P], F32, kind="ExternalInput")
    d_idb = nc.dram_tensor("identb", [P, P], BF16, kind="ExternalInput")
    d_cmask = nc.dram_tensor("cmask4", [P, 4 * P], F32, kind="ExternalInput")
    d_onesr = nc.dram_tensor("onesr", [1, P], F32R, kind="ExternalInput")
    d_onesc = nc.dram_tensor("onescol", [P, 1], F32R, kind="ExternalInput")
    d_onescb = nc.dram_tensor("onescolb", [P, 1], BF16, kind="ExternalInput")
    d_wk = nc.dram_tensor("wk", [P, FI, 512], F32, kind="ExternalInput")
    d_wv = nc.dram_tensor("wv", [P, FI, 512], F32, kind="ExternalInput")
    d_wq = nc.dram_tensor("wq", [4, FI, P, 512], F32, kind="ExternalInput")
    d_wo = nc.dram_tensor("wo", [FI, P, FI, P], F32, kind="ExternalInput")
    d_wg = nc.dram_tensor("wg", [NCB, 4, P, FI, P], F32, kind="ExternalInput")
    d_wu = nc.dram_tensor("wu", [NCB, 4, P, FI, P], F32, kind="ExternalInput")
    d_wd = nc.dram_tensor("wd", [8, NCB, P, 4, 256], F32, kind="ExternalInput")
    d_out = nc.dram_tensor("outT", [FI, P, TOWN], F32, kind="ExternalOutput")
    d_ao = nc.dram_tensor("aopark", [FI, P, TOWN], F32R)
    d_x2 = nc.dram_tensor("x2park", [FI, P, TOWN], F32R)
    d_st = nc.dram_tensor("st3park", [1, 2048], F32R)
    d_tap = {}
    for name in DEBUG_TAPS:
        if name == "kv":
            d_tap["tap_kf"] = nc.dram_tensor("tap_kf", [P, NOC * 512], F32, kind="ExternalOutput")
            d_tap["tap_v"] = nc.dram_tensor("tap_v", [P, NOC * 512], F32, kind="ExternalOutput")
            d_tap["tap_st"] = nc.dram_tensor("tap_st", [P, 512], F32, kind="ExternalOutput")
            d_tap["tap_z"] = nc.dram_tensor("tap_z", [P, 4], F32, kind="ExternalOutput")
        if name == "q":
            d_tap["tap_qfT"] = nc.dram_tensor("tap_qfT", [P, H * TOWN], F32, kind="ExternalOutput")
        if name == "o":
            d_tap["tap_oT"] = nc.dram_tensor("tap_oT", [P, H * TOWN], F32, kind="ExternalOutput")
        if name == "x2":
            d_tap["tap_x2"] = nc.dram_tensor("tap_x2", [FI, P, TOWN], F32, kind="ExternalOutput")
        if name == "x3":
            d_tap["tap_x3"] = nc.dram_tensor("tap_x3", [P, FI * TOWN], F32, kind="ExternalOutput")
        if name == "hs":
            d_tap["tap_hs"] = nc.dram_tensor("tap_hs", [P, 4 * TOWN], F32, kind="ExternalOutput")

    es = contextlib.ExitStack()

    ivh = es.enter_context(nc.sbuf_tensor("s_ivh", [P, 64], F32))
    ivl = es.enter_context(nc.sbuf_tensor("s_ivl", [P, 64], F32))
    posS = es.enter_context(nc.sbuf_tensor("posS", [P, NCH], F32))
    maskS = es.enter_context(nc.sbuf_tensor("maskS", [P, NCH], BF16))
    idf = es.enter_context(nc.sbuf_tensor("idf", [P, P], F32))
    idb = es.enter_context(nc.sbuf_tensor("idb", [P, P], BF16))
    onesr = es.enter_context(nc.sbuf_tensor("s_onesr", [1, P], F32R))
    onesc = es.enter_context(nc.sbuf_tensor("onesc", [P, 1], F32R))
    onescb = es.enter_context(nc.sbuf_tensor("onescb", [P, 1], BF16))

    ld = Ctr(es, nc, "c_ld")
    gq = Ctr(es, nc, "c_gq")
    vq = Ctr(es, nc, "c_vq")
    pe = Ctr(es, nc, "c_pe")
    ac = Ctr(es, nc, "c_ac")
    dv = Ctr(es, nc, "c_dv")
    gqp = [Ctr(es, nc, f"c_gqp{i}") for i in range(8)]
    crq = [Ctr(es, nc, f"c_crq{i}") for i in range(2)]
    gqw = [Ctr(es, nc, f"c_gqw{i}") for i in range(4)]
    ldr = [Ctr(es, nc, f"c_ldr{i}") for i in range(2)]
    ldp = [Ctr(es, nc, f"c_ldp{i}") for i in range(2)]

    L_A = contextlib.ExitStack()
    L_B = contextlib.ExitStack()
    stp = L_B.enter_context(nc.sbuf_tensor("stp", [P, 512], F32))
    zpS = L_B.enter_context(nc.sbuf_tensor("zpS", [P, 4], F32))
    cmask4 = L_B.enter_context(nc.sbuf_tensor("s_cmask4", [P, 4 * P], F32))
    kf_tm = L_B.enter_context(nc.sbuf_tensor("kf_tm", [P, NOC * 512], BF16))
    kfT = L_B.enter_context(nc.sbuf_tensor("kfT", [P, KVH * TOWN], BF16))
    v_tm = L_B.enter_context(nc.sbuf_tensor("v_tm", [P, NOC * 512], BF16))
    rcolA = L_B.enter_context(nc.sbuf_tensor("rcolA", [P, NCH], F32R))
    rowsNm8 = L_B.enter_context(nc.sbuf_tensor("rowsNm8", [1, NOC * P], F32R))

    L_C = contextlib.ExitStack()
    sinT = L_C.enter_context(nc.sbuf_tensor("sinT", [P, NCH * 64], F32))
    cosT = L_C.enter_context(nc.sbuf_tensor("cosT", [P, NCH * 64], F32))
    konerow = L_C.enter_context(nc.sbuf_tensor("konerow", [1, 1024], F32R))
    xob = L_C.enter_context(nc.sbuf_tensor("xob", [P, NOC * DM], BF16))

    def xob_tile(c):
        return xob[:, c * DM:(c + 1) * DM]

    def col_bc(t, c, n):
        ap = t[:, c:c + 1]
        return bass.AP(ap.tensor, ap.offset, [[ap.ap[0][0], P], [0, n]])

    def strided_half(t, base_off, blocks, blk_stride, width):
        ap = t[:]
        return bass.AP(ap.tensor, ap.offset + base_off,
                       [[ap.ap[0][0], P], [blk_stride, blocks], [1, width]])

    def rope_bc(base, c):
        ap = base[:]
        return bass.AP(ap.tensor, ap.offset + c * 64,
                       [[ap.ap[0][0], P], [0, 8], [1, 64]])

    def crep_src(c, which):
        base = cosT if which == 0 else sinT
        ap = base[:]
        return bass.AP(ap.tensor, ap.offset + c * 64,
                       [[NCH * 64, P], [0, 8], [1, 64]])

    with contextlib.ExitStack() as ps:
        th = ps.enter_context(nc.sbuf_tensor("p0_th", [P, NCH * 64], F32))
        uu = ps.enter_context(nc.sbuf_tensor("p0_u", [P, NCH * 64], F32))
        ki = ps.enter_context(nc.sbuf_tensor("p0_ki", [P, NCH * 64], I32))
        kw = ps.enter_context(nc.sbuf_tensor("p0_kw", [P, NCH * 64], F32))
        prp = ps.enter_context(nc.sbuf_tensor("p0_prp", [P, NCH * 64], F32))
        ivr = ps.enter_context(nc.sbuf_tensor("p0_ivr", [P, 2 * NCH * 64], F32))
        with nc.Block() as blk:
            @blk.sync
            def _(s):
                for t, d in [(ivh, d_ivh), (ivl, d_ivl), (posS, d_pos), (maskS, d_mask),
                             (idf, d_idf), (idb, d_idb), (cmask4, d_cmask),
                             (onesr, d_onesr), (onesc, d_onesc), (onescb, d_onescb)]:
                    ld.dma(s.dma_start(t[:], d.ap()))
                ld.dma(s.dma_start(prp[:], d_posr.ap()))
                ld.wait(s)
                ivh_src = bass.AP(ivh[:].tensor, ivh[:].offset,
                                  [[64, P], [0, NCH], [1, 64]])
                ivl_src = bass.AP(ivl[:].tensor, ivl[:].offset,
                                  [[64, P], [0, NCH], [1, 64]])
                ld.dma(s.dma_start(ivr[:, 0:NCH * 64].rearrange("p (c j) -> p c j", c=NCH), ivh_src))
                ld.dma(s.dma_start(ivr[:, NCH * 64:].rearrange("p (c j) -> p c j", c=NCH), ivl_src))
                ld.wait(s)

            ld_all = ld.v

            @blk.gpsimd
            def _(g):
                for c in range(NOC):
                    gq.dma(g.dma_start(
                        xob_tile(c).rearrange("p (f t) -> p f t", f=FI),
                        d_xt.ap()[NPC + c]))
                gq.wait(g)

            @blk.vector
            def _(v):
                ld.wait(v, ld_all)
                NW = NCH * 64
                dv.inc(v.tensor_tensor(th[:], prp[:], ivr[:, NW:], OP.mult))
                dv.wait(v)
                dv.inc(v.scalar_tensor_tensor(uu[:], prp[:], 1.0, ivr[:, 0:NW], OP.mult, OP.mult))
                dv.wait(v)
                dv.inc(v.tensor_tensor(th[:], uu[:], th[:], OP.add))
                dv.wait(v)
                dv.inc(v.tensor_scalar(uu[:], th[:], 1.0 / TWO_PI, None, OP.mult))
                dv.wait(v)
                dv.inc(v.tensor_copy(ki[:], uu[:]))
                dv.wait(v)
                dv.inc(v.tensor_copy(kw[:], ki[:]))
                dv.wait(v)
                dv.inc(v.scalar_tensor_tensor(th[:], kw[:], -CW1, th[:], OP.mult, OP.add))
                dv.wait(v)
                dv.inc(v.scalar_tensor_tensor(th[:], kw[:], -CW2, th[:], OP.mult, OP.add))
                dv.wait(v)
                dv.inc(v.tensor_scalar(uu[:], th[:], PB, None, OP.is_gt))
                dv.wait(v)
                dv.inc(v.scalar_tensor_tensor(th[:], uu[:], -TWO_PI, th[:], OP.mult, OP.add))
                dv.wait(v)
                dv.inc(v.tensor_scalar(uu[:], th[:], -PB, None, OP.is_lt))
                dv.wait(v)
                dv.inc(v.scalar_tensor_tensor(th[:], uu[:], TWO_PI, th[:], OP.mult, OP.add))
                dv.wait(v)
                dv.inc(v.tensor_scalar(uu[:], th[:], PB, -PB, OP.min, OP.max))
                dv.inc(v.tensor_scalar(kw[:], th[:], math.pi / 2, None, OP.add))
                dv.wait(v)
                dv.inc(v.tensor_scalar(th[:], kw[:], PB, None, OP.is_gt))
                dv.wait(v)
                dv.inc(v.scalar_tensor_tensor(kw[:], th[:], -TWO_PI, kw[:], OP.mult, OP.add))
                dv.wait(v)
                dv.inc(v.tensor_scalar(kw[:], kw[:], PB, -PB, OP.min, OP.max))

            dv_tab = dv.v

            @blk.scalar
            def _(s):
                dv.wait(s, dv_tab)
                ac.inc(s.activation(sinT[:], uu[:], AF.Sin))
                ac.inc(s.activation(cosT[:], kw[:], AF.Sin))

    with contextlib.ExitStack() as ps:
        NPB = 8
        xpb = [ps.enter_context(nc.sbuf_tensor(f"xp{i}", [P, DM], BF16)) for i in range(NPB)]
        wkb = ps.enter_context(nc.sbuf_tensor("wkb", [P, FI * 512], BF16))
        wvb = ps.enter_context(nc.sbuf_tensor("wvb", [P, FI * 512], BF16))
        krop = [ps.enter_context(nc.sbuf_tensor(f"krop{i}", [P, 512], F32)) for i in range(2)]
        kaux = [ps.enter_context(nc.sbuf_tensor(f"kaux{i}", [P, 1024], F32)) for i in range(2)]
        kfpb = [ps.enter_context(nc.sbuf_tensor(f"kfpb{i}", [P, 512], BF16)) for i in range(2)]
        vpb = [ps.enter_context(nc.sbuf_tensor(f"vpb{i}", [P, 512], BF16)) for i in range(2)]
        sqS = [ps.enter_context(nc.sbuf_tensor("sqa0", [P, DM], BF16))] * 2
        rPS = [ps.enter_context(nc.sbuf_tensor("rPS0", [1, 1024], F32R))] * 2
        rSg = [ps.enter_context(nc.sbuf_tensor(f"rSg{i}", [1, 1024], F32R)) for i in range(2)]
        rQg = [ps.enter_context(nc.sbuf_tensor(f"rQg{i}", [1, 1024], F32R)) for i in range(2)]
        gtmp = ps.enter_context(nc.sbuf_tensor("gtmp", [1, 1024], F32R))

        pkv = [ps.enter_context(nc.psum_tensor(f"pkv{i}", [P, 1024], F32)) for i in range(2)]
        pstate = ps.enter_context(nc.psum_tensor("pstate", [P, 512], F32))
        pzm = ps.enter_context(nc.psum_tensor("pzm", [P, 512], F32))
        pstA = ps.enter_context(nc.psum_tensor("pstA", [P, 1024], F32))
        pzp = pzm[:, 0:4]
        pmT = pzm[:, 256:512].bitcast(BF16)

        ck_xbld = [0] * NCH
        ck_sq = [0] * NCH
        ck_smm = [0] * NCH
        ck_sev = [0] * NCH
        ck_fold = [0] * NCH
        ck_grp = [0] * (NCH // 8)
        ck_k1use = [0] * (NCH // 8)
        ck_kv = [0] * NCH
        ck_aff = [0] * NCH
        ck_rope = [0] * NCH
        ck_emin = [0] * NCH
        ck_exp = [0] * NCH
        ck_elu = [0] * NCH
        ck_kft = [0] * NCH
        ck_kfte = [0] * NCH
        ck_state = [0] * NCH

        def xb(c):
            return xpb[c % NPB][:] if c < NPC else xob_tile(c - NPC)

        def kf_dst(c):
            if c < NPC:
                return kfpb[c % 2][:]
            return kf_tm[:, (c - NPC) * 512:(c - NPC + 1) * 512]

        def v_dst(c):
            if c < NPC:
                return vpb[c % 2][:]
            return v_tm[:, (c - NPC) * 512:(c - NPC + 1) * 512]

        with nc.Block() as blk:
            @blk.gpsimd
            def _(g):
                gq.dma(g.dma_start(wkb[:].rearrange("p (f n) -> p f n", f=FI), d_wk.ap()))
                gq.dma(g.dma_start(wvb[:].rearrange("p (f n) -> p f n", f=FI), d_wv.ap()))
                gq.wait(g)

            ck_wkv = gq.v

            @blk.tensor
            def _(t):
                gq.wait(t, ck_wkv)
                for fi in range(FI):
                    pe.inc(t.matmul(pkv[0][0:1, 0:512], onescb[:],
                                    wkb[:, fi * 512:(fi + 1) * 512],
                                    start=(fi == 0), stop=(fi == FI - 1)))
                    pe.inc(t.matmul(pkv[0][0:1, 512:1024], onescb[:],
                                    wvb[:, fi * 512:(fi + 1) * 512],
                                    start=(fi == 0), stop=(fi == FI - 1)))

            ck_one_mm = pe.v

            @blk.scalar
            def _(s):
                pe.wait(s, ck_one_mm)
                ac.inc(s.activation(konerow[:], pkv[0][0:1, :], AF.Copy))

            ck_kvone = ac.v

            pend = [None]
            pend_tail = [None]
            for grp in range(NCH // 8):
                gpar = grp % 2
                for c in range(grp * 8, grp * 8 + 8):
                    own = c >= NPC
                    spar = c % 2

                    if not own:
                        @blk.gpsimd
                        def _(g, c=c):
                            if c >= NPB:
                                pe.wait(g, ck_kv[c - NPB])
                                dv.wait(g, ck_sq[c - NPB])
                            ck_xbld[c] = gqp[c % NPB].dma(g.dma_start(
                                xpb[c % NPB][:].rearrange("p (f t) -> p f t", f=FI),
                                d_xt.ap()[c]))

                    @blk.vector
                    def _(v, c=c, spar=spar, own=own):
                        if not own:
                            gqp[c % NPB].wait(v, ck_xbld[c])
                        if c >= 1:
                            pe.wait(v, ck_smm[c - 1])
                        xbc = xb(c)
                        ck_sq[c] = dv.inc(v.tensor_tensor(sqS[spar][:], xbc, xbc,
                                                          OP.mult))

                    @blk.tensor
                    def _(t, c=c, spar=spar, own=own, grp=grp):
                        if not own:
                            gqp[c % NPB].wait(t, ck_xbld[c])
                        dv.wait(t, ck_sq[c])
                        if c >= 1:
                            ac.wait(t, ck_sev[c - 1])
                        if c % 8 == 0 and grp >= 1:
                            ac.wait(t, ck_grp[grp - 1])
                        for f4 in range(4):
                            pe.inc(t.matmul(pstA[0:1, 0:512], onescb[:],
                                            xb(c)[:, f4 * 512:(f4 + 1) * 512],
                                            start=(f4 == 0), stop=(f4 == 3)))
                            pe.inc(t.matmul(pstA[0:1, 512:1024], onescb[:],
                                            sqS[spar][:, f4 * 512:(f4 + 1) * 512],
                                            start=(f4 == 0), stop=(f4 == 3)))
                        ck_smm[c] = pe.v

                    @blk.scalar
                    def _(s, c=c, spar=spar):
                        pe.wait(s, ck_smm[c])
                        if c >= 1:
                            dv.wait(s, ck_fold[c - 1])
                        ck_sev[c] = ac.inc(s.activation(rPS[spar][:], pstA[0:1, :],
                                                        AF.Copy))

                    @blk.vector
                    def _(v, c=c, spar=spar, grp=grp, gpar=gpar):
                        ac.wait(v, ck_sev[c])
                        if grp >= 2 and c % 8 == 0:
                            pe.wait(v, ck_k1use[grp - 2])
                        ps_ = rPS[spar]
                        k = (c % 8) * P
                        sl = slice(k, k + P)
                        dv.inc(v.tensor_tensor(rSg[gpar][0:1, sl], ps_[0:1, 0:128],
                                               ps_[0:1, 128:256], OP.add))
                        dv.inc(v.tensor_tensor(rQg[gpar][0:1, sl], ps_[0:1, 512:640],
                                               ps_[0:1, 640:768], OP.add))
                        dv.wait(v)
                        dv.inc(v.scalar_tensor_tensor(rSg[gpar][0:1, sl],
                                                      ps_[0:1, 256:384], 1.0,
                                                      rSg[gpar][0:1, sl],
                                                      OP.mult, OP.add))
                        dv.inc(v.scalar_tensor_tensor(rQg[gpar][0:1, sl],
                                                      ps_[0:1, 768:896], 1.0,
                                                      rQg[gpar][0:1, sl],
                                                      OP.mult, OP.add))
                        dv.wait(v)
                        dv.inc(v.scalar_tensor_tensor(rSg[gpar][0:1, sl],
                                                      ps_[0:1, 384:512], 1.0,
                                                      rSg[gpar][0:1, sl],
                                                      OP.mult, OP.add))
                        dv.inc(v.scalar_tensor_tensor(rQg[gpar][0:1, sl],
                                                      ps_[0:1, 896:1024], 1.0,
                                                      rQg[gpar][0:1, sl],
                                                      OP.mult, OP.add))
                        ck_fold[c] = dv.v

                @blk.vector
                def _(v, grp=grp, gpar=gpar):
                    dv.wait(v, ck_fold[grp * 8 + 7])
                    dv.inc(v.tensor_scalar(rSg[gpar][:], rSg[gpar][:], -INV_DM,
                                           None, OP.mult))
                    dv.wait(v)
                    dv.inc(v.tensor_tensor(gtmp[:], rSg[gpar][:], rSg[gpar][:],
                                           OP.mult))
                    dv.wait(v)
                    dv.inc(v.scalar_tensor_tensor(rQg[gpar][:], rQg[gpar][:], INV_DM,
                                                  gtmp[:], OP.mult, OP.subtract))
                    dv.wait(v)
                    dv.inc(v.tensor_scalar(rQg[gpar][:], rQg[gpar][:], EPS_LN,
                                           None, OP.add))

                ck_gvar = dv.v

                @blk.scalar
                def _(s, gpar=gpar, ck_gvar=ck_gvar):
                    dv.wait(s, ck_gvar)
                    ac.inc(s.activation(gtmp[:], rQg[gpar][:], AF.Ln))
                    ac.wait(s)
                    ac.inc(s.activation(rQg[gpar][:], gtmp[:], AF.Exp, scale=-0.5))

                ck_grstd = ac.v

                @blk.tensor
                def _(t, grp=grp, gpar=gpar, ck_grstd=ck_grstd):
                    ac.wait(t, ck_grstd)
                    if grp >= 1:
                        ac.wait(t, ck_grp[grp - 1])
                    for k in range(8):
                        pe.inc(t.transpose(
                            pstA[:, k:k + 1],
                            rQg[gpar][0:1, k * P:(k + 1) * P].bitcast(F32),
                            idf[0:1, 0:1]))

                ck_rt = pe.v

                @blk.scalar
                def _(s, grp=grp, ck_rt=ck_rt):
                    pe.wait(s, ck_rt)
                    ck_grp[grp] = ac.inc(s.activation(
                        rcolA[:, grp * 8:(grp + 1) * 8], pstA[:, 0:8], AF.Copy))

                if grp == 3:
                    @blk.vector
                    def _(v, gpar=gpar, ck_grstd=ck_grstd):
                        ac.wait(v, ck_grstd)
                        dv.inc(v.tensor_copy(rowsNm8[:], rSg[gpar][:]))

                for c in range(grp * 8, grp * 8 + 8):
                    par = c % 2
                    own = c >= NPC
                    oc = c - NPC

                    if pend_tail[0] is not None and c % 8 == 1:
                        pend_tail[0]()
                        pend_tail[0] = None

                    @blk.tensor
                    def _(t, c=c, par=par, own=own, grp=grp, gpar=gpar):
                        if not own:
                            gqp[c % NPB].wait(t, ck_xbld[c])
                        gq.wait(t, ck_wkv)
                        if c < 2:
                            ac.wait(t, ck_kvone)
                        if c >= 2:
                            ac.wait(t, ck_aff[c - 2])
                        xbc = xb(c)
                        for fi in range(FI):
                            pe.inc(t.matmul(pkv[par][:, 0:512],
                                            xbc[:, fi * P:(fi + 1) * P],
                                            wkb[:, fi * 512:(fi + 1) * 512],
                                            start=(fi == 0), stop=False))
                            pe.inc(t.matmul(pkv[par][:, 512:1024],
                                            xbc[:, fi * P:(fi + 1) * P],
                                            wvb[:, fi * 512:(fi + 1) * 512],
                                            start=(fi == 0), stop=False))
                        nr = rSg[gpar][0:1, (c % 8) * P:(c % 8 + 1) * P]
                        pe.inc(t.matmul(pkv[par][:, 0:512], nr, konerow[0:1, 0:512],
                                        start=False, stop=True))
                        pe.inc(t.matmul(pkv[par][:, 512:1024], nr,
                                        konerow[0:1, 512:1024],
                                        start=False, stop=True))
                        ck_kv[c] = pe.v
                        if c % 8 == 7:
                            ck_k1use[grp] = pe.v

                    @blk.scalar
                    def _(s, c=c, par=par, own=own, grp=grp):
                        pe.wait(s, ck_kv[c])
                        ac.wait(s, ck_grp[grp])
                        if c >= 2:
                            dv.wait(s, ck_rope[c - 2])
                            if c - 2 < NPC:
                                pe.wait(s, ck_state[c - 2])
                            else:
                                pe.wait(s, ck_kft[c - 2])
                        rc = rcolA[:, c:c + 1].bitcast(F32)
                        ac.inc(s.activation(krop[par][:], pkv[par][:, 0:512],
                                            AF.Copy, scale=rc))
                        ck_aff[c] = ac.inc(s.activation(v_dst(c),
                                                        pkv[par][:, 512:1024],
                                                        AF.Copy, scale=rc))

                    @blk.vector
                    def _(v, c=c, par=par):
                        ac.wait(v, ck_aff[c])
                        if c >= 2:
                            dv.wait(v, ck_emin[c - 2])
                            ac.wait(v, ck_exp[c - 2])
                        kr = krop[par]
                        ka = kaux[par]
                        kr3 = kr[:].rearrange("p (r t) -> p r t", r=8)
                        dv.inc(v.tensor_tensor(
                            ka[:, 0:512].rearrange("p (r t) -> p r t", r=8),
                            kr3, rope_bc(cosT, c), OP.mult))
                        dv.inc(v.tensor_tensor(
                            ka[:, 512:1024].rearrange("p (r t) -> p r t", r=8),
                            kr3, rope_bc(sinT, c), OP.mult))
                        dv.wait(v)
                        alo = strided_half(ka, 0, 4, 128, 64)
                        ahi = strided_half(ka, 64, 4, 128, 64)
                        plo = strided_half(ka, 512, 4, 128, 64)
                        phi = strided_half(ka, 576, 4, 128, 64)
                        olo = strided_half(kr, 0, 4, 128, 64)
                        ohi = strided_half(kr, 64, 4, 128, 64)
                        dv.inc(v.tensor_tensor(olo, alo, phi, OP.subtract))
                        dv.inc(v.tensor_tensor(ohi, ahi, plo, OP.add))
                        ck_rope[c] = dv.v
                        dv.wait(v)
                        dv.inc(v.tensor_scalar(ka[:, 0:512], kr[:], 0.0, None,
                                               OP.min))
                        ck_emin[c] = dv.v

                    @blk.scalar
                    def _(s, c=c, par=par):
                        dv.wait(s, ck_emin[c])
                        ck_exp[c] = ac.inc(s.activation(kaux[par][:, 512:1024],
                                                        kaux[par][:, 0:512], AF.Exp))

                    @blk.vector
                    def _(v, c=c, par=par, own=own):
                        ac.wait(v, ck_exp[c])
                        if (not own) and c >= 2:
                            pe.wait(v, ck_state[c - 2])
                        dv.inc(v.scalar_tensor_tensor(kf_dst(c), krop[par][:], 0.0,
                                                      kaux[par][:, 512:1024],
                                                      OP.max, OP.add))
                        ck_elu[c] = dv.v

                    def mk_tail(c=c, oc=oc, own=own):
                        def emit():
                            if own:
                                @blk.tensor
                                def _(t, c=c, oc=oc):
                                    dv.wait(t, ck_elu[c])
                                    if c >= NPC + 1:
                                        ac.wait(t, ck_kfte[c - 1])
                                    src2 = kf_tm[:, oc * 512:(oc + 1) * 512]
                                    for h in range(KVH):
                                        pe.inc(t.transpose(
                                            pmT[:, h * 128:(h + 1) * 128],
                                            src2[:, h * 128:(h + 1) * 128],
                                            idb[:]))
                                    ck_kft[c] = pe.v

                                @blk.scalar
                                def _(s, c=c, oc=oc):
                                    pe.wait(s, ck_kft[c])
                                    for h in range(KVH):
                                        ac.inc(s.activation(
                                            kfT[:, h * TOWN + oc * P:
                                                h * TOWN + (oc + 1) * P],
                                            pmT[:, h * 128:(h + 1) * 128], AF.Copy))
                                    ck_kfte[c] = ac.v
                            else:
                                @blk.tensor
                                def _(t, c=c):
                                    dv.wait(t, ck_elu[c])
                                    kfb = kf_dst(c)
                                    vb = v_dst(c)
                                    last = (c == NPC - 1)
                                    for h in range(KVH):
                                        pe.inc(t.matmul(
                                            pstate[:, h * 128:(h + 1) * 128],
                                            kfb[:, h * 128:(h + 1) * 128],
                                            vb[:, h * 128:(h + 1) * 128],
                                            start=(c == 0 and h == 0),
                                            stop=(last and h == KVH - 1)))
                                        pe.inc(t.matmul(
                                            pzp[:, h:h + 1],
                                            kfb[:, h * 128:(h + 1) * 128],
                                            maskS[:, c:c + 1],
                                            start=(c == 0 and h == 0),
                                            stop=(last and h == KVH - 1)))
                                    ck_state[c] = pe.v
                        return emit

                    if pend[0] is not None:
                        pend[0]()
                    if c % 8 == 7:
                        pend[0] = None
                        pend_tail[0] = mk_tail()
                    else:
                        pend[0] = mk_tail()

            if pend_tail[0] is not None:
                pend_tail[0]()
                pend_tail[0] = None

            @blk.scalar
            def _(s):
                pe.wait(s, ck_state[NPC - 1])
                ac.inc(s.activation(stp[:], pstate[:], AF.Copy))
                ac.inc(s.activation(zpS[:], pzp, AF.Copy))

    qoT = L_A.enter_context(nc.sbuf_tensor("qoT", [P, H * TOWN], BF16))

    with contextlib.ExitStack() as ps:
        wqb = [ps.enter_context(nc.sbuf_tensor(f"wqb{i}", [P, 512], BF16)) for i in range(4)]
        qful = [ps.enter_context(nc.sbuf_tensor(f"qful{i}", [P, 512], F32)) for i in range(2)]
        qaux = [ps.enter_context(nc.sbuf_tensor(f"qaux{i}", [P, 1024], F32)) for i in range(2)]
        qbf = [ps.enter_context(nc.sbuf_tensor(f"qbf{i}", [P, 512], BF16)) for i in range(2)]
        qonerow = ps.enter_context(nc.sbuf_tensor("qonerow", [1, 512], F32R))
        pq = [ps.enter_context(nc.psum_tensor(f"pq{i}", [P, 2048], F32)) for i in range(2)]

        lastu = {"rope": [0, 0], "emin": [0, 0], "exp": [0, 0], "ft": [0, 0],
                 "crep": [0, 0], "wq": [0, 0, 0, 0], "qelu": [0, 0]}
        pq_free = [[0] * 4, [0] * 4]
        with nc.Block() as blk:
            chains_seq = [0]
            for eq in range(4):
                half_k1 = [None, None]
                chains = [None, None]
                for half in range(2):
                    tcs = list(range(half * 4, half * 4 + 4))
                    pqh = pq[half]
                    ck_wld = [0] * FI
                    ck_use = [0] * FI

                    for fi in range(FI):
                        @blk.gpsimd
                        def _(g, eq=eq, fi=fi, ck_wld=ck_wld, ck_use=ck_use):
                            if fi >= 4:
                                pe.wait(g, ck_use[fi - 4])
                            elif eq + half > 0:
                                pe.wait(g, lastu["wq"][fi % 4])
                            ck_wld[fi] = gqw[fi % 4].dma(g.dma_start(wqb[fi % 4][:],
                                                                     d_wq.ap()[eq][fi]))

                        @blk.tensor
                        def _(t, eq=eq, half=half, fi=fi, tcs=tcs, pqh=pqh,
                              ck_wld=ck_wld, ck_use=ck_use):
                            gqw[fi % 4].wait(t, ck_wld[fi])
                            if fi == 0:
                                for j in range(4):
                                    ac.wait(t, pq_free[half][j])
                            if half == 0:
                                if eq > 0 and fi == 0:
                                    ac.wait(t, pq_free[1][0])
                                pe.inc(t.matmul(pq[1][0:1, 0:512], onescb[:],
                                                wqb[fi % 4][:],
                                                start=(fi == 0), stop=(fi == FI - 1)))
                            for j, tc in enumerate(tcs):
                                pe.inc(t.matmul(pqh[:, j * 512:(j + 1) * 512],
                                                xob_tile(tc)[:, fi * P:(fi + 1) * P],
                                                wqb[fi % 4][:],
                                                start=(fi == 0), stop=False))
                            ck_use[fi] = pe.v
                            lastu["wq"][fi % 4] = pe.v

                    ck_qmm = pe.v

                    if half == 0:
                        @blk.scalar
                        def _(s, ck_qmm=ck_qmm):
                            pe.wait(s, ck_qmm)
                            ac.inc(s.activation(qonerow[:], pq[1][0:1, 0:512],
                                                AF.Copy))

                        ck_qone = ac.v
                        pq_free[1][0] = ck_qone

                    ck_k1s = [0] * 4
                    for j, tc in enumerate(tcs):
                        @blk.tensor
                        def _(t, j=j, tc=tc, pqh=pqh, ck_qone=ck_qone):
                            ac.wait(t, ck_qone)
                            pe.inc(t.matmul(pqh[:, j * 512:(j + 1) * 512],
                                            rowsNm8[0:1, tc * P:(tc + 1) * P],
                                            qonerow[:], start=False, stop=True))
                        ck_k1s[j] = pe.v

                    def mk_chains(eq=eq, half=half, tcs=tcs, pqh=pqh, ck_k1s=ck_k1s):
                        def emit():
                            nonlocal_seq = [None]
                            for j, tc in enumerate(tcs):
                                par = chains_seq[0] % 2
                                chains_seq[0] += 1

                                @blk.scalar
                                def _(s, j=j, tc=tc, par=par, pqh=pqh):
                                    pe.wait(s, ck_k1s[j])
                                    dv.wait(s, lastu["rope"][par])
                                    ac.inc(s.activation(
                                        qful[par][:],
                                        pqh[:, j * 512:(j + 1) * 512], AF.Copy,
                                        scale=rcolA[:, NPC + tc:NPC + tc + 1].bitcast(F32)))

                                ck_qaff = ac.v

                                @blk.vector
                                def _(v, tc=tc, par=par, ck_qaff=ck_qaff):
                                    ac.wait(v, ck_qaff)
                                    dv.wait(v, lastu["emin"][par])
                                    ac.wait(v, lastu["exp"][par])
                                    ka = qaux[par]
                                    kr = qful[par]
                                    kr3 = kr[:].rearrange("p (r t) -> p r t", r=8)
                                    dv.inc(v.tensor_tensor(
                                        ka[:, 0:512].rearrange("p (r t) -> p r t", r=8),
                                        kr3, rope_bc(cosT, NPC + tc), OP.mult))
                                    dv.inc(v.tensor_tensor(
                                        ka[:, 512:1024].rearrange("p (r t) -> p r t", r=8),
                                        kr3, rope_bc(sinT, NPC + tc), OP.mult))
                                    dv.wait(v)
                                    alo = strided_half(ka, 0, 4, 128, 64)
                                    ahi = strided_half(ka, 64, 4, 128, 64)
                                    plo = strided_half(ka, 512, 4, 128, 64)
                                    phi = strided_half(ka, 576, 4, 128, 64)
                                    olo = strided_half(kr, 0, 4, 128, 64)
                                    ohi = strided_half(kr, 64, 4, 128, 64)
                                    dv.inc(v.tensor_tensor(olo, alo, phi, OP.subtract))
                                    dv.inc(v.tensor_tensor(ohi, ahi, plo, OP.add))
                                    lastu["rope"][par] = dv.v
                                    dv.wait(v)
                                    dv.inc(v.tensor_scalar(ka[:, 0:512], kr[:], 0.0,
                                                           None, OP.min))
                                    lastu["emin"][par] = dv.v

                                ck_emin = dv.v

                                @blk.scalar
                                def _(s, par=par, ck_emin=ck_emin):
                                    dv.wait(s, ck_emin)
                                    ac.inc(s.activation(qaux[par][:, 512:1024],
                                                        qaux[par][:, 0:512], AF.Exp))
                                    lastu["exp"][par] = ac.v

                                ck_exp = ac.v

                                @blk.vector
                                def _(v, par=par, ck_exp=ck_exp):
                                    ac.wait(v, ck_exp)
                                    pe.wait(v, lastu["ft"][par])
                                    dv.inc(v.scalar_tensor_tensor(
                                        qbf[par][:], qful[par][:], 0.0,
                                        qaux[par][:, 512:1024], OP.max, OP.add))
                                    lastu["qelu"][par] = dv.v

                                ck_qelu = dv.v

                                @blk.tensor
                                def _(t, j=j, par=par, pqh=pqh, ck_qelu=ck_qelu,
                                      ck_qaff=ck_qaff):
                                    dv.wait(t, ck_qelu)
                                    ac.wait(t, ck_qaff)
                                    pm = pqh[:, j * 512:j * 512 + 256].bitcast(BF16)
                                    for h in range(4):
                                        pe.inc(t.transpose(
                                            pm[:, h * 128:(h + 1) * 128],
                                            qbf[par][:, h * 128:(h + 1) * 128],
                                            idb[:]))
                                    lastu["ft"][par] = pe.v

                                ck_ft = pe.v

                                @blk.scalar
                                def _(s, j=j, tc=tc, eq=eq, half=half, pqh=pqh,
                                      ck_ft=ck_ft):
                                    pe.wait(s, ck_ft)
                                    pm = pqh[:, j * 512:j * 512 + 256].bitcast(BF16)
                                    for h in range(4):
                                        hg = eq * 4 + h
                                        ac.inc(s.activation(
                                            qoT[:, hg * TOWN + tc * P:
                                                hg * TOWN + (tc + 1) * P],
                                            pm[:, h * 128:(h + 1) * 128], AF.Copy))
                                    pq_free[half][j] = ac.v
                        return emit

                    chains[half] = mk_chains()

                chains[0]()
                chains[1]()

            if "tap_qfT" in d_tap:
                @blk.vector
                def _(v):
                    for j in range(4):
                        ac.wait(v, pq_free[0][j])
                        ac.wait(v, pq_free[1][j])
                    dv.inc(v.tensor_copy(qaux[0][:, 0:512], qoT[:, 0:512]))

                @blk.sync
                def _(s):
                    dv.wait(s)
                    pass

    with contextlib.ExitStack() as ps:
        stsb = ps.enter_context(nc.sbuf_tensor("stsb", [P, 512], BF16))
        zsb = ps.enter_context(nc.sbuf_tensor("zsb", [P, 4], BF16))
        smb = [ps.enter_context(nc.sbuf_tensor(f"smb{i}", [P, 512], BF16)) for i in range(2)]
        denS = ps.enter_context(nc.sbuf_tensor("denS", [1, 512], F32))
        recS = [ps.enter_context(nc.sbuf_tensor(f"recS{i}", [1, 512], F32R)) for i in range(2)]
        rbS = ps.enter_context(nc.sbuf_tensor("rbS", [P, 512], F32))

        pstate3 = ps.enter_context(nc.psum_tensor("pstate3", [P, 512], F32))
        pz3 = ps.enter_context(nc.psum_tensor("pz3", [P, 4], F32))
        psS = [ps.enter_context(nc.psum_tensor(f"psS{i}", [P, 512], F32)) for i in range(2)]
        pnum = [ps.enter_context(nc.psum_tensor(f"pnum{i}", [P, 512], F32)) for i in range(2)]
        pden = ps.enter_context(nc.psum_tensor("pden", [1, 512], F32))
        prb = ps.enter_context(nc.psum_tensor("prb", [P, 512], F32))

        with nc.Block() as blk:
            ck_stev = [0] * NOC
            ck_pedone = [0] * NOC
            ck_supd = [0] * NOC
            ck_smev = [0, 0]
            ck_smuse = [0, 0]
            ck_div_last = [0, 0]
            ck_rec_last = 0
            ck_rb_last = 0
            ck_rb_par = [0, 0]
            ck_rbev_last = 0
            ck_div2_last = [0]
            pend5 = [None]
            for oc in range(NOC):
                @blk.vector
                def _(v, oc=oc):
                    if oc == 0:
                        dv.inc(v.tensor_copy(stsb[:], stp[:]))
                        dv.inc(v.tensor_copy(zsb[:], zpS[:]))
                    else:
                        pe.wait(v, ck_supd[oc - 1])
                        pe.wait(v, ck_pedone[oc - 1])
                        dv.inc(v.tensor_tensor(stsb[:], pstate3[:], stp[:], OP.add))
                        dv.inc(v.tensor_tensor(zsb[:], pz3[:], zpS[:], OP.add))
                    ck_stev[oc] = dv.v

                @blk.tensor
                def _(t, oc=oc):
                    dv.wait(t, ck_stev[oc])
                    kfb = kf_tm[:, oc * 512:(oc + 1) * 512]
                    vb = v_tm[:, oc * 512:(oc + 1) * 512]
                    for h in range(KVH):
                        pe.inc(t.matmul(pstate3[:, h * 128:(h + 1) * 128],
                                        kfb[:, h * 128:(h + 1) * 128],
                                        vb[:, h * 128:(h + 1) * 128],
                                        start=(oc == 0 and h == 0),
                                        stop=(h == KVH - 1),
                                        skip_group_check=(oc > 0)))
                        pe.inc(t.matmul(pz3[:, h:h + 1],
                                        kfb[:, h * 128:(h + 1) * 128],
                                        maskS[:, NPC + oc:NPC + oc + 1],
                                        start=(oc == 0 and h == 0),
                                        stop=(h == KVH - 1),
                                        skip_group_check=(oc > 0)))
                    ck_supd[oc] = pe.v

                for kv in range(KVH):
                    par = (oc * KVH + kv) % 2

                    @blk.tensor
                    def _(t, oc=oc, kv=kv, par=par):
                        dv.wait(t, ck_smev[par])
                        for h in range(4):
                            hg = kv * 4 + h
                            pe.inc(t.matmul(
                                psS[par][:, h * 128:(h + 1) * 128],
                                kfT[:, kv * TOWN + oc * P:kv * TOWN + (oc + 1) * P],
                                qoT[:, hg * TOWN + oc * P:hg * TOWN + (oc + 1) * P],
                                start=(h == 0), stop=(h == 3)))

                    ck_smm = pe.v

                    @blk.vector
                    def _(v, par=par, ck_smm=ck_smm):
                        pe.wait(v, ck_smm)
                        pe.wait(v, ck_smuse[par])
                        dv.inc(v.tensor_tensor(smb[par][:], psS[par][:], cmask4[:], OP.mult))
                        ck_smev[par] = dv.v

                    ck_sm = dv.v

                    @blk.tensor
                    def _(t, oc=oc, kv=kv, par=par, ck_sm=ck_sm):
                        dv.wait(t, ck_sm)
                        dv.wait(t, ck_stev[oc])
                        dv.wait(t, ck_div_last[par])
                        vb = v_tm[:, oc * 512 + kv * 128:oc * 512 + (kv + 1) * 128]
                        for h in range(4):
                            pe.inc(t.matmul(pnum[par][:, h * 128:(h + 1) * 128],
                                            vb, smb[par][:, h * 128:(h + 1) * 128],
                                            start=(h == 0), stop=False))
                        for h in range(4):
                            hg = kv * 4 + h
                            pe.inc(t.matmul(
                                pnum[par][:, h * 128:(h + 1) * 128],
                                stsb[:, kv * 128:(kv + 1) * 128],
                                qoT[:, hg * TOWN + oc * P:hg * TOWN + (oc + 1) * P],
                                start=False, stop=(h == 3)))

                    ck_num = pe.v

                    @blk.tensor
                    def _(t, oc=oc, kv=kv, par=par):
                        dv.wait(t, ck_rec_last)
                        for h in range(4):
                            pe.inc(t.matmul(pden[0:1, h * 128:(h + 1) * 128],
                                            onescb[:], smb[par][:, h * 128:(h + 1) * 128],
                                            start=(h == 0), stop=False))
                        for h in range(4):
                            hg = kv * 4 + h
                            pe.inc(t.matmul(
                                pden[0:1, h * 128:(h + 1) * 128],
                                zsb[:, kv:kv + 1],
                                qoT[:, hg * TOWN + oc * P:hg * TOWN + (oc + 1) * P],
                                start=False, stop=(h == 3)))

                    ck_denmm = pe.v
                    ck_smuse[par] = pe.v

                    sidx = oc * KVH + kv
                    rpar = sidx % 2

                    @blk.vector
                    def _(v, rpar=rpar, ck_denmm=ck_denmm):
                        pe.wait(v, ck_denmm)
                        pe.wait(v, ck_rb_par[rpar])
                        dv.inc(v.tensor_scalar(denS[:], pden[0:1, :], EPS_ATT, None, OP.add))
                        dv.wait(v)
                        with nc.allow_low_precision(reason="f32r view of fp32 recip"):
                            dv.inc(v.reciprocal(recS[rpar][:], denS[:]))

                    ck_rec_last = dv.v

                    def mk_tail5(oc=oc, kv=kv, par=par, rpar=rpar,
                                 ck_rec=ck_rec_last, ck_num=ck_num):
                        def emit():
                            global_ck = {}

                            @blk.tensor
                            def _(t, ck_rec=ck_rec, ck_rbev=ck_rbev_last):
                                dv.wait(t, ck_rec)
                                ac.wait(t, ck_rbev)
                                pe.inc(t.matmul(prb[:], onesr[:], recS[rpar][:],
                                                start=True, stop=True))

                            ck_rb = pe.v

                            @blk.scalar
                            def _(s, ck_rb=ck_rb):
                                pe.wait(s, ck_rb)
                                dv.wait(s, ck_div2_last[0])
                                ac.inc(s.activation(rbS[:], prb[:], AF.Copy))

                            ck_rbev = ac.v

                            @blk.vector
                            def _(v, oc=oc, kv=kv, par=par, ck_rbev=ck_rbev,
                                  ck_num=ck_num):
                                ac.wait(v, ck_rbev)
                                pe.wait(v, ck_num)
                                otap = qoT[:]
                                dst = bass.AP(otap.tensor,
                                              otap.offset + (kv * 4) * TOWN + oc * P,
                                              [[H * TOWN, P], [TOWN, 4], [1, P]])
                                dv.inc(v.tensor_tensor(dst, pnum[par][:], rbS[:],
                                                       OP.mult))
                                ck_div_last[par] = dv.v
                                ck_div2_last[0] = dv.v
                            return ck_rb, ck_rbev
                        return emit

                    if pend5[0] is not None:
                        ck_rb_par[1 - rpar], ck_rbev_last = pend5[0]()
                        ck_rb_last = ck_rb_par[1 - rpar]
                    pend5[0] = mk_tail5()

                ck_pedone[oc] = pe.v

            if pend5[0] is not None:
                rp_last = (NOC * KVH - 1) % 2
                ck_rb_par[rp_last], ck_rbev_last = pend5[0]()
                pend5[0] = None

    with contextlib.ExitStack() as ps:
        wob = [ps.enter_context(nc.sbuf_tensor(f"wob{i}", [P, FI * P], BF16)) for i in range(2)]
        aor = [ps.enter_context(nc.sbuf_tensor(f"aor{i}", [P, TOWN], F32R)) for i in range(2)]
        sqr = ps.enter_context(nc.sbuf_tensor("sqr", [P, TOWN], F32R))
        rws = ps.enter_context(nc.sbuf_tensor("rws", [1, 3 * TOWN], F32R))
        mrb = ps.enter_context(nc.sbuf_tensor("mrb", [P, 2 * TOWN], F32))
        x2w = [ps.enter_context(nc.sbuf_tensor(f"x2w{i}", [P, TOWN], F32R)) for i in range(2)]
        xre = [ps.enter_context(nc.sbuf_tensor(f"xre{i}", [P, TOWN], F32R)) for i in range(2)]

        pwo = [ps.enter_context(nc.psum_tensor(f"pwo{i}", [P, 1024], F32)) for i in range(2)]
        pls = ps.enter_context(nc.psum_tensor("pls", [1, 1024], F32))
        plq = ps.enter_context(nc.psum_tensor("plq", [1, 1024], F32))

        with nc.Block() as blk:
            ck_wld = [0] * FI
            ck_womm = [0] * FI
            ck_aoev = [0] * FI
            ck_aopark = [0] * FI
            ck_sqt = [0] * FI
            ck_sqmm = [0] * FI

            pend4 = [None]
            for fc in range(FI):
                par = fc % 2

                @blk.gpsimd
                def _(g, fc=fc, par=par):
                    if fc >= 2:
                        pe.wait(g, ck_womm[fc - 2])
                    ck_wld[fc] = gqw[par].dma(g.dma_start(
                        wob[par][:].rearrange("p (f n) -> p f n", f=FI),
                        d_wo.ap()[fc]))

                @blk.tensor
                def _(t, fc=fc, par=par):
                    gqw[par].wait(t, ck_wld[fc])
                    if fc >= 2:
                        ac.wait(t, ck_aoev[fc - 2])
                    for fo in range(FI):
                        wtile = wob[par][:, fo * P:(fo + 1) * P]
                        for th in range(2):
                            pe.inc(t.matmul(
                                pwo[par][:, th * 512:(th + 1) * 512],
                                wtile,
                                qoT[:, fo * TOWN + th * 512:fo * TOWN + (th + 1) * 512],
                                start=(fo == 0), stop=(fo == FI - 1)))
                    ck_womm[fc] = pe.v

                @blk.scalar
                def _(s, fc=fc, par=par):
                    pe.wait(s, ck_womm[fc])
                    if fc >= 2:
                        ldp[par].wait(s, ck_aopark[fc - 2])
                        pe.wait(s, ck_sqmm[fc - 2] if fc >= 2 else 0)
                    ck_aoev[fc] = ac.inc(s.activation(aor[par][:], pwo[par][:], AF.Copy))

                @blk.sync
                def _(s, fc=fc, par=par):
                    ac.wait(s, ck_aoev[fc])
                    ck_aopark[fc] = ldp[par].dma(s.dma_start(d_ao.ap()[fc], aor[par][:]))

                def mk_stats(fc=fc, par=par):
                    def emit():
                        @blk.tensor
                        def _(t, fc=fc, par=par):
                            ac.wait(t, ck_aoev[fc])
                            for th in range(2):
                                pe.inc(t.matmul(pls[0:1, th * 512:(th + 1) * 512],
                                                onesc[:],
                                                aor[par][:, th * 512:(th + 1) * 512],
                                                start=(fc == 0), stop=(fc == FI - 1)))
                            dv.wait(t, ck_sqt[fc])
                            for th in range(2):
                                pe.inc(t.matmul(plq[0:1, th * 512:(th + 1) * 512],
                                                onesc[:],
                                                sqr[:, th * 512:(th + 1) * 512],
                                                start=(fc == 0), stop=(fc == FI - 1)))
                            ck_sqmm[fc] = pe.v
                    return emit

                if pend4[0] is not None:
                    pend4[0]()
                pend4[0] = mk_stats()

                @blk.vector
                def _(v, fc=fc, par=par):
                    ac.wait(v, ck_aoev[fc])
                    if fc >= 1:
                        pe.wait(v, ck_sqmm[fc - 1])
                    ck_sqt[fc] = dv.inc(v.tensor_tensor(sqr[:], aor[par][:],
                                                        aor[par][:], OP.mult))

            pend4[0]()

            @blk.vector
            def _(v):
                pe.wait(v, ck_sqmm[FI - 1])
                dv.inc(v.tensor_scalar(rws[0:1, 0:1024], pls[0:1, :], INV_DM, None, OP.mult))
                dv.wait(v)
                dv.inc(v.tensor_tensor(rws[0:1, 1024:2048], rws[0:1, 0:1024],
                                       rws[0:1, 0:1024], OP.mult))
                dv.wait(v)
                dv.inc(v.scalar_tensor_tensor(rws[0:1, 2048:3072], plq[0:1, :], INV_DM,
                                              rws[0:1, 1024:2048], OP.mult, OP.subtract))
                dv.wait(v)
                dv.inc(v.tensor_scalar(rws[0:1, 2048:3072], rws[0:1, 2048:3072], EPS_LN,
                                       None, OP.add))

            ck_r1 = dv.v

            @blk.scalar
            def _(s):
                dv.wait(s, ck_r1)
                ac.inc(s.activation(rws[0:1, 1024:2048], rws[0:1, 2048:3072], AF.Ln))
                ac.wait(s)
                ac.inc(s.activation(rws[0:1, 2048:3072], rws[0:1, 1024:2048], AF.Exp, scale=-0.5))

            ck_r2 = ac.v

            @blk.tensor
            def _(t):
                ac.wait(t, ck_r2)
                ac.wait(t, ck_aoev[FI - 1])
                for th in range(2):
                    pe.inc(t.matmul(pwo[0][:, th * 512:(th + 1) * 512], onesr[:],
                                    rws[0:1, th * 512:(th + 1) * 512],
                                    start=True, stop=True))
                    pe.inc(t.matmul(pwo[1][:, th * 512:(th + 1) * 512], onesr[:],
                                    rws[0:1, 2048 + th * 512:2048 + (th + 1) * 512],
                                    start=True, stop=True))

            ck_bc = pe.v

            @blk.scalar
            def _(s):
                pe.wait(s, ck_bc)
                ac.inc(s.activation(mrb[:, 0:1024], pwo[0][:], AF.Copy))
                ac.inc(s.activation(mrb[:, 1024:2048], pwo[1][:], AF.Copy))

            ck_mrb = ac.v

            ck_aold = [0] * FI
            ck_xld2 = [0] * FI
            ck_t1 = [0] * FI
            ck_x2w = [0] * FI
            ck_x2pk = [0] * FI
            ck_x2sq = [0] * FI
            ck_x2smm = [0] * FI
            for fc in range(FI):
                par = fc % 2

                @blk.sync
                def _(s, fc=fc, par=par):
                    if fc == 0:
                        ldp[0].wait(s)
                        ldp[1].wait(s)
                    if fc < 2:
                        pe.wait(s, ck_sqmm[FI - 1])
                    if fc >= 2:
                        dv.wait(s, ck_t1[fc - 2])
                        dv.wait(s, ck_x2w[fc - 2])
                        pe.wait(s, ck_x2smm[fc - 2])
                    ck_aold[fc] = ldr[par].dma(s.dma_start(aor[par][:], d_ao.ap()[fc]))
                    ck_xld2[fc] = ldr[par].dma(s.dma_start(
                        xre[par][:],
                        bass.AP(d_xt, NPC * P * FI * P + fc * P,
                                [[FI * P, P], [P * FI * P, NOC], [1, P]])))

                @blk.vector
                def _(v, fc=fc, par=par):
                    ldr[par].wait(v, ck_xld2[fc])
                    ac.wait(v, ck_mrb)
                    if fc >= 2:
                        ldp[par].wait(v, ck_x2pk[fc - 2])
                    dv.inc(v.tensor_tensor(x2w[par][:], aor[par][:], mrb[:, 0:1024], OP.subtract))
                    dv.wait(v)
                    dv.inc(v.tensor_tensor(aor[par][:], x2w[par][:], mrb[:, 1024:2048], OP.mult))
                    ck_t1[fc] = dv.v
                    dv.wait(v)
                    ldr[par].wait(v, ck_xld2[fc])
                    dv.inc(v.tensor_tensor(x2w[par][:], aor[par][:], xre[par][:], OP.add))
                    ck_x2w[fc] = dv.v
                    dv.wait(v)
                    if fc == 0:
                        pe.wait(v, ck_sqmm[FI - 1])
                    else:
                        pe.wait(v, ck_x2smm[fc - 1])
                    dv.inc(v.tensor_tensor(sqr[:], x2w[par][:], x2w[par][:], OP.mult))
                    ck_x2sq[fc] = dv.v

                @blk.tensor
                def _(t, fc=fc, par=par):
                    dv.wait(t, ck_x2w[fc])
                    if fc == 0:
                        dv.wait(t, ck_r1)
                    for th in range(2):
                        pe.inc(t.matmul(pls[0:1, th * 512:(th + 1) * 512], onesc[:],
                                        x2w[par][:, th * 512:(th + 1) * 512],
                                        start=(fc == 0), stop=(fc == FI - 1)))
                    dv.wait(t, ck_x2sq[fc])
                    for th in range(2):
                        pe.inc(t.matmul(plq[0:1, th * 512:(th + 1) * 512], onesc[:],
                                        sqr[:, th * 512:(th + 1) * 512],
                                        start=(fc == 0), stop=(fc == FI - 1)))
                    ck_x2smm[fc] = pe.v

                @blk.sync
                def _(s, fc=fc, par=par):
                    dv.wait(s, ck_x2w[fc])
                    ck_x2pk[fc] = ldp[par].dma(s.dma_start(d_x2.ap()[fc], x2w[par][:]))

            @blk.scalar
            def _(s):
                pe.wait(s, ck_x2smm[FI - 1])
                ac.inc(s.activation(rws[0:1, 0:1024], pls[0:1, :], AF.Copy))
                ck_stev = ac.inc(s.activation(rws[0:1, 1024:2048], plq[0:1, :], AF.Copy))
                ac.wait(s, ck_stev)
                vq.dma(s.dma_start(d_st.ap(), rws[0:1, 0:2048]))

            ck_stpark = vq.v

            @blk.sync
            def _(s):
                ldp[0].wait(s)
                ldp[1].wait(s)
                vq.wait(s, ck_stpark)

    L_A.close()
    L_C.close()
    L_B.close()

    P56 = contextlib.ExitStack()
    acc = P56.enter_context(nc.sbuf_tensor("acc", [P, FI * TOWN], F32R))
    P5ab = contextlib.ExitStack()
    x3h = P5ab.enter_context(nc.sbuf_tensor("x3h", [P, FI * TOWN], BF16))
    m3b = P5ab.enter_context(nc.sbuf_tensor("m3b", [P, TOWN], F32))
    r3b = P5ab.enter_context(nc.sbuf_tensor("r3b", [P, TOWN], F32))

    with contextlib.ExitStack() as ps:
        x2r = [ps.enter_context(nc.sbuf_tensor(f"x2r{i}", [P, TOWN], F32R)) for i in range(2)]
        rw5 = ps.enter_context(nc.sbuf_tensor("rw5", [1, 3 * TOWN], F32R))
        scr5 = ps.enter_context(nc.sbuf_tensor("scr5", [P, TOWN], F32))
        pb5 = [ps.enter_context(nc.psum_tensor(f"pb5{i}", [P, 512], F32)) for i in range(4)]

        with nc.Block() as blk:
            ck_x2l = [0] * FI

            @blk.sync
            def _(s):
                ck_stld = vq.dma(s.dma_start(rw5[0:1, 0:2048], d_st.ap()))
                vq.wait(s, ck_stld)

            ck_stld = vq.v

            @blk.vector
            def _(v):
                vq.wait(v, ck_stld)
                dv.inc(v.tensor_scalar(rw5[0:1, 0:1024], rw5[0:1, 0:1024], INV_DM,
                                       None, OP.mult))
                dv.wait(v)
                dv.inc(v.tensor_tensor(rw5[0:1, 2048:3072], rw5[0:1, 0:1024],
                                       rw5[0:1, 0:1024], OP.mult))
                dv.wait(v)
                dv.inc(v.scalar_tensor_tensor(rw5[0:1, 1024:2048], rw5[0:1, 1024:2048],
                                              INV_DM, rw5[0:1, 2048:3072],
                                              OP.mult, OP.subtract))
                dv.wait(v)
                dv.inc(v.tensor_scalar(rw5[0:1, 2048:3072], rw5[0:1, 1024:2048], EPS_LN,
                                       None, OP.add))

            ck_r1 = dv.v

            @blk.scalar
            def _(s):
                dv.wait(s, ck_r1)
                ac.inc(s.activation(rw5[0:1, 1024:2048], rw5[0:1, 2048:3072], AF.Ln))
                ac.wait(s)
                ac.inc(s.activation(rw5[0:1, 2048:3072], rw5[0:1, 1024:2048], AF.Exp, scale=-0.5))

            ck_r2 = ac.v

            @blk.tensor
            def _(t):
                ac.wait(t, ck_r2)
                for th in range(2):
                    pe.inc(t.matmul(pb5[th][:], onesr[:],
                                    rw5[0:1, th * 512:(th + 1) * 512], start=True, stop=True))
                    pe.inc(t.matmul(pb5[2 + th][:], onesr[:],
                                    rw5[0:1, 2048 + th * 512:2048 + (th + 1) * 512],
                                    start=True, stop=True))

            ck_bc = pe.v

            @blk.scalar
            def _(s):
                pe.wait(s, ck_bc)
                for th in range(2):
                    ac.inc(s.activation(m3b[:, th * 512:(th + 1) * 512], pb5[th][:], AF.Copy))
                    ac.inc(s.activation(r3b[:, th * 512:(th + 1) * 512], pb5[2 + th][:], AF.Copy))

            ck_m3 = ac.v

            ck_x3 = [0] * FI
            for fc in range(FI):
                par = fc % 2

                @blk.sync
                def _(s, fc=fc, par=par):
                    if fc >= 2:
                        dv.wait(s, ck_x3[fc - 2])
                    ck_x2l[fc] = ldr[par].dma(s.dma_start(x2r[par][:], d_x2.ap()[fc]))

                @blk.vector
                def _(v, fc=fc, par=par):
                    ldr[par].wait(v, ck_x2l[fc])
                    ac.wait(v, ck_m3)
                    dv.wait(v)
                    dv.inc(v.tensor_tensor(scr5[:], x2r[par][:], m3b[:], OP.subtract))
                    dv.wait(v)
                    dv.inc(v.tensor_tensor(x3h[:, fc * TOWN:(fc + 1) * TOWN], scr5[:],
                                           r3b[:], OP.mult))
                    ck_x3[fc] = dv.v

            ck_x3_all = dv.v

    P5ab_keep = (x3h,)

    with contextlib.ExitStack() as ps:
        wgb = [ps.enter_context(nc.sbuf_tensor(f"wgb{i}", [P, FI * P], BF16)) for i in range(2)]
        wub = [ps.enter_context(nc.sbuf_tensor(f"wub{i}", [P, FI * P], BF16)) for i in range(2)]
        wdb = [ps.enter_context(nc.sbuf_tensor(f"wdb{i}", [P, 1024], BF16)) for i in range(2)]
        sgb = ps.enter_context(nc.sbuf_tensor("sgb", [P, 1024], F32))
        hb = ps.enter_context(nc.sbuf_tensor("hb", [P, 4 * 1024], F32))
        hq = ps.enter_context(nc.sbuf_tensor("hq", [P, 4 * 1024], F32))
        tr = [ps.enter_context(nc.sbuf_tensor(f"tr{i}", [P, 1024], F32)) for i in range(3)]
        hsb = ps.enter_context(nc.sbuf_tensor("hsb", [P, CBG * 4096], BF16))
        pgu = [ps.enter_context(nc.psum_tensor(f"pgu{i}", [P, 2048], F32)) for i in range(2)]
        pd = pgu[0]

        with nc.Block() as blk:
            ck_wg = {}
            ck_wd = {}
            ck_gu = {}
            ck_sil = {}
            ck_h = {}
            ck_hs = [0] * NCB
            ck_dmm = {}
            ck_fl = {}
            ck_hq = [0] * NCB
            last_flush = 0
            last_dmm_cbg = 0

            for cbg in range(NCB // CBG):
                for cbb in range(CBG):
                    cb = cbg * CBG + cbb
                    for j in range(4):
                        pgi = (cb * 4 + j) % 2

                        @blk.gpsimd
                        def _(g, cb=cb, j=j, pgi=pgi):
                            prev = ck_gu.get((cb * 4 + j - 2), None)
                            if prev is not None:
                                pe.wait(g, prev)
                            gqw[pgi].dma(g.dma_start(
                                wgb[pgi][:].rearrange("p (f n) -> p f n", f=FI),
                                d_wg.ap()[cb][j]))
                            ck_wg[cb * 4 + j] = gqw[pgi].dma(g.dma_start(
                                wub[pgi][:].rearrange("p (f n) -> p f n", f=FI),
                                d_wu.ap()[cb][j]))

                        @blk.tensor
                        def _(t, cb=cb, j=j, pgi=pgi, cbg=cbg):
                            gqw[pgi].wait(t, ck_wg[cb * 4 + j])
                            dv.wait(t, ck_x3_all)
                            prevh = ck_h.get(cb * 4 + j - 2)
                            if prevh is not None:
                                dv.wait(t, prevh)
                            if cbg > 0 and cb == cbg * CBG and j < 2:
                                dv.wait(t, last_flush)
                            for fi in range(FI):
                                wgt = wgb[pgi][:, fi * P:(fi + 1) * P]
                                wut = wub[pgi][:, fi * P:(fi + 1) * P]
                                for th in range(2):
                                    pe.inc(t.matmul(
                                        pgu[pgi][:, th * 512:(th + 1) * 512], wgt,
                                        x3h[:, fi * TOWN + th * 512:fi * TOWN + (th + 1) * 512],
                                        start=(fi == 0), stop=(fi == FI - 1)))
                                for th in range(2):
                                    pe.inc(t.matmul(
                                        pgu[pgi][:, 1024 + th * 512:1024 + (th + 1) * 512], wut,
                                        x3h[:, fi * TOWN + th * 512:fi * TOWN + (th + 1) * 512],
                                        start=(fi == 0), stop=(fi == FI - 1)))
                            ck_gu[cb * 4 + j] = pe.v

                        @blk.scalar
                        def _(s, cb=cb, j=j, pgi=pgi):
                            pe.wait(s, ck_gu[cb * 4 + j])
                            prevh = ck_h.get(cb * 4 + j - 1)
                            if prevh is not None:
                                dv.wait(s, prevh)
                            ck_sil[cb * 4 + j] = ac.inc(
                                s.activation(sgb[:], pgu[pgi][:, 0:1024], AF.Silu))

                        @blk.vector
                        def _(v, cb=cb, j=j, pgi=pgi):
                            ac.wait(v, ck_sil[cb * 4 + j])
                            if j == 0:
                                dv.wait(v, ck_hq[cb - 1] if cb >= 1 else 0)
                            dv.inc(v.tensor_tensor(hb[:, j * 1024:(j + 1) * 1024], sgb[:],
                                                   pgu[pgi][:, 1024:2048], OP.mult))
                            ck_h[cb * 4 + j] = dv.v

                    @blk.vector
                    def _(v, cb=cb, cbb=cbb):
                        dv.wait(v, ck_h[cb * 4 + 3])
                        ck_hq[cb] = dv.inc(v.tensor_tensor(hq[:], hb[:], hb[:], OP.mult))

                    @blk.vector
                    def _(v, cb=cb, cbb=cbb, cbg=cbg):
                        dv.wait(v, ck_hq[cb])
                        if cbg > 0 and cbb == 0:
                            pe.wait(v, last_dmm_cbg)
                        q = [hq[:, j * 1024:(j + 1) * 1024] for j in range(4)]
                        dv.inc(v.tensor_tensor(tr[0][:], q[0], q[1], OP.min))
                        dv.inc(v.tensor_tensor(tr[1][:], q[0], q[1], OP.max))
                        dv.inc(v.tensor_tensor(tr[2][:], q[2], q[3], OP.min))
                        dv.wait(v)
                        dv.inc(v.tensor_tensor(tr[0][:], tr[0][:], tr[2][:], OP.max))
                        dv.wait(v)
                        dv.inc(v.tensor_tensor(tr[2][:], q[2], q[3], OP.max))
                        dv.wait(v)
                        dv.inc(v.tensor_tensor(tr[1][:], tr[1][:], tr[2][:], OP.min))
                        dv.wait(v)
                        dv.inc(v.tensor_tensor(tr[0][:], tr[0][:], tr[1][:], OP.max))
                        dv.wait(v)
                        for j in range(4):
                            dv.inc(v.tensor_tensor(tr[1][:], q[j], tr[0][:], OP.is_ge))
                            dv.wait(v)
                            dv.inc(v.tensor_tensor(
                                hsb[:, cbb * 4096 + j * 1024:(cbb * 4096 + (j + 1) * 1024)],
                                hb[:, j * 1024:(j + 1) * 1024], tr[1][:], OP.mult))
                            dv.wait(v)
                        ck_hs[cb] = dv.v

                for fcp in range(8):
                    pdl = pgu[fcp % 2]
                    for cbb in range(CBG):
                        idx = fcp * CBG + cbb

                        @blk.gpsimd
                        def _(g, cbg=cbg, fcp=fcp, cbb=cbb, idx=idx):
                            prev = ck_dmm.get((cbg, idx - 2))
                            if prev is not None:
                                pe.wait(g, prev)
                            elif cbg > 0:
                                pe.wait(g, last_dmm_cbg)
                            ck_wd[(cbg, idx)] = gqw[2 + idx % 2].dma(g.dma_start(
                                wdb[idx % 2][:].rearrange("p (f n) -> p f n", f=4),
                                d_wd.ap()[fcp][cbg * CBG + cbb]))

                        @blk.tensor
                        def _(t, cbg=cbg, fcp=fcp, cbb=cbb, idx=idx, pdl=pdl):
                            if fcp == 0:
                                dv.wait(t, ck_hs[cbg * CBG + cbb])
                                if cbb == 0:
                                    dv.wait(t, ck_h[(cbg * CBG + 3) * 4 + 3])
                            if cbb == 0:
                                prevfl = ck_fl.get((cbg, fcp - 2))
                                if prevfl is not None:
                                    dv.wait(t, prevfl)
                                elif cbg > 0:
                                    dv.wait(t, ck_fl[(cbg - 1, 7 - fcp % 2)])
                            gqw[2 + idx % 2].wait(t, ck_wd[(cbg, idx)])
                            for j in range(4):
                                for f01 in range(2):
                                    wdt = wdb[idx % 2][:, j * 256 + f01 * 128:
                                                       j * 256 + (f01 + 1) * 128]
                                    for th in range(2):
                                        pe.inc(t.matmul(
                                            pdl[:, f01 * 1024 + th * 512:
                                                f01 * 1024 + (th + 1) * 512],
                                            wdt,
                                            hsb[:, cbb * 4096 + j * 1024 + th * 512:
                                                cbb * 4096 + j * 1024 + (th + 1) * 512],
                                            start=(cbb == 0 and j == 0),
                                            stop=(cbb == CBG - 1 and j == 3)))
                            ck_dmm[(cbg, idx)] = pe.v
                            if cbb == CBG - 1:
                                ck_dmm[(cbg, "fcp%d" % fcp)] = pe.v

                    @blk.vector
                    def _(v, cbg=cbg, fcp=fcp, pdl=pdl):
                        pe.wait(v, ck_dmm[(cbg, "fcp%d" % fcp)])
                        for f01 in range(2):
                            for th in range(2):
                                fcn = fcp * 2 + f01
                                dst = acc[:, fcn * TOWN + th * 512:fcn * TOWN + (th + 1) * 512]
                                srcp = pdl[:, f01 * 1024 + th * 512:f01 * 1024 + (th + 1) * 512]
                                if cbg == 0:
                                    dv.inc(v.tensor_copy(dst, srcp))
                                else:
                                    dv.inc(v.tensor_tensor(dst, dst, srcp, OP.add))
                        ck_fl[(cbg, fcp)] = dv.v

                last_flush = dv.v
                last_dmm_cbg = pe.v

    P5ab.close()

    with contextlib.ExitStack() as ps:
        x2r = [ps.enter_context(nc.sbuf_tensor(f"x6r{i}", [P, TOWN], F32R)) for i in range(2)]
        sq6 = ps.enter_context(nc.sbuf_tensor("sq6", [P, TOWN], F32R))
        rw6 = ps.enter_context(nc.sbuf_tensor("rw6", [1, 3 * TOWN], F32R))
        m4b = ps.enter_context(nc.sbuf_tensor("m4b", [P, TOWN], F32))
        r4b = ps.enter_context(nc.sbuf_tensor("r4b", [P, TOWN], F32))
        scr6 = ps.enter_context(nc.sbuf_tensor("scr6", [P, TOWN], F32))
        ob = [ps.enter_context(nc.sbuf_tensor(f"ob{i}", [P, TOWN], F32)) for i in range(2)]
        pl6 = ps.enter_context(nc.psum_tensor("pl6", [1, 1024], F32))
        pq6 = ps.enter_context(nc.psum_tensor("pq6", [1, 1024], F32))
        pb6 = [ps.enter_context(nc.psum_tensor(f"pb6{i}", [P, 512], F32)) for i in range(4)]

        with nc.Block() as blk:
            ck_sqt = [0] * FI
            ck_smm = [0] * FI

            for fc in range(FI):
                @blk.vector
                def _(v, fc=fc):
                    if fc >= 1:
                        pe.wait(v, ck_smm[fc - 1])
                    a_ = acc[:, fc * TOWN:(fc + 1) * TOWN]
                    ck_sqt[fc] = dv.inc(v.tensor_tensor(sq6[:], a_, a_, OP.mult))

                @blk.tensor
                def _(t, fc=fc):
                    for th in range(2):
                        pe.inc(t.matmul(pl6[0:1, th * 512:(th + 1) * 512], onesc[:],
                                        acc[:, fc * TOWN + th * 512:fc * TOWN + (th + 1) * 512],
                                        start=(fc == 0), stop=(fc == FI - 1)))
                    dv.wait(t, ck_sqt[fc])
                    for th in range(2):
                        pe.inc(t.matmul(pq6[0:1, th * 512:(th + 1) * 512], onesc[:],
                                        sq6[:, th * 512:(th + 1) * 512],
                                        start=(fc == 0), stop=(fc == FI - 1)))
                    ck_smm[fc] = pe.v

            @blk.vector
            def _(v):
                pe.wait(v, ck_smm[FI - 1])
                dv.inc(v.tensor_scalar(rw6[0:1, 0:1024], pl6[0:1, :], INV_DM, None, OP.mult))
                dv.wait(v)
                dv.inc(v.tensor_tensor(rw6[0:1, 1024:2048], rw6[0:1, 0:1024],
                                       rw6[0:1, 0:1024], OP.mult))
                dv.wait(v)
                dv.inc(v.scalar_tensor_tensor(rw6[0:1, 2048:3072], pq6[0:1, :], INV_DM,
                                              rw6[0:1, 1024:2048], OP.mult, OP.subtract))
                dv.wait(v)
                dv.inc(v.tensor_scalar(rw6[0:1, 2048:3072], rw6[0:1, 2048:3072], EPS_LN,
                                       None, OP.add))

            ck_r1 = dv.v

            @blk.scalar
            def _(s):
                dv.wait(s, ck_r1)
                ac.inc(s.activation(rw6[0:1, 1024:2048], rw6[0:1, 2048:3072], AF.Ln))
                ac.wait(s)
                ac.inc(s.activation(rw6[0:1, 2048:3072], rw6[0:1, 1024:2048], AF.Exp, scale=-0.5))

            ck_r2 = ac.v

            @blk.tensor
            def _(t):
                ac.wait(t, ck_r2)
                for th in range(2):
                    pe.inc(t.matmul(pb6[th][:], onesr[:],
                                    rw6[0:1, th * 512:(th + 1) * 512], start=True, stop=True))
                    pe.inc(t.matmul(pb6[2 + th][:], onesr[:],
                                    rw6[0:1, 2048 + th * 512:2048 + (th + 1) * 512],
                                    start=True, stop=True))

            ck_bc = pe.v

            @blk.scalar
            def _(s):
                pe.wait(s, ck_bc)
                for th in range(2):
                    ac.inc(s.activation(m4b[:, th * 512:(th + 1) * 512], pb6[th][:], AF.Copy))
                    ac.inc(s.activation(r4b[:, th * 512:(th + 1) * 512], pb6[2 + th][:], AF.Copy))

            ck_m4 = ac.v

            ck_x2l = [0] * FI
            ck_ow = [0] * FI
            ck_ost = [0] * FI
            for fc in range(FI):
                par = fc % 2

                @blk.sync
                def _(s, fc=fc, par=par):
                    if fc >= 2:
                        dv.wait(s, ck_ow[fc - 2])
                        ldp[par].wait(s, ck_ost[fc - 2])
                    ck_x2l[fc] = ldr[par].dma(s.dma_start(x2r[par][:], d_x2.ap()[fc]))

                @blk.vector
                def _(v, fc=fc, par=par):
                    ac.wait(v, ck_m4)
                    dv.wait(v)
                    dv.inc(v.tensor_tensor(scr6[:], acc[:, fc * TOWN:(fc + 1) * TOWN],
                                           m4b[:], OP.subtract))
                    dv.wait(v)
                    dv.inc(v.tensor_tensor(scr6[:], scr6[:], r4b[:], OP.mult))
                    dv.wait(v)
                    ldr[par].wait(v, ck_x2l[fc])
                    dv.inc(v.tensor_tensor(ob[par][:], scr6[:], x2r[par][:], OP.add))
                    ck_ow[fc] = dv.v

                @blk.sync
                def _(s, fc=fc, par=par):
                    dv.wait(s, ck_ow[fc])
                    ck_ost[fc] = ldp[par].dma(s.dma_start(d_out.ap()[fc], ob[par][:]))

            @blk.sync
            def _(s):
                ldp[0].wait(s)
                ldp[1].wait(s)

    P56.close()
    es.close()
    _regroup_blocks(nc)
    return nc


def _regroup_blocks(nc):
    ORDER = {"EngineType.SP": 0, "EngineType.Pool": 1, "EngineType.PE": 2,
             "EngineType.DVE": 3, "EngineType.Activation": 4}
    f = nc.m.functions[0]
    out, seg = [], []

    def flush():
        seg.sort(key=lambda pair: pair[0])
        out.extend(b for _, b in seg)
        seg.clear()

    for b in f.blocks:
        engs = {str(i.engine) for i in b.instructions}
        if len(engs) == 1:
            seg.append((ORDER.get(next(iter(engs)), 99), b))
        else:
            flush()
            out.append(b)
    flush()
    f.blocks = out


def _prep_shared(Wq, Wk, Wv, Wo, Wg, Wu, Wd):
    sh = {}
    sh["wq"] = np.ascontiguousarray(
        Wq.T.reshape(FI, P, 4, 512).transpose(2, 0, 1, 3))
    sh["wk"] = np.ascontiguousarray(Wk.T.reshape(FI, P, 512).transpose(1, 0, 2))
    sh["wv"] = np.ascontiguousarray(Wv.T.reshape(FI, P, 512).transpose(1, 0, 2))
    sh["wo"] = np.ascontiguousarray(
        Wo.T.reshape(FI, P, FI, P).transpose(2, 1, 0, 3))
    cb = np.arange(NCB)[:, None, None]
    jj = np.arange(4)[None, :, None]
    pp = np.arange(P)[None, None, :]
    idx = (4 * (P * cb + pp) + jj).reshape(-1)
    wgP = Wg[:, idx]
    wuP = Wu[:, idx]
    wdP = Wd[idx, :]
    sh["wg"] = np.ascontiguousarray(
        wgP.reshape(FI, P, NCB, 4, P).transpose(2, 3, 1, 0, 4))
    sh["wu"] = np.ascontiguousarray(
        wuP.reshape(FI, P, NCB, 4, P).transpose(2, 3, 1, 0, 4))
    sh["wd"] = np.ascontiguousarray(
        wdP.reshape(NCB, 4, P, 8, 256).transpose(3, 0, 2, 1, 4))
    return sh


def _prep_consts():
    import ml_dtypes
    c = {}
    iv64 = np.exp(np.arange(64, dtype=np.float64) * (-math.log(10000.0) / 64.0))
    hi = iv64.astype(np.float32)
    lo = (iv64 - hi.astype(np.float64)).astype(np.float32)
    c["ivh"] = np.ascontiguousarray(np.broadcast_to(hi[None, :], (P, 64)))
    c["ivl"] = np.ascontiguousarray(np.broadcast_to(lo[None, :], (P, 64)))
    c["identf"] = np.eye(P, dtype=np.float32)
    c["identb"] = np.eye(P, dtype=np.float32).astype(ml_dtypes.bfloat16)
    cm = (np.arange(P)[:, None] <= np.arange(P)[None, :]).astype(np.float32)
    c["cmask4"] = np.ascontiguousarray(np.tile(cm, (1, 4)))
    c["onesr"] = np.ones((1, P), dtype=np.float32)
    c["onescol"] = np.ones((P, 1), dtype=np.float32)
    c["onescolb"] = np.ones((P, 1), dtype=np.float32).astype(ml_dtypes.bfloat16)
    return c


def _prep_core(x, core, consts):
    import ml_dtypes
    b, q = core // 4, core % 4
    start = q * TOWN
    xt = np.zeros((NCH, P, FI, P), dtype=np.float32)
    pos = np.zeros((P, NCH), dtype=np.float32)
    mask = np.zeros((P, NCH), dtype=np.float32)
    for c in range(NCH):
        t0 = start - TPREF + c * P
        pos[:, c] = np.maximum(t0 + np.arange(P), 0)
        if t0 + P <= 0:
            continue
        lo = max(t0, 0)
        hit = t0 + P
        seg = x[b, lo:hit, :]
        n = seg.shape[0]
        tile = np.zeros((DM, P), dtype=np.float32)
        tile[:, P - n:] = seg.T
        xt[c] = tile.reshape(FI, P, P).transpose(1, 0, 2)
        mask[:, c][P - n:] = 1.0
    m = dict(consts)
    m["xt"] = xt
    m["pos"] = pos
    m["posrep"] = np.ascontiguousarray(np.repeat(pos, 64, axis=1))
    m["maskc"] = mask.astype(ml_dtypes.bfloat16)
    return m


def kernel(L0_attn_x, L0_attn_Wq, L0_attn_Wk, L0_attn_Wv, L0_attn_Wo,
           L0_ffn_w_gate, L0_ffn_w_up, L0_ffn_w_down):
    x = np.ascontiguousarray(np.asarray(L0_attn_x, dtype=np.float32))
    sh = _prep_shared(np.asarray(L0_attn_Wq, dtype=np.float32),
                      np.asarray(L0_attn_Wk, dtype=np.float32),
                      np.asarray(L0_attn_Wv, dtype=np.float32),
                      np.asarray(L0_attn_Wo, dtype=np.float32),
                      np.asarray(L0_ffn_w_gate, dtype=np.float32),
                      np.asarray(L0_ffn_w_up, dtype=np.float32),
                      np.asarray(L0_ffn_w_down, dtype=np.float32))
    consts = _prep_consts()
    consts.update(sh)
    in_maps = [_prep_core(x, c, consts) for c in range(8)]
    nc = build_program()
    res = run_bass_kernel_spmd(nc, in_maps, core_ids=list(range(8)))
    out = np.empty((2, 4096, DM), dtype=np.float32)
    for c in range(8):
        b, q = c // 4, c % 4
        o = np.asarray(res.results[c]["outT"])
        out[b, q * TOWN:(q + 1) * TOWN, :] = o.reshape(DM, TOWN).T
    return out


# version 14 (final)
# speedup vs baseline: 1.1713x; 1.0139x over previous
import contextlib
import math

import numpy as np

import concourse.bass as bass
import concourse.mybir as mybir
from concourse.alu_op_type import AluOpType as OP
from concourse.bass_utils import run_bass_kernel_spmd

F32 = mybir.dt.float32
F32R = mybir.dt.float32r
BF16 = mybir.dt.bfloat16
I32 = mybir.dt.int32
AF = mybir.ActivationFunctionType

P = 128
DM = 2048
FI = DM // P
TOWN = 1024
NOC = TOWN // P
TPREF = 3072
NPC = TPREF // P
NCH = NPC + NOC
KVH = 4
H = 16
IDIM = 8192
NCB = 16
CBG = 4
EPS_LN = 1e-5
EPS_ATT = 1e-6
PB = 3.141592502593994
TWO_PI = 2 * math.pi
CW1 = 6.28125
CW2 = float(np.float32(TWO_PI - CW1))
INV_DM = 1.0 / DM

DEBUG_TAPS = []


class Ctr:
    def __init__(self, es, nc, name):
        self.sem = es.enter_context(nc.semaphore(name))
        self.v = 0

    def inc(self, ins, n=1):
        ins.then_inc(self.sem, n)
        self.v += n
        return self.v

    def dma(self, ins):
        return self.inc(ins, 16)

    def wait(self, eng, v=None):
        eng.wait_ge(self.sem, self.v if v is None else v)


def build_program():
    nc = bass.Bass("TRN2", target_bir_lowering=False, debug=False)

    d_xt = nc.dram_tensor("xt", [NCH, P, FI, P], F32R, kind="ExternalInput")
    d_pos = nc.dram_tensor("pos", [P, NCH], F32, kind="ExternalInput")
    d_posr = nc.dram_tensor("posrep", [P, NCH * 64], F32, kind="ExternalInput")
    d_mask = nc.dram_tensor("maskc", [P, NCH], BF16, kind="ExternalInput")
    d_ivh = nc.dram_tensor("ivh", [P, 64], F32, kind="ExternalInput")
    d_ivl = nc.dram_tensor("ivl", [P, 64], F32, kind="ExternalInput")
    d_idf = nc.dram_tensor("identf", [P, P], F32, kind="ExternalInput")
    d_idb = nc.dram_tensor("identb", [P, P], BF16, kind="ExternalInput")
    d_cmask = nc.dram_tensor("cmask4", [P, 4 * P], F32, kind="ExternalInput")
    d_onesr = nc.dram_tensor("onesr", [1, P], F32R, kind="ExternalInput")
    d_onesc = nc.dram_tensor("onescol", [P, 1], F32R, kind="ExternalInput")
    d_onescb = nc.dram_tensor("onescolb", [P, 1], BF16, kind="ExternalInput")
    d_wk = nc.dram_tensor("wk", [P, FI, 512], F32, kind="ExternalInput")
    d_wv = nc.dram_tensor("wv", [P, FI, 512], F32, kind="ExternalInput")
    d_wq = nc.dram_tensor("wq", [4, FI, P, 512], F32, kind="ExternalInput")
    d_wo = nc.dram_tensor("wo", [FI, P, FI, P], F32, kind="ExternalInput")
    d_wg = nc.dram_tensor("wg", [NCB, 4, P, FI, P], F32, kind="ExternalInput")
    d_wu = nc.dram_tensor("wu", [NCB, 4, P, FI, P], F32, kind="ExternalInput")
    d_wd = nc.dram_tensor("wd", [8, NCB, P, 4, 256], F32, kind="ExternalInput")
    d_out = nc.dram_tensor("outT", [FI, P, TOWN], F32, kind="ExternalOutput")
    d_ao = nc.dram_tensor("aopark", [FI, P, TOWN], F32R)
    d_x2 = nc.dram_tensor("x2park", [FI, P, TOWN], F32R)
    d_st = nc.dram_tensor("st3park", [1, 2048], F32R)
    d_tap = {}
    for name in DEBUG_TAPS:
        if name == "kv":
            d_tap["tap_kf"] = nc.dram_tensor("tap_kf", [P, NOC * 512], F32, kind="ExternalOutput")
            d_tap["tap_v"] = nc.dram_tensor("tap_v", [P, NOC * 512], F32, kind="ExternalOutput")
            d_tap["tap_st"] = nc.dram_tensor("tap_st", [P, 512], F32, kind="ExternalOutput")
            d_tap["tap_z"] = nc.dram_tensor("tap_z", [P, 4], F32, kind="ExternalOutput")
        if name == "q":
            d_tap["tap_qfT"] = nc.dram_tensor("tap_qfT", [P, H * TOWN], F32, kind="ExternalOutput")
        if name == "o":
            d_tap["tap_oT"] = nc.dram_tensor("tap_oT", [P, H * TOWN], F32, kind="ExternalOutput")
        if name == "x2":
            d_tap["tap_x2"] = nc.dram_tensor("tap_x2", [FI, P, TOWN], F32, kind="ExternalOutput")
        if name == "x3":
            d_tap["tap_x3"] = nc.dram_tensor("tap_x3", [P, FI * TOWN], F32, kind="ExternalOutput")
        if name == "hs":
            d_tap["tap_hs"] = nc.dram_tensor("tap_hs", [P, 4 * TOWN], F32, kind="ExternalOutput")

    es = contextlib.ExitStack()

    ivh = es.enter_context(nc.sbuf_tensor("s_ivh", [P, 64], F32))
    ivl = es.enter_context(nc.sbuf_tensor("s_ivl", [P, 64], F32))
    posS = es.enter_context(nc.sbuf_tensor("posS", [P, NCH], F32))
    maskS = es.enter_context(nc.sbuf_tensor("maskS", [P, NCH], BF16))
    idf = es.enter_context(nc.sbuf_tensor("idf", [P, P], F32))
    idb = es.enter_context(nc.sbuf_tensor("idb", [P, P], BF16))
    onesr = es.enter_context(nc.sbuf_tensor("s_onesr", [1, P], F32R))
    onesc = es.enter_context(nc.sbuf_tensor("onesc", [P, 1], F32R))
    onescb = es.enter_context(nc.sbuf_tensor("onescb", [P, 1], BF16))

    ld = Ctr(es, nc, "c_ld")
    gq = Ctr(es, nc, "c_gq")
    vq = Ctr(es, nc, "c_vq")
    pe = Ctr(es, nc, "c_pe")
    ac = Ctr(es, nc, "c_ac")
    dv = Ctr(es, nc, "c_dv")
    gqp = [Ctr(es, nc, f"c_gqp{i}") for i in range(8)]
    crq = [Ctr(es, nc, f"c_crq{i}") for i in range(2)]
    gqw = [Ctr(es, nc, f"c_gqw{i}") for i in range(4)]
    ldr = [Ctr(es, nc, f"c_ldr{i}") for i in range(2)]
    ldp = [Ctr(es, nc, f"c_ldp{i}") for i in range(2)]

    L_A = contextlib.ExitStack()
    L_B = contextlib.ExitStack()
    stp = L_B.enter_context(nc.sbuf_tensor("stp", [P, 512], F32))
    zpS = L_B.enter_context(nc.sbuf_tensor("zpS", [P, 4], F32))
    cmask4 = L_B.enter_context(nc.sbuf_tensor("s_cmask4", [P, 4 * P], F32))
    kf_tm = L_B.enter_context(nc.sbuf_tensor("kf_tm", [P, NOC * 512], BF16))
    kfT = L_B.enter_context(nc.sbuf_tensor("kfT", [P, KVH * TOWN], BF16))
    v_tm = L_B.enter_context(nc.sbuf_tensor("v_tm", [P, NOC * 512], BF16))
    rcolA = L_B.enter_context(nc.sbuf_tensor("rcolA", [P, NCH], F32R))
    rowsNm8 = L_B.enter_context(nc.sbuf_tensor("rowsNm8", [1, NOC * P], F32R))

    L_C = contextlib.ExitStack()
    sinT = L_C.enter_context(nc.sbuf_tensor("sinT", [P, NCH * 64], F32))
    cosT = L_C.enter_context(nc.sbuf_tensor("cosT", [P, NCH * 64], F32))
    konerow = L_C.enter_context(nc.sbuf_tensor("konerow", [1, 1024], F32R))
    xob = L_C.enter_context(nc.sbuf_tensor("xob", [P, NOC * DM], BF16))

    def xob_tile(c):
        return xob[:, c * DM:(c + 1) * DM]

    def col_bc(t, c, n):
        ap = t[:, c:c + 1]
        return bass.AP(ap.tensor, ap.offset, [[ap.ap[0][0], P], [0, n]])

    def strided_half(t, base_off, blocks, blk_stride, width):
        ap = t[:]
        return bass.AP(ap.tensor, ap.offset + base_off,
                       [[ap.ap[0][0], P], [blk_stride, blocks], [1, width]])

    def rope_bc(base, c):
        ap = base[:]
        return bass.AP(ap.tensor, ap.offset + c * 64,
                       [[ap.ap[0][0], P], [0, 8], [1, 64]])

    def crep_src(c, which):
        base = cosT if which == 0 else sinT
        ap = base[:]
        return bass.AP(ap.tensor, ap.offset + c * 64,
                       [[NCH * 64, P], [0, 8], [1, 64]])

    with contextlib.ExitStack() as ps:
        th = ps.enter_context(nc.sbuf_tensor("p0_th", [P, NCH * 64], F32))
        uu = ps.enter_context(nc.sbuf_tensor("p0_u", [P, NCH * 64], F32))
        ki = ps.enter_context(nc.sbuf_tensor("p0_ki", [P, NCH * 64], I32))
        kw = ps.enter_context(nc.sbuf_tensor("p0_kw", [P, NCH * 64], F32))
        prp = ps.enter_context(nc.sbuf_tensor("p0_prp", [P, NCH * 64], F32))
        ivr = ps.enter_context(nc.sbuf_tensor("p0_ivr", [P, 2 * NCH * 64], F32))
        with nc.Block() as blk:
            @blk.sync
            def _(s):
                for t, d in [(ivh, d_ivh), (ivl, d_ivl), (posS, d_pos), (maskS, d_mask),
                             (idf, d_idf), (idb, d_idb), (cmask4, d_cmask),
                             (onesr, d_onesr), (onesc, d_onesc), (onescb, d_onescb)]:
                    ld.dma(s.dma_start(t[:], d.ap()))
                ld.dma(s.dma_start(prp[:], d_posr.ap()))
                ld.wait(s)
                ivh_src = bass.AP(ivh[:].tensor, ivh[:].offset,
                                  [[64, P], [0, NCH], [1, 64]])
                ivl_src = bass.AP(ivl[:].tensor, ivl[:].offset,
                                  [[64, P], [0, NCH], [1, 64]])
                ld.dma(s.dma_start(ivr[:, 0:NCH * 64].rearrange("p (c j) -> p c j", c=NCH), ivh_src))
                ld.dma(s.dma_start(ivr[:, NCH * 64:].rearrange("p (c j) -> p c j", c=NCH), ivl_src))
                ld.wait(s)

            ld_all = ld.v

            @blk.gpsimd
            def _(g):
                for c in range(NOC):
                    gq.dma(g.dma_start(
                        xob_tile(c).rearrange("p (f t) -> p f t", f=FI),
                        d_xt.ap()[NPC + c]))
                gq.wait(g)

            @blk.vector
            def _(v):
                ld.wait(v, ld_all)
                NW = NCH * 64
                dv.inc(v.tensor_tensor(th[:], prp[:], ivr[:, NW:], OP.mult))
                dv.wait(v)
                dv.inc(v.scalar_tensor_tensor(uu[:], prp[:], 1.0, ivr[:, 0:NW], OP.mult, OP.mult))
                dv.wait(v)
                dv.inc(v.tensor_tensor(th[:], uu[:], th[:], OP.add))
                dv.wait(v)
                dv.inc(v.tensor_scalar(uu[:], th[:], 1.0 / TWO_PI, None, OP.mult))
                dv.wait(v)
                dv.inc(v.tensor_copy(ki[:], uu[:]))
                dv.wait(v)
                dv.inc(v.tensor_copy(kw[:], ki[:]))
                dv.wait(v)
                dv.inc(v.scalar_tensor_tensor(th[:], kw[:], -CW1, th[:], OP.mult, OP.add))
                dv.wait(v)
                dv.inc(v.scalar_tensor_tensor(th[:], kw[:], -CW2, th[:], OP.mult, OP.add))
                dv.wait(v)
                dv.inc(v.tensor_scalar(uu[:], th[:], PB, None, OP.is_gt))
                dv.wait(v)
                dv.inc(v.scalar_tensor_tensor(th[:], uu[:], -TWO_PI, th[:], OP.mult, OP.add))
                dv.wait(v)
                dv.inc(v.tensor_scalar(uu[:], th[:], -PB, None, OP.is_lt))
                dv.wait(v)
                dv.inc(v.scalar_tensor_tensor(th[:], uu[:], TWO_PI, th[:], OP.mult, OP.add))
                dv.wait(v)
                dv.inc(v.tensor_scalar(uu[:], th[:], PB, -PB, OP.min, OP.max))
                dv.inc(v.tensor_scalar(kw[:], th[:], math.pi / 2, None, OP.add))
                dv.wait(v)
                dv.inc(v.tensor_scalar(th[:], kw[:], PB, None, OP.is_gt))
                dv.wait(v)
                dv.inc(v.scalar_tensor_tensor(kw[:], th[:], -TWO_PI, kw[:], OP.mult, OP.add))
                dv.wait(v)
                dv.inc(v.tensor_scalar(kw[:], kw[:], PB, -PB, OP.min, OP.max))

            dv_tab = dv.v

            @blk.scalar
            def _(s):
                dv.wait(s, dv_tab)
                ac.inc(s.activation(sinT[:], uu[:], AF.Sin))
                ac.inc(s.activation(cosT[:], kw[:], AF.Sin))

    with contextlib.ExitStack() as ps:
        NPB = 8
        xpb = [ps.enter_context(nc.sbuf_tensor(f"xp{i}", [P, DM], BF16)) for i in range(NPB)]
        wkb = ps.enter_context(nc.sbuf_tensor("wkb", [P, FI * 512], BF16))
        wvb = ps.enter_context(nc.sbuf_tensor("wvb", [P, FI * 512], BF16))
        krop = [ps.enter_context(nc.sbuf_tensor(f"krop{i}", [P, 512], F32)) for i in range(2)]
        kaux = [ps.enter_context(nc.sbuf_tensor(f"kaux{i}", [P, 1024], F32)) for i in range(2)]
        kfpb = [ps.enter_context(nc.sbuf_tensor(f"kfpb{i}", [P, 512], BF16)) for i in range(2)]
        vpb = [ps.enter_context(nc.sbuf_tensor(f"vpb{i}", [P, 512], BF16)) for i in range(2)]
        sqS = [ps.enter_context(nc.sbuf_tensor("sqa0", [P, DM], BF16))] * 2
        rPS = [ps.enter_context(nc.sbuf_tensor("rPS0", [1, 1024], F32R))] * 2
        rSg = [ps.enter_context(nc.sbuf_tensor(f"rSg{i}", [1, 1024], F32R)) for i in range(2)]
        rQg = [ps.enter_context(nc.sbuf_tensor(f"rQg{i}", [1, 1024], F32R)) for i in range(2)]
        gtmp = ps.enter_context(nc.sbuf_tensor("gtmp", [1, 1024], F32R))

        pkv = [ps.enter_context(nc.psum_tensor(f"pkv{i}", [P, 1024], F32)) for i in range(2)]
        pstate = ps.enter_context(nc.psum_tensor("pstate", [P, 512], F32))
        pzm = ps.enter_context(nc.psum_tensor("pzm", [P, 512], F32))
        pstA = ps.enter_context(nc.psum_tensor("pstA", [P, 1024], F32))
        pzp = pzm[:, 0:4]
        pmT = pzm[:, 256:512].bitcast(BF16)

        ck_xbld = [0] * NCH
        ck_sq = [0] * NCH
        ck_smm = [0] * NCH
        ck_sev = [0] * NCH
        ck_fold = [0] * NCH
        ck_grp = [0] * (NCH // 8)
        ck_rows_g = [0] * (NCH // 8)
        ck_k1use = [0] * (NCH // 8)
        ck_kv = [0] * NCH
        ck_aff = [0] * NCH
        ck_rope = [0] * NCH
        ck_emin = [0] * NCH
        ck_exp = [0] * NCH
        ck_elu = [0] * NCH
        ck_kft = [0] * NCH
        ck_kfte = [0] * NCH
        ck_state = [0] * NCH

        def xb(c):
            return xpb[c % NPB][:] if c < NPC else xob_tile(c - NPC)

        def kf_dst(c):
            if c < NPC:
                return kfpb[c % 2][:]
            return kf_tm[:, (c - NPC) * 512:(c - NPC + 1) * 512]

        def v_dst(c):
            if c < NPC:
                return vpb[c % 2][:]
            return v_tm[:, (c - NPC) * 512:(c - NPC + 1) * 512]

        with nc.Block() as blk:
            @blk.gpsimd
            def _(g):
                gq.dma(g.dma_start(wkb[:].rearrange("p (f n) -> p f n", f=FI), d_wk.ap()))
                gq.dma(g.dma_start(wvb[:].rearrange("p (f n) -> p f n", f=FI), d_wv.ap()))
                gq.wait(g)

            ck_wkv = gq.v

            @blk.tensor
            def _(t):
                gq.wait(t, ck_wkv)
                for fi in range(FI):
                    pe.inc(t.matmul(pkv[0][0:1, 0:512], onescb[:],
                                    wkb[:, fi * 512:(fi + 1) * 512],
                                    start=(fi == 0), stop=(fi == FI - 1)))
                    pe.inc(t.matmul(pkv[0][0:1, 512:1024], onescb[:],
                                    wvb[:, fi * 512:(fi + 1) * 512],
                                    start=(fi == 0), stop=(fi == FI - 1)))

            ck_one_mm = pe.v

            @blk.scalar
            def _(s):
                pe.wait(s, ck_one_mm)
                ac.inc(s.activation(konerow[:], pkv[0][0:1, :], AF.Copy))

            ck_kvone = ac.v

            pend = [None]
            pend_tail = [None]
            for grp in range(NCH // 8):
                gpar = grp % 2
                for c in range(grp * 8, grp * 8 + 8):
                    own = c >= NPC
                    spar = c % 2

                    if not own:
                        @blk.gpsimd
                        def _(g, c=c):
                            if c >= NPB:
                                pe.wait(g, ck_kv[c - NPB])
                                dv.wait(g, ck_sq[c - NPB])
                            ck_xbld[c] = gqp[c % NPB].dma(g.dma_start(
                                xpb[c % NPB][:].rearrange("p (f t) -> p f t", f=FI),
                                d_xt.ap()[c]))

                    @blk.vector
                    def _(v, c=c, spar=spar, own=own):
                        if not own:
                            gqp[c % NPB].wait(v, ck_xbld[c])
                        if c >= 1:
                            pe.wait(v, ck_smm[c - 1])
                        xbc = xb(c)
                        ck_sq[c] = dv.inc(v.tensor_tensor(sqS[spar][:], xbc, xbc,
                                                          OP.mult))

                    @blk.tensor
                    def _(t, c=c, spar=spar, own=own, grp=grp):
                        if not own:
                            gqp[c % NPB].wait(t, ck_xbld[c])
                        dv.wait(t, ck_sq[c])
                        if c >= 1:
                            ac.wait(t, ck_sev[c - 1])
                        if c % 8 == 0 and grp >= 1:
                            ac.wait(t, ck_grp[grp - 1])
                        for f4 in range(4):
                            pe.inc(t.matmul(pstA[0:1, 0:512], onescb[:],
                                            xb(c)[:, f4 * 512:(f4 + 1) * 512],
                                            start=(f4 == 0), stop=(f4 == 3)))
                            pe.inc(t.matmul(pstA[0:1, 512:1024], onescb[:],
                                            sqS[spar][:, f4 * 512:(f4 + 1) * 512],
                                            start=(f4 == 0), stop=(f4 == 3)))
                        ck_smm[c] = pe.v

                    @blk.scalar
                    def _(s, c=c, spar=spar):
                        pe.wait(s, ck_smm[c])
                        if c >= 1:
                            dv.wait(s, ck_fold[c - 1])
                        ck_sev[c] = ac.inc(s.activation(rPS[spar][:], pstA[0:1, :],
                                                        AF.Copy))

                    @blk.vector
                    def _(v, c=c, spar=spar, grp=grp, gpar=gpar):
                        ac.wait(v, ck_sev[c])
                        if grp >= 2 and c % 8 == 0:
                            pe.wait(v, ck_k1use[grp - 2])
                        ps_ = rPS[spar]
                        k = (c % 8) * P
                        sl = slice(k, k + P)
                        dv.inc(v.tensor_tensor(rSg[gpar][0:1, sl], ps_[0:1, 0:128],
                                               ps_[0:1, 128:256], OP.add))
                        dv.inc(v.tensor_tensor(rQg[gpar][0:1, sl], ps_[0:1, 512:640],
                                               ps_[0:1, 640:768], OP.add))
                        dv.wait(v)
                        dv.inc(v.scalar_tensor_tensor(rSg[gpar][0:1, sl],
                                                      ps_[0:1, 256:384], 1.0,
                                                      rSg[gpar][0:1, sl],
                                                      OP.mult, OP.add))
                        dv.inc(v.scalar_tensor_tensor(rQg[gpar][0:1, sl],
                                                      ps_[0:1, 768:896], 1.0,
                                                      rQg[gpar][0:1, sl],
                                                      OP.mult, OP.add))
                        dv.wait(v)
                        dv.inc(v.scalar_tensor_tensor(rSg[gpar][0:1, sl],
                                                      ps_[0:1, 384:512], 1.0,
                                                      rSg[gpar][0:1, sl],
                                                      OP.mult, OP.add))
                        dv.inc(v.scalar_tensor_tensor(rQg[gpar][0:1, sl],
                                                      ps_[0:1, 896:1024], 1.0,
                                                      rQg[gpar][0:1, sl],
                                                      OP.mult, OP.add))
                        ck_fold[c] = dv.v

                @blk.vector
                def _(v, grp=grp, gpar=gpar):
                    dv.wait(v, ck_fold[grp * 8 + 7])
                    dv.inc(v.tensor_scalar(rSg[gpar][:], rSg[gpar][:], -INV_DM,
                                           None, OP.mult))
                    dv.wait(v)
                    dv.inc(v.tensor_tensor(gtmp[:], rSg[gpar][:], rSg[gpar][:],
                                           OP.mult))
                    dv.wait(v)
                    dv.inc(v.scalar_tensor_tensor(rQg[gpar][:], rQg[gpar][:], INV_DM,
                                                  gtmp[:], OP.mult, OP.subtract))
                    dv.wait(v)
                    dv.inc(v.tensor_scalar(rQg[gpar][:], rQg[gpar][:], EPS_LN,
                                           None, OP.add))

                ck_gvar = dv.v
                ck_rows_g[grp] = ck_gvar

                @blk.scalar
                def _(s, gpar=gpar, ck_gvar=ck_gvar):
                    dv.wait(s, ck_gvar)
                    ac.inc(s.activation(gtmp[:], rQg[gpar][:], AF.Ln))
                    ac.wait(s)
                    ac.inc(s.activation(rQg[gpar][:], gtmp[:], AF.Exp, scale=-0.5))

                ck_grstd = ac.v

                def mk_rt(grp=grp, gpar=gpar, ck_grstd=ck_grstd):
                    def emit():
                        @blk.tensor
                        def _(t, grp=grp, gpar=gpar, ck_grstd=ck_grstd):
                            ac.wait(t, ck_grstd)
                            if grp >= 1:
                                ac.wait(t, ck_grp[grp - 1])
                            for k in range(8):
                                pe.inc(t.transpose(
                                    pstA[:, k:k + 1],
                                    rQg[gpar][0:1, k * P:(k + 1) * P].bitcast(F32),
                                    idf[0:1, 0:1]))

                        ck_rt = pe.v

                        @blk.scalar
                        def _(s, grp=grp, ck_rt=ck_rt):
                            pe.wait(s, ck_rt)
                            ck_grp[grp] = ac.inc(s.activation(
                                rcolA[:, grp * 8:(grp + 1) * 8], pstA[:, 0:8],
                                AF.Copy))
                    return emit

                pend_rt = mk_rt()

                if grp == 3:
                    @blk.vector
                    def _(v, gpar=gpar, ck_grstd=ck_grstd):
                        ac.wait(v, ck_grstd)
                        dv.inc(v.tensor_copy(rowsNm8[:], rSg[gpar][:]))

                for c in range(grp * 8, grp * 8 + 8):
                    par = c % 2
                    own = c >= NPC
                    oc = c - NPC

                    if pend_tail[0] is not None and c % 8 == 1:
                        pend_tail[0]()
                        pend_tail[0] = None

                    @blk.tensor
                    def _(t, c=c, par=par, own=own, grp=grp, gpar=gpar):
                        if not own:
                            gqp[c % NPB].wait(t, ck_xbld[c])
                        gq.wait(t, ck_wkv)
                        dv.wait(t, ck_rows_g[grp])
                        if c < 2:
                            ac.wait(t, ck_kvone)
                        if c >= 2:
                            ac.wait(t, ck_aff[c - 2])
                        xbc = xb(c)
                        for fi in range(FI):
                            pe.inc(t.matmul(pkv[par][:, 0:512],
                                            xbc[:, fi * P:(fi + 1) * P],
                                            wkb[:, fi * 512:(fi + 1) * 512],
                                            start=(fi == 0), stop=False))
                            pe.inc(t.matmul(pkv[par][:, 512:1024],
                                            xbc[:, fi * P:(fi + 1) * P],
                                            wvb[:, fi * 512:(fi + 1) * 512],
                                            start=(fi == 0), stop=False))
                        nr = rSg[gpar][0:1, (c % 8) * P:(c % 8 + 1) * P]
                        pe.inc(t.matmul(pkv[par][:, 0:512], nr, konerow[0:1, 0:512],
                                        start=False, stop=True))
                        pe.inc(t.matmul(pkv[par][:, 512:1024], nr,
                                        konerow[0:1, 512:1024],
                                        start=False, stop=True))
                        ck_kv[c] = pe.v
                        if c % 8 == 7:
                            ck_k1use[grp] = pe.v

                    if c % 8 == 0 and pend_rt is not None:
                        pend_rt()
                        pend_rt = None

                    @blk.scalar
                    def _(s, c=c, par=par, own=own, grp=grp):
                        pe.wait(s, ck_kv[c])
                        ac.wait(s, ck_grp[grp])
                        if c >= 2:
                            dv.wait(s, ck_rope[c - 2])
                            if c - 2 < NPC:
                                pe.wait(s, ck_state[c - 2])
                            else:
                                pe.wait(s, ck_kft[c - 2])
                        rc = rcolA[:, c:c + 1].bitcast(F32)
                        ac.inc(s.activation(krop[par][:], pkv[par][:, 0:512],
                                            AF.Copy, scale=rc))
                        ck_aff[c] = ac.inc(s.activation(v_dst(c),
                                                        pkv[par][:, 512:1024],
                                                        AF.Copy, scale=rc))

                    @blk.vector
                    def _(v, c=c, par=par):
                        ac.wait(v, ck_aff[c])
                        if c >= 2:
                            dv.wait(v, ck_emin[c - 2])
                            ac.wait(v, ck_exp[c - 2])
                        kr = krop[par]
                        ka = kaux[par]
                        kr3 = kr[:].rearrange("p (r t) -> p r t", r=8)
                        dv.inc(v.tensor_tensor(
                            ka[:, 0:512].rearrange("p (r t) -> p r t", r=8),
                            kr3, rope_bc(cosT, c), OP.mult))
                        dv.inc(v.tensor_tensor(
                            ka[:, 512:1024].rearrange("p (r t) -> p r t", r=8),
                            kr3, rope_bc(sinT, c), OP.mult))
                        dv.wait(v)
                        alo = strided_half(ka, 0, 4, 128, 64)
                        ahi = strided_half(ka, 64, 4, 128, 64)
                        plo = strided_half(ka, 512, 4, 128, 64)
                        phi = strided_half(ka, 576, 4, 128, 64)
                        olo = strided_half(kr, 0, 4, 128, 64)
                        ohi = strided_half(kr, 64, 4, 128, 64)
                        dv.inc(v.tensor_tensor(olo, alo, phi, OP.subtract))
                        dv.inc(v.tensor_tensor(ohi, ahi, plo, OP.add))
                        ck_rope[c] = dv.v
                        dv.wait(v)
                        dv.inc(v.tensor_scalar(ka[:, 0:512], kr[:], 0.0, None,
                                               OP.min))
                        ck_emin[c] = dv.v

                    @blk.scalar
                    def _(s, c=c, par=par):
                        dv.wait(s, ck_emin[c])
                        ck_exp[c] = ac.inc(s.activation(kaux[par][:, 512:1024],
                                                        kaux[par][:, 0:512], AF.Exp))

                    @blk.vector
                    def _(v, c=c, par=par, own=own):
                        ac.wait(v, ck_exp[c])
                        if (not own) and c >= 2:
                            pe.wait(v, ck_state[c - 2])
                        dv.inc(v.scalar_tensor_tensor(kf_dst(c), krop[par][:], 0.0,
                                                      kaux[par][:, 512:1024],
                                                      OP.max, OP.add))
                        ck_elu[c] = dv.v

                    def mk_tail(c=c, oc=oc, own=own):
                        def emit():
                            if own:
                                @blk.tensor
                                def _(t, c=c, oc=oc):
                                    dv.wait(t, ck_elu[c])
                                    if c >= NPC + 1:
                                        ac.wait(t, ck_kfte[c - 1])
                                    src2 = kf_tm[:, oc * 512:(oc + 1) * 512]
                                    for h in range(KVH):
                                        pe.inc(t.transpose(
                                            pmT[:, h * 128:(h + 1) * 128],
                                            src2[:, h * 128:(h + 1) * 128],
                                            idb[:]))
                                    ck_kft[c] = pe.v

                                @blk.scalar
                                def _(s, c=c, oc=oc):
                                    pe.wait(s, ck_kft[c])
                                    for h in range(KVH):
                                        ac.inc(s.activation(
                                            kfT[:, h * TOWN + oc * P:
                                                h * TOWN + (oc + 1) * P],
                                            pmT[:, h * 128:(h + 1) * 128], AF.Copy))
                                    ck_kfte[c] = ac.v
                            else:
                                @blk.tensor
                                def _(t, c=c):
                                    dv.wait(t, ck_elu[c])
                                    kfb = kf_dst(c)
                                    vb = v_dst(c)
                                    last = (c == NPC - 1)
                                    for h in range(KVH):
                                        pe.inc(t.matmul(
                                            pstate[:, h * 128:(h + 1) * 128],
                                            kfb[:, h * 128:(h + 1) * 128],
                                            vb[:, h * 128:(h + 1) * 128],
                                            start=(c == 0 and h == 0),
                                            stop=(last and h == KVH - 1)))
                                        pe.inc(t.matmul(
                                            pzp[:, h:h + 1],
                                            kfb[:, h * 128:(h + 1) * 128],
                                            maskS[:, c:c + 1],
                                            start=(c == 0 and h == 0),
                                            stop=(last and h == KVH - 1)))
                                    ck_state[c] = pe.v
                        return emit

                    if pend[0] is not None:
                        pend[0]()
                    if c % 8 == 7:
                        pend[0] = None
                        pend_tail[0] = mk_tail()
                    else:
                        pend[0] = mk_tail()

            if pend_tail[0] is not None:
                pend_tail[0]()
                pend_tail[0] = None

            @blk.scalar
            def _(s):
                pe.wait(s, ck_state[NPC - 1])
                ac.inc(s.activation(stp[:], pstate[:], AF.Copy))
                ac.inc(s.activation(zpS[:], pzp, AF.Copy))

    qoT = L_A.enter_context(nc.sbuf_tensor("qoT", [P, H * TOWN], BF16))

    with contextlib.ExitStack() as ps:
        wqb = [ps.enter_context(nc.sbuf_tensor(f"wqb{i}", [P, 512], BF16)) for i in range(4)]
        qful = [ps.enter_context(nc.sbuf_tensor(f"qful{i}", [P, 512], F32)) for i in range(2)]
        qaux = [ps.enter_context(nc.sbuf_tensor(f"qaux{i}", [P, 1024], F32)) for i in range(2)]
        qbf = [ps.enter_context(nc.sbuf_tensor(f"qbf{i}", [P, 512], BF16)) for i in range(2)]
        qonerow = ps.enter_context(nc.sbuf_tensor("qonerow", [1, 512], F32R))
        pq = [ps.enter_context(nc.psum_tensor(f"pq{i}", [P, 2048], F32)) for i in range(2)]

        lastu = {"rope": [0, 0], "emin": [0, 0], "exp": [0, 0], "ft": [0, 0],
                 "crep": [0, 0], "wq": [0, 0, 0, 0], "qelu": [0, 0]}
        pq_free = [[0] * 4, [0] * 4]
        with nc.Block() as blk:
            chains_seq = [0]
            for eq in range(4):
                half_k1 = [None, None]
                chains = [None, None]
                for half in range(2):
                    tcs = list(range(half * 4, half * 4 + 4))
                    pqh = pq[half]
                    ck_wld = [0] * FI
                    ck_use = [0] * FI

                    for fi in range(FI):
                        @blk.gpsimd
                        def _(g, eq=eq, fi=fi, ck_wld=ck_wld, ck_use=ck_use):
                            if fi >= 4:
                                pe.wait(g, ck_use[fi - 4])
                            elif eq + half > 0:
                                pe.wait(g, lastu["wq"][fi % 4])
                            ck_wld[fi] = gqw[fi % 4].dma(g.dma_start(wqb[fi % 4][:],
                                                                     d_wq.ap()[eq][fi]))

                        @blk.tensor
                        def _(t, eq=eq, half=half, fi=fi, tcs=tcs, pqh=pqh,
                              ck_wld=ck_wld, ck_use=ck_use):
                            gqw[fi % 4].wait(t, ck_wld[fi])
                            if fi == 0:
                                for j in range(4):
                                    ac.wait(t, pq_free[half][j])
                            if half == 0:
                                if eq > 0 and fi == 0:
                                    ac.wait(t, pq_free[1][0])
                                pe.inc(t.matmul(pq[1][0:1, 0:512], onescb[:],
                                                wqb[fi % 4][:],
                                                start=(fi == 0), stop=(fi == FI - 1)))
                            for j, tc in enumerate(tcs):
                                pe.inc(t.matmul(pqh[:, j * 512:(j + 1) * 512],
                                                xob_tile(tc)[:, fi * P:(fi + 1) * P],
                                                wqb[fi % 4][:],
                                                start=(fi == 0), stop=False))
                            ck_use[fi] = pe.v
                            lastu["wq"][fi % 4] = pe.v

                    ck_qmm = pe.v

                    if half == 0:
                        @blk.scalar
                        def _(s, ck_qmm=ck_qmm):
                            pe.wait(s, ck_qmm)
                            ac.inc(s.activation(qonerow[:], pq[1][0:1, 0:512],
                                                AF.Copy))

                        ck_qone = ac.v
                        pq_free[1][0] = ck_qone

                    ck_k1s = [0] * 4
                    for j, tc in enumerate(tcs):
                        @blk.tensor
                        def _(t, j=j, tc=tc, pqh=pqh, ck_qone=ck_qone):
                            ac.wait(t, ck_qone)
                            pe.inc(t.matmul(pqh[:, j * 512:(j + 1) * 512],
                                            rowsNm8[0:1, tc * P:(tc + 1) * P],
                                            qonerow[:], start=False, stop=True))
                        ck_k1s[j] = pe.v

                    def mk_chains(eq=eq, half=half, tcs=tcs, pqh=pqh, ck_k1s=ck_k1s):
                        def emit():
                            nonlocal_seq = [None]
                            for j, tc in enumerate(tcs):
                                par = chains_seq[0] % 2
                                chains_seq[0] += 1

                                @blk.scalar
                                def _(s, j=j, tc=tc, par=par, pqh=pqh):
                                    pe.wait(s, ck_k1s[j])
                                    dv.wait(s, lastu["rope"][par])
                                    ac.inc(s.activation(
                                        qful[par][:],
                                        pqh[:, j * 512:(j + 1) * 512], AF.Copy,
                                        scale=rcolA[:, NPC + tc:NPC + tc + 1].bitcast(F32)))

                                ck_qaff = ac.v

                                @blk.vector
                                def _(v, tc=tc, par=par, ck_qaff=ck_qaff):
                                    ac.wait(v, ck_qaff)
                                    dv.wait(v, lastu["emin"][par])
                                    ac.wait(v, lastu["exp"][par])
                                    ka = qaux[par]
                                    kr = qful[par]
                                    kr3 = kr[:].rearrange("p (r t) -> p r t", r=8)
                                    dv.inc(v.tensor_tensor(
                                        ka[:, 0:512].rearrange("p (r t) -> p r t", r=8),
                                        kr3, rope_bc(cosT, NPC + tc), OP.mult))
                                    dv.inc(v.tensor_tensor(
                                        ka[:, 512:1024].rearrange("p (r t) -> p r t", r=8),
                                        kr3, rope_bc(sinT, NPC + tc), OP.mult))
                                    dv.wait(v)
                                    alo = strided_half(ka, 0, 4, 128, 64)
                                    ahi = strided_half(ka, 64, 4, 128, 64)
                                    plo = strided_half(ka, 512, 4, 128, 64)
                                    phi = strided_half(ka, 576, 4, 128, 64)
                                    olo = strided_half(kr, 0, 4, 128, 64)
                                    ohi = strided_half(kr, 64, 4, 128, 64)
                                    dv.inc(v.tensor_tensor(olo, alo, phi, OP.subtract))
                                    dv.inc(v.tensor_tensor(ohi, ahi, plo, OP.add))
                                    lastu["rope"][par] = dv.v
                                    dv.wait(v)
                                    dv.inc(v.tensor_scalar(ka[:, 0:512], kr[:], 0.0,
                                                           None, OP.min))
                                    lastu["emin"][par] = dv.v

                                ck_emin = dv.v

                                @blk.scalar
                                def _(s, par=par, ck_emin=ck_emin):
                                    dv.wait(s, ck_emin)
                                    ac.inc(s.activation(qaux[par][:, 512:1024],
                                                        qaux[par][:, 0:512], AF.Exp))
                                    lastu["exp"][par] = ac.v

                                ck_exp = ac.v

                                @blk.vector
                                def _(v, par=par, ck_exp=ck_exp):
                                    ac.wait(v, ck_exp)
                                    pe.wait(v, lastu["ft"][par])
                                    dv.inc(v.scalar_tensor_tensor(
                                        qbf[par][:], qful[par][:], 0.0,
                                        qaux[par][:, 512:1024], OP.max, OP.add))
                                    lastu["qelu"][par] = dv.v

                                ck_qelu = dv.v

                                @blk.tensor
                                def _(t, j=j, par=par, pqh=pqh, ck_qelu=ck_qelu,
                                      ck_qaff=ck_qaff):
                                    dv.wait(t, ck_qelu)
                                    ac.wait(t, ck_qaff)
                                    pm = pqh[:, j * 512:j * 512 + 256].bitcast(BF16)
                                    for h in range(4):
                                        pe.inc(t.transpose(
                                            pm[:, h * 128:(h + 1) * 128],
                                            qbf[par][:, h * 128:(h + 1) * 128],
                                            idb[:]))
                                    lastu["ft"][par] = pe.v

                                ck_ft = pe.v

                                @blk.scalar
                                def _(s, j=j, tc=tc, eq=eq, half=half, pqh=pqh,
                                      ck_ft=ck_ft):
                                    pe.wait(s, ck_ft)
                                    pm = pqh[:, j * 512:j * 512 + 256].bitcast(BF16)
                                    for h in range(4):
                                        hg = eq * 4 + h
                                        ac.inc(s.activation(
                                            qoT[:, hg * TOWN + tc * P:
                                                hg * TOWN + (tc + 1) * P],
                                            pm[:, h * 128:(h + 1) * 128], AF.Copy))
                                    pq_free[half][j] = ac.v
                        return emit

                    chains[half] = mk_chains()

                chains[0]()
                chains[1]()

            if "tap_qfT" in d_tap:
                @blk.vector
                def _(v):
                    for j in range(4):
                        ac.wait(v, pq_free[0][j])
                        ac.wait(v, pq_free[1][j])
                    dv.inc(v.tensor_copy(qaux[0][:, 0:512], qoT[:, 0:512]))

                @blk.sync
                def _(s):
                    dv.wait(s)
                    pass

    with contextlib.ExitStack() as ps:
        stsb = ps.enter_context(nc.sbuf_tensor("stsb", [P, 512], BF16))
        zsb = ps.enter_context(nc.sbuf_tensor("zsb", [P, 4], BF16))
        smb = [ps.enter_context(nc.sbuf_tensor(f"smb{i}", [P, 512], BF16)) for i in range(2)]
        denS = ps.enter_context(nc.sbuf_tensor("denS", [1, 512], F32))
        recS = [ps.enter_context(nc.sbuf_tensor(f"recS{i}", [1, 512], F32R)) for i in range(2)]
        rbS = ps.enter_context(nc.sbuf_tensor("rbS", [P, 512], F32))

        pstate3 = ps.enter_context(nc.psum_tensor("pstate3", [P, 512], F32))
        pz3 = ps.enter_context(nc.psum_tensor("pz3", [P, 4], F32))
        psS = [ps.enter_context(nc.psum_tensor(f"psS{i}", [P, 512], F32)) for i in range(2)]
        pnum = [ps.enter_context(nc.psum_tensor(f"pnum{i}", [P, 512], F32)) for i in range(2)]
        pden = ps.enter_context(nc.psum_tensor("pden", [1, 512], F32))
        prb = ps.enter_context(nc.psum_tensor("prb", [P, 512], F32))

        with nc.Block() as blk:
            ck_stev = [0] * NOC
            ck_pedone = [0] * NOC
            ck_supd = [0] * NOC
            ck_smev = [0, 0]
            ck_smuse = [0, 0]
            ck_div_last = [0, 0]
            ck_rec_last = 0
            ck_rb_last = 0
            ck_rb_par = [0, 0]
            ck_rbev_last = 0
            ck_div2_last = [0]
            pend5 = [None]
            for oc in range(NOC):
                @blk.vector
                def _(v, oc=oc):
                    if oc == 0:
                        dv.inc(v.tensor_copy(stsb[:], stp[:]))
                        dv.inc(v.tensor_copy(zsb[:], zpS[:]))
                    else:
                        pe.wait(v, ck_supd[oc - 1])
                        pe.wait(v, ck_pedone[oc - 1])
                        dv.inc(v.tensor_tensor(stsb[:], pstate3[:], stp[:], OP.add))
                        dv.inc(v.tensor_tensor(zsb[:], pz3[:], zpS[:], OP.add))
                    ck_stev[oc] = dv.v

                @blk.tensor
                def _(t, oc=oc):
                    dv.wait(t, ck_stev[oc])
                    kfb = kf_tm[:, oc * 512:(oc + 1) * 512]
                    vb = v_tm[:, oc * 512:(oc + 1) * 512]
                    for h in range(KVH):
                        pe.inc(t.matmul(pstate3[:, h * 128:(h + 1) * 128],
                                        kfb[:, h * 128:(h + 1) * 128],
                                        vb[:, h * 128:(h + 1) * 128],
                                        start=(oc == 0 and h == 0),
                                        stop=(h == KVH - 1),
                                        skip_group_check=(oc > 0)))
                        pe.inc(t.matmul(pz3[:, h:h + 1],
                                        kfb[:, h * 128:(h + 1) * 128],
                                        maskS[:, NPC + oc:NPC + oc + 1],
                                        start=(oc == 0 and h == 0),
                                        stop=(h == KVH - 1),
                                        skip_group_check=(oc > 0)))
                    ck_supd[oc] = pe.v

                for kv in range(KVH):
                    par = (oc * KVH + kv) % 2

                    @blk.tensor
                    def _(t, oc=oc, kv=kv, par=par):
                        dv.wait(t, ck_smev[par])
                        for h in range(4):
                            hg = kv * 4 + h
                            pe.inc(t.matmul(
                                psS[par][:, h * 128:(h + 1) * 128],
                                kfT[:, kv * TOWN + oc * P:kv * TOWN + (oc + 1) * P],
                                qoT[:, hg * TOWN + oc * P:hg * TOWN + (oc + 1) * P],
                                start=(h == 0), stop=(h == 3)))

                    ck_smm = pe.v

                    @blk.vector
                    def _(v, par=par, ck_smm=ck_smm):
                        pe.wait(v, ck_smm)
                        pe.wait(v, ck_smuse[par])
                        dv.inc(v.tensor_tensor(smb[par][:], psS[par][:], cmask4[:], OP.mult))
                        ck_smev[par] = dv.v

                    ck_sm = dv.v

                    @blk.tensor
                    def _(t, oc=oc, kv=kv, par=par, ck_sm=ck_sm):
                        dv.wait(t, ck_sm)
                        dv.wait(t, ck_stev[oc])
                        dv.wait(t, ck_div_last[par])
                        vb = v_tm[:, oc * 512 + kv * 128:oc * 512 + (kv + 1) * 128]
                        for h in range(4):
                            pe.inc(t.matmul(pnum[par][:, h * 128:(h + 1) * 128],
                                            vb, smb[par][:, h * 128:(h + 1) * 128],
                                            start=(h == 0), stop=False))
                        for h in range(4):
                            hg = kv * 4 + h
                            pe.inc(t.matmul(
                                pnum[par][:, h * 128:(h + 1) * 128],
                                stsb[:, kv * 128:(kv + 1) * 128],
                                qoT[:, hg * TOWN + oc * P:hg * TOWN + (oc + 1) * P],
                                start=False, stop=(h == 3)))

                    ck_num = pe.v

                    @blk.tensor
                    def _(t, oc=oc, kv=kv, par=par):
                        dv.wait(t, ck_rec_last)
                        for h in range(4):
                            pe.inc(t.matmul(pden[0:1, h * 128:(h + 1) * 128],
                                            onescb[:], smb[par][:, h * 128:(h + 1) * 128],
                                            start=(h == 0), stop=False))
                        for h in range(4):
                            hg = kv * 4 + h
                            pe.inc(t.matmul(
                                pden[0:1, h * 128:(h + 1) * 128],
                                zsb[:, kv:kv + 1],
                                qoT[:, hg * TOWN + oc * P:hg * TOWN + (oc + 1) * P],
                                start=False, stop=(h == 3)))

                    ck_denmm = pe.v
                    ck_smuse[par] = pe.v

                    sidx = oc * KVH + kv
                    rpar = sidx % 2

                    @blk.vector
                    def _(v, rpar=rpar, ck_denmm=ck_denmm):
                        pe.wait(v, ck_denmm)
                        pe.wait(v, ck_rb_par[rpar])
                        dv.inc(v.tensor_scalar(denS[:], pden[0:1, :], EPS_ATT, None, OP.add))
                        dv.wait(v)
                        with nc.allow_low_precision(reason="f32r view of fp32 recip"):
                            dv.inc(v.reciprocal(recS[rpar][:], denS[:]))

                    ck_rec_last = dv.v

                    def mk_tail5(oc=oc, kv=kv, par=par, rpar=rpar,
                                 ck_rec=ck_rec_last, ck_num=ck_num):
                        def emit():
                            global_ck = {}

                            @blk.tensor
                            def _(t, ck_rec=ck_rec, ck_rbev=ck_rbev_last):
                                dv.wait(t, ck_rec)
                                ac.wait(t, ck_rbev)
                                pe.inc(t.matmul(prb[:], onesr[:], recS[rpar][:],
                                                start=True, stop=True))

                            ck_rb = pe.v

                            @blk.scalar
                            def _(s, ck_rb=ck_rb):
                                pe.wait(s, ck_rb)
                                dv.wait(s, ck_div2_last[0])
                                ac.inc(s.activation(rbS[:], prb[:], AF.Copy))

                            ck_rbev = ac.v

                            @blk.vector
                            def _(v, oc=oc, kv=kv, par=par, ck_rbev=ck_rbev,
                                  ck_num=ck_num):
                                ac.wait(v, ck_rbev)
                                pe.wait(v, ck_num)
                                otap = qoT[:]
                                dst = bass.AP(otap.tensor,
                                              otap.offset + (kv * 4) * TOWN + oc * P,
                                              [[H * TOWN, P], [TOWN, 4], [1, P]])
                                dv.inc(v.tensor_tensor(dst, pnum[par][:], rbS[:],
                                                       OP.mult))
                                ck_div_last[par] = dv.v
                                ck_div2_last[0] = dv.v
                            return ck_rb, ck_rbev
                        return emit

                    if pend5[0] is not None:
                        ck_rb_par[1 - rpar], ck_rbev_last = pend5[0]()
                        ck_rb_last = ck_rb_par[1 - rpar]
                    pend5[0] = mk_tail5()

                ck_pedone[oc] = pe.v

            if pend5[0] is not None:
                rp_last = (NOC * KVH - 1) % 2
                ck_rb_par[rp_last], ck_rbev_last = pend5[0]()
                pend5[0] = None

    with contextlib.ExitStack() as ps:
        wob = [ps.enter_context(nc.sbuf_tensor(f"wob{i}", [P, FI * P], BF16)) for i in range(2)]
        aor = [ps.enter_context(nc.sbuf_tensor(f"aor{i}", [P, TOWN], F32R)) for i in range(2)]
        sqr = ps.enter_context(nc.sbuf_tensor("sqr", [P, TOWN], F32R))
        rws = ps.enter_context(nc.sbuf_tensor("rws", [1, 3 * TOWN], F32R))
        mrb = ps.enter_context(nc.sbuf_tensor("mrb", [P, 2 * TOWN], F32))
        x2w = [ps.enter_context(nc.sbuf_tensor(f"x2w{i}", [P, TOWN], F32R)) for i in range(2)]
        xre = [ps.enter_context(nc.sbuf_tensor(f"xre{i}", [P, TOWN], F32R)) for i in range(2)]

        pwo = [ps.enter_context(nc.psum_tensor(f"pwo{i}", [P, 1024], F32)) for i in range(2)]
        pls = ps.enter_context(nc.psum_tensor("pls", [1, 1024], F32))
        plq = ps.enter_context(nc.psum_tensor("plq", [1, 1024], F32))

        with nc.Block() as blk:
            ck_wld = [0] * FI
            ck_womm = [0] * FI
            ck_aoev = [0] * FI
            ck_aopark = [0] * FI
            ck_sqt = [0] * FI
            ck_sqmm = [0] * FI

            pend4 = [None]
            for fc in range(FI):
                par = fc % 2

                @blk.gpsimd
                def _(g, fc=fc, par=par):
                    if fc >= 2:
                        pe.wait(g, ck_womm[fc - 2])
                    ck_wld[fc] = gqw[par].dma(g.dma_start(
                        wob[par][:].rearrange("p (f n) -> p f n", f=FI),
                        d_wo.ap()[fc]))

                @blk.tensor
                def _(t, fc=fc, par=par):
                    gqw[par].wait(t, ck_wld[fc])
                    if fc >= 2:
                        ac.wait(t, ck_aoev[fc - 2])
                    for fo in range(FI):
                        wtile = wob[par][:, fo * P:(fo + 1) * P]
                        for th in range(2):
                            pe.inc(t.matmul(
                                pwo[par][:, th * 512:(th + 1) * 512],
                                wtile,
                                qoT[:, fo * TOWN + th * 512:fo * TOWN + (th + 1) * 512],
                                start=(fo == 0), stop=(fo == FI - 1)))
                    ck_womm[fc] = pe.v

                @blk.scalar
                def _(s, fc=fc, par=par):
                    pe.wait(s, ck_womm[fc])
                    if fc >= 2:
                        ldp[par].wait(s, ck_aopark[fc - 2])
                        pe.wait(s, ck_sqmm[fc - 2] if fc >= 2 else 0)
                    ck_aoev[fc] = ac.inc(s.activation(aor[par][:], pwo[par][:], AF.Copy))

                @blk.sync
                def _(s, fc=fc, par=par):
                    ac.wait(s, ck_aoev[fc])
                    ck_aopark[fc] = ldp[par].dma(s.dma_start(d_ao.ap()[fc], aor[par][:]))

                def mk_stats(fc=fc, par=par):
                    def emit():
                        @blk.tensor
                        def _(t, fc=fc, par=par):
                            ac.wait(t, ck_aoev[fc])
                            for th in range(2):
                                pe.inc(t.matmul(pls[0:1, th * 512:(th + 1) * 512],
                                                onesc[:],
                                                aor[par][:, th * 512:(th + 1) * 512],
                                                start=(fc == 0), stop=(fc == FI - 1)))
                            dv.wait(t, ck_sqt[fc])
                            for th in range(2):
                                pe.inc(t.matmul(plq[0:1, th * 512:(th + 1) * 512],
                                                onesc[:],
                                                sqr[:, th * 512:(th + 1) * 512],
                                                start=(fc == 0), stop=(fc == FI - 1)))
                            ck_sqmm[fc] = pe.v
                    return emit

                if pend4[0] is not None:
                    pend4[0]()
                pend4[0] = mk_stats()

                @blk.vector
                def _(v, fc=fc, par=par):
                    ac.wait(v, ck_aoev[fc])
                    if fc >= 1:
                        pe.wait(v, ck_sqmm[fc - 1])
                    ck_sqt[fc] = dv.inc(v.tensor_tensor(sqr[:], aor[par][:],
                                                        aor[par][:], OP.mult))

            pend4[0]()

            @blk.vector
            def _(v):
                pe.wait(v, ck_sqmm[FI - 1])
                dv.inc(v.tensor_scalar(rws[0:1, 0:1024], pls[0:1, :], INV_DM, None, OP.mult))
                dv.wait(v)
                dv.inc(v.tensor_tensor(rws[0:1, 1024:2048], rws[0:1, 0:1024],
                                       rws[0:1, 0:1024], OP.mult))
                dv.wait(v)
                dv.inc(v.scalar_tensor_tensor(rws[0:1, 2048:3072], plq[0:1, :], INV_DM,
                                              rws[0:1, 1024:2048], OP.mult, OP.subtract))
                dv.wait(v)
                dv.inc(v.tensor_scalar(rws[0:1, 2048:3072], rws[0:1, 2048:3072], EPS_LN,
                                       None, OP.add))

            ck_r1 = dv.v

            @blk.scalar
            def _(s):
                dv.wait(s, ck_r1)
                ac.inc(s.activation(rws[0:1, 1024:2048], rws[0:1, 2048:3072], AF.Ln))
                ac.wait(s)
                ac.inc(s.activation(rws[0:1, 2048:3072], rws[0:1, 1024:2048], AF.Exp, scale=-0.5))

            ck_r2 = ac.v

            @blk.tensor
            def _(t):
                ac.wait(t, ck_r2)
                ac.wait(t, ck_aoev[FI - 1])
                for th in range(2):
                    pe.inc(t.matmul(pwo[0][:, th * 512:(th + 1) * 512], onesr[:],
                                    rws[0:1, th * 512:(th + 1) * 512],
                                    start=True, stop=True))
                    pe.inc(t.matmul(pwo[1][:, th * 512:(th + 1) * 512], onesr[:],
                                    rws[0:1, 2048 + th * 512:2048 + (th + 1) * 512],
                                    start=True, stop=True))

            ck_bc = pe.v

            @blk.scalar
            def _(s):
                pe.wait(s, ck_bc)
                ac.inc(s.activation(mrb[:, 0:1024], pwo[0][:], AF.Copy))
                ac.inc(s.activation(mrb[:, 1024:2048], pwo[1][:], AF.Copy))

            ck_mrb = ac.v

            ck_aold = [0] * FI
            ck_xld2 = [0] * FI
            ck_t1 = [0] * FI
            ck_x2w = [0] * FI
            ck_x2pk = [0] * FI
            ck_x2sq = [0] * FI
            ck_x2smm = [0] * FI
            for fc in range(FI):
                par = fc % 2

                @blk.sync
                def _(s, fc=fc, par=par):
                    if fc == 0:
                        ldp[0].wait(s)
                        ldp[1].wait(s)
                    if fc < 2:
                        pe.wait(s, ck_sqmm[FI - 1])
                    if fc >= 2:
                        dv.wait(s, ck_t1[fc - 2])
                        dv.wait(s, ck_x2w[fc - 2])
                        pe.wait(s, ck_x2smm[fc - 2])
                    ck_aold[fc] = ldr[par].dma(s.dma_start(aor[par][:], d_ao.ap()[fc]))
                    ck_xld2[fc] = ldr[par].dma(s.dma_start(
                        xre[par][:],
                        bass.AP(d_xt, NPC * P * FI * P + fc * P,
                                [[FI * P, P], [P * FI * P, NOC], [1, P]])))

                @blk.vector
                def _(v, fc=fc, par=par):
                    ldr[par].wait(v, ck_xld2[fc])
                    ac.wait(v, ck_mrb)
                    if fc >= 2:
                        ldp[par].wait(v, ck_x2pk[fc - 2])
                    dv.inc(v.tensor_tensor(x2w[par][:], aor[par][:], mrb[:, 0:1024], OP.subtract))
                    dv.wait(v)
                    dv.inc(v.tensor_tensor(aor[par][:], x2w[par][:], mrb[:, 1024:2048], OP.mult))
                    ck_t1[fc] = dv.v
                    dv.wait(v)
                    ldr[par].wait(v, ck_xld2[fc])
                    dv.inc(v.tensor_tensor(x2w[par][:], aor[par][:], xre[par][:], OP.add))
                    ck_x2w[fc] = dv.v
                    dv.wait(v)
                    if fc == 0:
                        pe.wait(v, ck_sqmm[FI - 1])
                    else:
                        pe.wait(v, ck_x2smm[fc - 1])
                    dv.inc(v.tensor_tensor(sqr[:], x2w[par][:], x2w[par][:], OP.mult))
                    ck_x2sq[fc] = dv.v

                @blk.tensor
                def _(t, fc=fc, par=par):
                    dv.wait(t, ck_x2w[fc])
                    if fc == 0:
                        dv.wait(t, ck_r1)
                    for th in range(2):
                        pe.inc(t.matmul(pls[0:1, th * 512:(th + 1) * 512], onesc[:],
                                        x2w[par][:, th * 512:(th + 1) * 512],
                                        start=(fc == 0), stop=(fc == FI - 1)))
                    dv.wait(t, ck_x2sq[fc])
                    for th in range(2):
                        pe.inc(t.matmul(plq[0:1, th * 512:(th + 1) * 512], onesc[:],
                                        sqr[:, th * 512:(th + 1) * 512],
                                        start=(fc == 0), stop=(fc == FI - 1)))
                    ck_x2smm[fc] = pe.v

                @blk.sync
                def _(s, fc=fc, par=par):
                    dv.wait(s, ck_x2w[fc])
                    ck_x2pk[fc] = ldp[par].dma(s.dma_start(d_x2.ap()[fc], x2w[par][:]))

            @blk.scalar
            def _(s):
                pe.wait(s, ck_x2smm[FI - 1])
                ac.inc(s.activation(rws[0:1, 0:1024], pls[0:1, :], AF.Copy))
                ck_stev = ac.inc(s.activation(rws[0:1, 1024:2048], plq[0:1, :], AF.Copy))
                ac.wait(s, ck_stev)
                vq.dma(s.dma_start(d_st.ap(), rws[0:1, 0:2048]))

            ck_stpark = vq.v

            @blk.sync
            def _(s):
                ldp[0].wait(s)
                ldp[1].wait(s)
                vq.wait(s, ck_stpark)

    L_A.close()
    L_C.close()
    L_B.close()

    P56 = contextlib.ExitStack()
    acc = P56.enter_context(nc.sbuf_tensor("acc", [P, FI * TOWN], F32R))
    P5ab = contextlib.ExitStack()
    x3h = P5ab.enter_context(nc.sbuf_tensor("x3h", [P, FI * TOWN], BF16))
    m3b = P5ab.enter_context(nc.sbuf_tensor("m3b", [P, TOWN], F32))
    r3b = P5ab.enter_context(nc.sbuf_tensor("r3b", [P, TOWN], F32))

    with contextlib.ExitStack() as ps:
        x2r = [ps.enter_context(nc.sbuf_tensor(f"x2r{i}", [P, TOWN], F32R)) for i in range(2)]
        rw5 = ps.enter_context(nc.sbuf_tensor("rw5", [1, 3 * TOWN], F32R))
        scr5 = ps.enter_context(nc.sbuf_tensor("scr5", [P, TOWN], F32))
        pb5 = [ps.enter_context(nc.psum_tensor(f"pb5{i}", [P, 512], F32)) for i in range(4)]

        with nc.Block() as blk:
            ck_x2l = [0] * FI

            @blk.sync
            def _(s):
                ck_stld = vq.dma(s.dma_start(rw5[0:1, 0:2048], d_st.ap()))
                vq.wait(s, ck_stld)

            ck_stld = vq.v

            @blk.vector
            def _(v):
                vq.wait(v, ck_stld)
                dv.inc(v.tensor_scalar(rw5[0:1, 0:1024], rw5[0:1, 0:1024], INV_DM,
                                       None, OP.mult))
                dv.wait(v)
                dv.inc(v.tensor_tensor(rw5[0:1, 2048:3072], rw5[0:1, 0:1024],
                                       rw5[0:1, 0:1024], OP.mult))
                dv.wait(v)
                dv.inc(v.scalar_tensor_tensor(rw5[0:1, 1024:2048], rw5[0:1, 1024:2048],
                                              INV_DM, rw5[0:1, 2048:3072],
                                              OP.mult, OP.subtract))
                dv.wait(v)
                dv.inc(v.tensor_scalar(rw5[0:1, 2048:3072], rw5[0:1, 1024:2048], EPS_LN,
                                       None, OP.add))

            ck_r1 = dv.v

            @blk.scalar
            def _(s):
                dv.wait(s, ck_r1)
                ac.inc(s.activation(rw5[0:1, 1024:2048], rw5[0:1, 2048:3072], AF.Ln))
                ac.wait(s)
                ac.inc(s.activation(rw5[0:1, 2048:3072], rw5[0:1, 1024:2048], AF.Exp, scale=-0.5))

            ck_r2 = ac.v

            @blk.tensor
            def _(t):
                ac.wait(t, ck_r2)
                for th in range(2):
                    pe.inc(t.matmul(pb5[th][:], onesr[:],
                                    rw5[0:1, th * 512:(th + 1) * 512], start=True, stop=True))
                    pe.inc(t.matmul(pb5[2 + th][:], onesr[:],
                                    rw5[0:1, 2048 + th * 512:2048 + (th + 1) * 512],
                                    start=True, stop=True))

            ck_bc = pe.v

            @blk.scalar
            def _(s):
                pe.wait(s, ck_bc)
                for th in range(2):
                    ac.inc(s.activation(m3b[:, th * 512:(th + 1) * 512], pb5[th][:], AF.Copy))
                    ac.inc(s.activation(r3b[:, th * 512:(th + 1) * 512], pb5[2 + th][:], AF.Copy))

            ck_m3 = ac.v

            ck_x3 = [0] * FI
            for fc in range(FI):
                par = fc % 2

                @blk.sync
                def _(s, fc=fc, par=par):
                    if fc >= 2:
                        dv.wait(s, ck_x3[fc - 2])
                    ck_x2l[fc] = ldr[par].dma(s.dma_start(x2r[par][:], d_x2.ap()[fc]))

                @blk.vector
                def _(v, fc=fc, par=par):
                    ldr[par].wait(v, ck_x2l[fc])
                    ac.wait(v, ck_m3)
                    dv.wait(v)
                    dv.inc(v.tensor_tensor(scr5[:], x2r[par][:], m3b[:], OP.subtract))
                    dv.wait(v)
                    dv.inc(v.tensor_tensor(x3h[:, fc * TOWN:(fc + 1) * TOWN], scr5[:],
                                           r3b[:], OP.mult))
                    ck_x3[fc] = dv.v

            ck_x3_all = dv.v

    P5ab_keep = (x3h,)

    with contextlib.ExitStack() as ps:
        wgb = [ps.enter_context(nc.sbuf_tensor(f"wgb{i}", [P, FI * P], BF16)) for i in range(2)]
        wub = [ps.enter_context(nc.sbuf_tensor(f"wub{i}", [P, FI * P], BF16)) for i in range(2)]
        wdb = [ps.enter_context(nc.sbuf_tensor(f"wdb{i}", [P, 1024], BF16)) for i in range(2)]
        sgb = ps.enter_context(nc.sbuf_tensor("sgb", [P, 1024], F32))
        hb = ps.enter_context(nc.sbuf_tensor("hb", [P, 4 * 1024], F32))
        hq = ps.enter_context(nc.sbuf_tensor("hq", [P, 4 * 1024], F32))
        tr = [ps.enter_context(nc.sbuf_tensor(f"tr{i}", [P, 1024], F32)) for i in range(3)]
        hsb = ps.enter_context(nc.sbuf_tensor("hsb", [P, CBG * 4096], BF16))
        pgu = [ps.enter_context(nc.psum_tensor(f"pgu{i}", [P, 2048], F32)) for i in range(2)]
        pd = pgu[0]

        with nc.Block() as blk:
            ck_wg = {}
            ck_wd = {}
            ck_gu = {}
            ck_sil = {}
            ck_h = {}
            ck_hs = [0] * NCB
            ck_dmm = {}
            ck_fl = {}
            ck_hq = [0] * NCB
            last_flush = 0
            last_dmm_cbg = 0

            for cbg in range(NCB // CBG):
                for cbb in range(CBG):
                    cb = cbg * CBG + cbb
                    for j in range(4):
                        pgi = (cb * 4 + j) % 2

                        @blk.gpsimd
                        def _(g, cb=cb, j=j, pgi=pgi):
                            prev = ck_gu.get((cb * 4 + j - 2), None)
                            if prev is not None:
                                pe.wait(g, prev)
                            gqw[pgi].dma(g.dma_start(
                                wgb[pgi][:].rearrange("p (f n) -> p f n", f=FI),
                                d_wg.ap()[cb][j]))
                            ck_wg[cb * 4 + j] = gqw[pgi].dma(g.dma_start(
                                wub[pgi][:].rearrange("p (f n) -> p f n", f=FI),
                                d_wu.ap()[cb][j]))

                        @blk.tensor
                        def _(t, cb=cb, j=j, pgi=pgi, cbg=cbg):
                            gqw[pgi].wait(t, ck_wg[cb * 4 + j])
                            dv.wait(t, ck_x3_all)
                            prevh = ck_h.get(cb * 4 + j - 2)
                            if prevh is not None:
                                dv.wait(t, prevh)
                            if cbg > 0 and cb == cbg * CBG and j < 2:
                                dv.wait(t, last_flush)
                            for fi in range(FI):
                                wgt = wgb[pgi][:, fi * P:(fi + 1) * P]
                                wut = wub[pgi][:, fi * P:(fi + 1) * P]
                                for th in range(2):
                                    pe.inc(t.matmul(
                                        pgu[pgi][:, th * 512:(th + 1) * 512], wgt,
                                        x3h[:, fi * TOWN + th * 512:fi * TOWN + (th + 1) * 512],
                                        start=(fi == 0), stop=(fi == FI - 1)))
                                for th in range(2):
                                    pe.inc(t.matmul(
                                        pgu[pgi][:, 1024 + th * 512:1024 + (th + 1) * 512], wut,
                                        x3h[:, fi * TOWN + th * 512:fi * TOWN + (th + 1) * 512],
                                        start=(fi == 0), stop=(fi == FI - 1)))
                            ck_gu[cb * 4 + j] = pe.v

                        @blk.scalar
                        def _(s, cb=cb, j=j, pgi=pgi):
                            pe.wait(s, ck_gu[cb * 4 + j])
                            prevh = ck_h.get(cb * 4 + j - 1)
                            if prevh is not None:
                                dv.wait(s, prevh)
                            ck_sil[cb * 4 + j] = ac.inc(
                                s.activation(sgb[:], pgu[pgi][:, 0:1024], AF.Silu))

                        @blk.vector
                        def _(v, cb=cb, j=j, pgi=pgi):
                            ac.wait(v, ck_sil[cb * 4 + j])
                            if j == 0:
                                dv.wait(v, ck_hq[cb - 1] if cb >= 1 else 0)
                            dv.inc(v.tensor_tensor(hb[:, j * 1024:(j + 1) * 1024], sgb[:],
                                                   pgu[pgi][:, 1024:2048], OP.mult))
                            ck_h[cb * 4 + j] = dv.v

                    @blk.vector
                    def _(v, cb=cb, cbb=cbb):
                        dv.wait(v, ck_h[cb * 4 + 3])
                        ck_hq[cb] = dv.inc(v.tensor_tensor(hq[:], hb[:], hb[:], OP.mult))

                    @blk.vector
                    def _(v, cb=cb, cbb=cbb, cbg=cbg):
                        dv.wait(v, ck_hq[cb])
                        if cbg > 0 and cbb == 0:
                            pe.wait(v, last_dmm_cbg)
                        q = [hq[:, j * 1024:(j + 1) * 1024] for j in range(4)]
                        dv.inc(v.tensor_tensor(tr[0][:], q[0], q[1], OP.min))
                        dv.inc(v.tensor_tensor(tr[1][:], q[0], q[1], OP.max))
                        dv.inc(v.tensor_tensor(tr[2][:], q[2], q[3], OP.min))
                        dv.wait(v)
                        dv.inc(v.tensor_tensor(tr[0][:], tr[0][:], tr[2][:], OP.max))
                        dv.wait(v)
                        dv.inc(v.tensor_tensor(tr[2][:], q[2], q[3], OP.max))
                        dv.wait(v)
                        dv.inc(v.tensor_tensor(tr[1][:], tr[1][:], tr[2][:], OP.min))
                        dv.wait(v)
                        dv.inc(v.tensor_tensor(tr[0][:], tr[0][:], tr[1][:], OP.max))
                        dv.wait(v)
                        for j in range(4):
                            dv.inc(v.tensor_tensor(tr[1][:], q[j], tr[0][:], OP.is_ge))
                            dv.wait(v)
                            dv.inc(v.tensor_tensor(
                                hsb[:, cbb * 4096 + j * 1024:(cbb * 4096 + (j + 1) * 1024)],
                                hb[:, j * 1024:(j + 1) * 1024], tr[1][:], OP.mult))
                            dv.wait(v)
                        ck_hs[cb] = dv.v

                for fcp in range(8):
                    pdl = pgu[fcp % 2]
                    for cbb in range(CBG):
                        idx = fcp * CBG + cbb

                        @blk.gpsimd
                        def _(g, cbg=cbg, fcp=fcp, cbb=cbb, idx=idx):
                            prev = ck_dmm.get((cbg, idx - 2))
                            if prev is not None:
                                pe.wait(g, prev)
                            elif cbg > 0:
                                pe.wait(g, last_dmm_cbg)
                            ck_wd[(cbg, idx)] = gqw[2 + idx % 2].dma(g.dma_start(
                                wdb[idx % 2][:].rearrange("p (f n) -> p f n", f=4),
                                d_wd.ap()[fcp][cbg * CBG + cbb]))

                        @blk.tensor
                        def _(t, cbg=cbg, fcp=fcp, cbb=cbb, idx=idx, pdl=pdl):
                            if fcp == 0:
                                dv.wait(t, ck_hs[cbg * CBG + cbb])
                                if cbb == 0:
                                    dv.wait(t, ck_h[(cbg * CBG + 3) * 4 + 3])
                            if cbb == 0:
                                prevfl = ck_fl.get((cbg, fcp - 2))
                                if prevfl is not None:
                                    dv.wait(t, prevfl)
                                elif cbg > 0:
                                    dv.wait(t, ck_fl[(cbg - 1, 7 - fcp % 2)])
                            gqw[2 + idx % 2].wait(t, ck_wd[(cbg, idx)])
                            for j in range(4):
                                for f01 in range(2):
                                    wdt = wdb[idx % 2][:, j * 256 + f01 * 128:
                                                       j * 256 + (f01 + 1) * 128]
                                    for th in range(2):
                                        pe.inc(t.matmul(
                                            pdl[:, f01 * 1024 + th * 512:
                                                f01 * 1024 + (th + 1) * 512],
                                            wdt,
                                            hsb[:, cbb * 4096 + j * 1024 + th * 512:
                                                cbb * 4096 + j * 1024 + (th + 1) * 512],
                                            start=(cbb == 0 and j == 0),
                                            stop=(cbb == CBG - 1 and j == 3)))
                            ck_dmm[(cbg, idx)] = pe.v
                            if cbb == CBG - 1:
                                ck_dmm[(cbg, "fcp%d" % fcp)] = pe.v

                    @blk.vector
                    def _(v, cbg=cbg, fcp=fcp, pdl=pdl):
                        pe.wait(v, ck_dmm[(cbg, "fcp%d" % fcp)])
                        for f01 in range(2):
                            for th in range(2):
                                fcn = fcp * 2 + f01
                                dst = acc[:, fcn * TOWN + th * 512:fcn * TOWN + (th + 1) * 512]
                                srcp = pdl[:, f01 * 1024 + th * 512:f01 * 1024 + (th + 1) * 512]
                                if cbg == 0:
                                    dv.inc(v.tensor_copy(dst, srcp))
                                else:
                                    dv.inc(v.tensor_tensor(dst, dst, srcp, OP.add))
                        ck_fl[(cbg, fcp)] = dv.v

                last_flush = dv.v
                last_dmm_cbg = pe.v

    P5ab.close()

    with contextlib.ExitStack() as ps:
        x2r = [ps.enter_context(nc.sbuf_tensor(f"x6r{i}", [P, TOWN], F32R)) for i in range(2)]
        sq6 = ps.enter_context(nc.sbuf_tensor("sq6", [P, TOWN], F32R))
        rw6 = ps.enter_context(nc.sbuf_tensor("rw6", [1, 3 * TOWN], F32R))
        m4b = ps.enter_context(nc.sbuf_tensor("m4b", [P, TOWN], F32))
        r4b = ps.enter_context(nc.sbuf_tensor("r4b", [P, TOWN], F32))
        scr6 = ps.enter_context(nc.sbuf_tensor("scr6", [P, TOWN], F32))
        ob = [ps.enter_context(nc.sbuf_tensor(f"ob{i}", [P, TOWN], F32)) for i in range(2)]
        pl6 = ps.enter_context(nc.psum_tensor("pl6", [1, 1024], F32))
        pq6 = ps.enter_context(nc.psum_tensor("pq6", [1, 1024], F32))
        pb6 = [ps.enter_context(nc.psum_tensor(f"pb6{i}", [P, 512], F32)) for i in range(4)]

        with nc.Block() as blk:
            ck_sqt = [0] * FI
            ck_smm = [0] * FI

            pend6 = [None]
            for fc in range(FI):
                @blk.tensor
                def _(t, fc=fc):
                    for th in range(2):
                        pe.inc(t.matmul(pl6[0:1, th * 512:(th + 1) * 512], onesc[:],
                                        acc[:, fc * TOWN + th * 512:fc * TOWN + (th + 1) * 512],
                                        start=(fc == 0), stop=(fc == FI - 1)))

                def mk6(fc=fc):
                    def emit():
                        @blk.tensor
                        def _(t, fc=fc):
                            dv.wait(t, ck_sqt[fc])
                            for th in range(2):
                                pe.inc(t.matmul(pq6[0:1, th * 512:(th + 1) * 512],
                                                onesc[:],
                                                sq6[:, th * 512:(th + 1) * 512],
                                                start=(fc == 0), stop=(fc == FI - 1)))
                            ck_smm[fc] = pe.v
                    return emit

                if pend6[0] is not None:
                    pend6[0]()
                pend6[0] = mk6()

                @blk.vector
                def _(v, fc=fc):
                    if fc >= 1:
                        pe.wait(v, ck_smm[fc - 1])
                    a_ = acc[:, fc * TOWN:(fc + 1) * TOWN]
                    ck_sqt[fc] = dv.inc(v.tensor_tensor(sq6[:], a_, a_, OP.mult))

            pend6[0]()

            @blk.vector
            def _(v):
                pe.wait(v, ck_smm[FI - 1])
                dv.inc(v.tensor_scalar(rw6[0:1, 0:1024], pl6[0:1, :], INV_DM, None, OP.mult))
                dv.wait(v)
                dv.inc(v.tensor_tensor(rw6[0:1, 1024:2048], rw6[0:1, 0:1024],
                                       rw6[0:1, 0:1024], OP.mult))
                dv.wait(v)
                dv.inc(v.scalar_tensor_tensor(rw6[0:1, 2048:3072], pq6[0:1, :], INV_DM,
                                              rw6[0:1, 1024:2048], OP.mult, OP.subtract))
                dv.wait(v)
                dv.inc(v.tensor_scalar(rw6[0:1, 2048:3072], rw6[0:1, 2048:3072], EPS_LN,
                                       None, OP.add))

            ck_r1 = dv.v

            @blk.scalar
            def _(s):
                dv.wait(s, ck_r1)
                ac.inc(s.activation(rw6[0:1, 1024:2048], rw6[0:1, 2048:3072], AF.Ln))
                ac.wait(s)
                ac.inc(s.activation(rw6[0:1, 2048:3072], rw6[0:1, 1024:2048], AF.Exp, scale=-0.5))

            ck_r2 = ac.v

            @blk.tensor
            def _(t):
                ac.wait(t, ck_r2)
                for th in range(2):
                    pe.inc(t.matmul(pb6[th][:], onesr[:],
                                    rw6[0:1, th * 512:(th + 1) * 512], start=True, stop=True))
                    pe.inc(t.matmul(pb6[2 + th][:], onesr[:],
                                    rw6[0:1, 2048 + th * 512:2048 + (th + 1) * 512],
                                    start=True, stop=True))

            ck_bc = pe.v

            @blk.scalar
            def _(s):
                pe.wait(s, ck_bc)
                for th in range(2):
                    ac.inc(s.activation(m4b[:, th * 512:(th + 1) * 512], pb6[th][:], AF.Copy))
                    ac.inc(s.activation(r4b[:, th * 512:(th + 1) * 512], pb6[2 + th][:], AF.Copy))

            ck_m4 = ac.v

            ck_x2l = [0] * FI
            ck_ow = [0] * FI
            ck_ost = [0] * FI
            for fc in range(FI):
                par = fc % 2

                @blk.sync
                def _(s, fc=fc, par=par):
                    if fc >= 2:
                        dv.wait(s, ck_ow[fc - 2])
                        ldp[par].wait(s, ck_ost[fc - 2])
                    ck_x2l[fc] = ldr[par].dma(s.dma_start(x2r[par][:], d_x2.ap()[fc]))

                @blk.vector
                def _(v, fc=fc, par=par):
                    ac.wait(v, ck_m4)
                    dv.wait(v)
                    dv.inc(v.tensor_tensor(scr6[:], acc[:, fc * TOWN:(fc + 1) * TOWN],
                                           m4b[:], OP.subtract))
                    dv.wait(v)
                    dv.inc(v.tensor_tensor(scr6[:], scr6[:], r4b[:], OP.mult))
                    dv.wait(v)
                    ldr[par].wait(v, ck_x2l[fc])
                    dv.inc(v.tensor_tensor(ob[par][:], scr6[:], x2r[par][:], OP.add))
                    ck_ow[fc] = dv.v

                @blk.sync
                def _(s, fc=fc, par=par):
                    dv.wait(s, ck_ow[fc])
                    ck_ost[fc] = ldp[par].dma(s.dma_start(d_out.ap()[fc], ob[par][:]))

            @blk.sync
            def _(s):
                ldp[0].wait(s)
                ldp[1].wait(s)

    P56.close()
    es.close()
    _regroup_blocks(nc)
    return nc


def _regroup_blocks(nc):
    ORDER = {"EngineType.SP": 0, "EngineType.Pool": 1, "EngineType.PE": 2,
             "EngineType.DVE": 3, "EngineType.Activation": 4}
    f = nc.m.functions[0]
    out, seg = [], []

    def flush():
        seg.sort(key=lambda pair: pair[0])
        out.extend(b for _, b in seg)
        seg.clear()

    for b in f.blocks:
        engs = {str(i.engine) for i in b.instructions}
        if len(engs) == 1:
            seg.append((ORDER.get(next(iter(engs)), 99), b))
        else:
            flush()
            out.append(b)
    flush()
    f.blocks = out


def _prep_shared(Wq, Wk, Wv, Wo, Wg, Wu, Wd):
    sh = {}
    sh["wq"] = np.ascontiguousarray(
        Wq.T.reshape(FI, P, 4, 512).transpose(2, 0, 1, 3))
    sh["wk"] = np.ascontiguousarray(Wk.T.reshape(FI, P, 512).transpose(1, 0, 2))
    sh["wv"] = np.ascontiguousarray(Wv.T.reshape(FI, P, 512).transpose(1, 0, 2))
    sh["wo"] = np.ascontiguousarray(
        Wo.T.reshape(FI, P, FI, P).transpose(2, 1, 0, 3))
    cb = np.arange(NCB)[:, None, None]
    jj = np.arange(4)[None, :, None]
    pp = np.arange(P)[None, None, :]
    idx = (4 * (P * cb + pp) + jj).reshape(-1)
    wgP = Wg[:, idx]
    wuP = Wu[:, idx]
    wdP = Wd[idx, :]
    sh["wg"] = np.ascontiguousarray(
        wgP.reshape(FI, P, NCB, 4, P).transpose(2, 3, 1, 0, 4))
    sh["wu"] = np.ascontiguousarray(
        wuP.reshape(FI, P, NCB, 4, P).transpose(2, 3, 1, 0, 4))
    sh["wd"] = np.ascontiguousarray(
        wdP.reshape(NCB, 4, P, 8, 256).transpose(3, 0, 2, 1, 4))
    return sh


def _prep_consts():
    import ml_dtypes
    c = {}
    iv64 = np.exp(np.arange(64, dtype=np.float64) * (-math.log(10000.0) / 64.0))
    hi = iv64.astype(np.float32)
    lo = (iv64 - hi.astype(np.float64)).astype(np.float32)
    c["ivh"] = np.ascontiguousarray(np.broadcast_to(hi[None, :], (P, 64)))
    c["ivl"] = np.ascontiguousarray(np.broadcast_to(lo[None, :], (P, 64)))
    c["identf"] = np.eye(P, dtype=np.float32)
    c["identb"] = np.eye(P, dtype=np.float32).astype(ml_dtypes.bfloat16)
    cm = (np.arange(P)[:, None] <= np.arange(P)[None, :]).astype(np.float32)
    c["cmask4"] = np.ascontiguousarray(np.tile(cm, (1, 4)))
    c["onesr"] = np.ones((1, P), dtype=np.float32)
    c["onescol"] = np.ones((P, 1), dtype=np.float32)
    c["onescolb"] = np.ones((P, 1), dtype=np.float32).astype(ml_dtypes.bfloat16)
    return c


def _prep_core(x, core, consts):
    import ml_dtypes
    b, q = core // 4, core % 4
    start = q * TOWN
    xt = np.zeros((NCH, P, FI, P), dtype=np.float32)
    pos = np.zeros((P, NCH), dtype=np.float32)
    mask = np.zeros((P, NCH), dtype=np.float32)
    for c in range(NCH):
        t0 = start - TPREF + c * P
        pos[:, c] = np.maximum(t0 + np.arange(P), 0)
        if t0 + P <= 0:
            continue
        lo = max(t0, 0)
        hit = t0 + P
        seg = x[b, lo:hit, :]
        n = seg.shape[0]
        tile = np.zeros((DM, P), dtype=np.float32)
        tile[:, P - n:] = seg.T
        xt[c] = tile.reshape(FI, P, P).transpose(1, 0, 2)
        mask[:, c][P - n:] = 1.0
    m = dict(consts)
    m["xt"] = xt
    m["pos"] = pos
    m["posrep"] = np.ascontiguousarray(np.repeat(pos, 64, axis=1))
    m["maskc"] = mask.astype(ml_dtypes.bfloat16)
    return m


def kernel(L0_attn_x, L0_attn_Wq, L0_attn_Wk, L0_attn_Wv, L0_attn_Wo,
           L0_ffn_w_gate, L0_ffn_w_up, L0_ffn_w_down):
    x = np.ascontiguousarray(np.asarray(L0_attn_x, dtype=np.float32))
    sh = _prep_shared(np.asarray(L0_attn_Wq, dtype=np.float32),
                      np.asarray(L0_attn_Wk, dtype=np.float32),
                      np.asarray(L0_attn_Wv, dtype=np.float32),
                      np.asarray(L0_attn_Wo, dtype=np.float32),
                      np.asarray(L0_ffn_w_gate, dtype=np.float32),
                      np.asarray(L0_ffn_w_up, dtype=np.float32),
                      np.asarray(L0_ffn_w_down, dtype=np.float32))
    consts = _prep_consts()
    consts.update(sh)
    in_maps = [_prep_core(x, c, consts) for c in range(8)]
    nc = build_program()
    res = run_bass_kernel_spmd(nc, in_maps, core_ids=list(range(8)))
    out = np.empty((2, 4096, DM), dtype=np.float32)
    for c in range(8):
        b, q = c // 4, c % 4
        o = np.asarray(res.results[c]["outT"])
        out[b, q * TOWN:(q + 1) * TOWN, :] = o.reshape(DM, TOWN).T
    return out


# version 15
# speedup vs baseline: 1.1716x; 1.0002x over previous
import contextlib
import math

import numpy as np

import concourse.bass as bass
import concourse.mybir as mybir
from concourse.alu_op_type import AluOpType as OP
from concourse.bass_utils import run_bass_kernel_spmd

F32 = mybir.dt.float32
F32R = mybir.dt.float32r
BF16 = mybir.dt.bfloat16
I32 = mybir.dt.int32
AF = mybir.ActivationFunctionType

P = 128
DM = 2048
FI = DM // P
TOWN = 1024
NOC = TOWN // P
TPREF = 3072
NPC = TPREF // P
NCH = NPC + NOC
KVH = 4
H = 16
IDIM = 8192
NCB = 16
CBG = 4
EPS_LN = 1e-5
EPS_ATT = 1e-6
PB = 3.141592502593994
TWO_PI = 2 * math.pi
CW1 = 6.28125
CW2 = float(np.float32(TWO_PI - CW1))
INV_DM = 1.0 / DM

DEBUG_TAPS = []


class Ctr:
    def __init__(self, es, nc, name):
        self.sem = es.enter_context(nc.semaphore(name))
        self.v = 0

    def inc(self, ins, n=1):
        ins.then_inc(self.sem, n)
        self.v += n
        return self.v

    def dma(self, ins):
        return self.inc(ins, 16)

    def wait(self, eng, v=None):
        eng.wait_ge(self.sem, self.v if v is None else v)


def build_program():
    nc = bass.Bass("TRN2", target_bir_lowering=False, debug=False)

    d_xt = nc.dram_tensor("xt", [NCH, P, FI, P], F32R, kind="ExternalInput")
    d_pos = nc.dram_tensor("pos", [P, NCH], F32, kind="ExternalInput")
    d_posr = nc.dram_tensor("posrep", [P, NCH * 64], F32, kind="ExternalInput")
    d_mask = nc.dram_tensor("maskc", [P, NCH], BF16, kind="ExternalInput")
    d_ivh = nc.dram_tensor("ivh", [P, 64], F32, kind="ExternalInput")
    d_ivl = nc.dram_tensor("ivl", [P, 64], F32, kind="ExternalInput")
    d_idf = nc.dram_tensor("identf", [P, P], F32, kind="ExternalInput")
    d_idb = nc.dram_tensor("identb", [P, P], BF16, kind="ExternalInput")
    d_cmask = nc.dram_tensor("cmask4", [P, 4 * P], F32, kind="ExternalInput")
    d_onesr = nc.dram_tensor("onesr", [1, P], F32R, kind="ExternalInput")
    d_onesc = nc.dram_tensor("onescol", [P, 1], F32R, kind="ExternalInput")
    d_onescb = nc.dram_tensor("onescolb", [P, 1], BF16, kind="ExternalInput")
    d_wk = nc.dram_tensor("wk", [P, FI, 512], F32, kind="ExternalInput")
    d_wv = nc.dram_tensor("wv", [P, FI, 512], F32, kind="ExternalInput")
    d_wq = nc.dram_tensor("wq", [4, FI, P, 512], F32, kind="ExternalInput")
    d_wo = nc.dram_tensor("wo", [FI, P, FI, P], F32, kind="ExternalInput")
    d_wg = nc.dram_tensor("wg", [NCB, 4, P, FI, P], F32, kind="ExternalInput")
    d_wu = nc.dram_tensor("wu", [NCB, 4, P, FI, P], F32, kind="ExternalInput")
    d_wd = nc.dram_tensor("wd", [8, NCB, P, 4, 256], F32, kind="ExternalInput")
    d_out = nc.dram_tensor("outT", [FI, P, TOWN], F32, kind="ExternalOutput")
    d_ao = nc.dram_tensor("aopark", [FI, P, TOWN], F32R)
    d_x2 = nc.dram_tensor("x2park", [FI, P, TOWN], F32R)
    d_st = nc.dram_tensor("st3park", [1, 2048], F32R)
    d_tap = {}
    for name in DEBUG_TAPS:
        if name == "kv":
            d_tap["tap_kf"] = nc.dram_tensor("tap_kf", [P, NOC * 512], F32, kind="ExternalOutput")
            d_tap["tap_v"] = nc.dram_tensor("tap_v", [P, NOC * 512], F32, kind="ExternalOutput")
            d_tap["tap_st"] = nc.dram_tensor("tap_st", [P, 512], F32, kind="ExternalOutput")
            d_tap["tap_z"] = nc.dram_tensor("tap_z", [P, 4], F32, kind="ExternalOutput")
        if name == "q":
            d_tap["tap_qfT"] = nc.dram_tensor("tap_qfT", [P, H * TOWN], F32, kind="ExternalOutput")
        if name == "o":
            d_tap["tap_oT"] = nc.dram_tensor("tap_oT", [P, H * TOWN], F32, kind="ExternalOutput")
        if name == "x2":
            d_tap["tap_x2"] = nc.dram_tensor("tap_x2", [FI, P, TOWN], F32, kind="ExternalOutput")
        if name == "x3":
            d_tap["tap_x3"] = nc.dram_tensor("tap_x3", [P, FI * TOWN], F32, kind="ExternalOutput")
        if name == "hs":
            d_tap["tap_hs"] = nc.dram_tensor("tap_hs", [P, 4 * TOWN], F32, kind="ExternalOutput")

    es = contextlib.ExitStack()

    ivh = es.enter_context(nc.sbuf_tensor("s_ivh", [P, 64], F32))
    ivl = es.enter_context(nc.sbuf_tensor("s_ivl", [P, 64], F32))
    posS = es.enter_context(nc.sbuf_tensor("posS", [P, NCH], F32))
    maskS = es.enter_context(nc.sbuf_tensor("maskS", [P, NCH], BF16))
    idf = es.enter_context(nc.sbuf_tensor("idf", [P, P], F32))
    idb = es.enter_context(nc.sbuf_tensor("idb", [P, P], BF16))
    onesr = es.enter_context(nc.sbuf_tensor("s_onesr", [1, P], F32R))
    onesc = es.enter_context(nc.sbuf_tensor("onesc", [P, 1], F32R))
    onescb = es.enter_context(nc.sbuf_tensor("onescb", [P, 1], BF16))

    ld = Ctr(es, nc, "c_ld")
    gq = Ctr(es, nc, "c_gq")
    vq = Ctr(es, nc, "c_vq")
    pe = Ctr(es, nc, "c_pe")
    ac = Ctr(es, nc, "c_ac")
    dv = Ctr(es, nc, "c_dv")
    gqp = [Ctr(es, nc, f"c_gqp{i}") for i in range(8)]
    crq = [Ctr(es, nc, f"c_crq{i}") for i in range(2)]
    gqw = [Ctr(es, nc, f"c_gqw{i}") for i in range(4)]
    ldr = [Ctr(es, nc, f"c_ldr{i}") for i in range(2)]
    ldp = [Ctr(es, nc, f"c_ldp{i}") for i in range(2)]

    L_A = contextlib.ExitStack()
    L_B = contextlib.ExitStack()
    stp = L_B.enter_context(nc.sbuf_tensor("stp", [P, 512], F32))
    zpS = L_B.enter_context(nc.sbuf_tensor("zpS", [P, 4], F32))
    cmask4 = L_B.enter_context(nc.sbuf_tensor("s_cmask4", [P, 4 * P], F32))
    kf_tm = L_B.enter_context(nc.sbuf_tensor("kf_tm", [P, NOC * 512], BF16))
    kfT = L_B.enter_context(nc.sbuf_tensor("kfT", [P, KVH * TOWN], BF16))
    v_tm = L_B.enter_context(nc.sbuf_tensor("v_tm", [P, NOC * 512], BF16))
    rcolA = L_B.enter_context(nc.sbuf_tensor("rcolA", [P, NCH], F32R))
    rowsNm8 = L_B.enter_context(nc.sbuf_tensor("rowsNm8", [1, NOC * P], F32R))

    L_C = contextlib.ExitStack()
    sinT = L_C.enter_context(nc.sbuf_tensor("sinT", [P, NCH * 64], F32))
    cosT = L_C.enter_context(nc.sbuf_tensor("cosT", [P, NCH * 64], F32))
    konerow = L_C.enter_context(nc.sbuf_tensor("konerow", [1, 1024], F32R))
    xob = L_C.enter_context(nc.sbuf_tensor("xob", [P, NOC * DM], BF16))

    def xob_tile(c):
        return xob[:, c * DM:(c + 1) * DM]

    def col_bc(t, c, n):
        ap = t[:, c:c + 1]
        return bass.AP(ap.tensor, ap.offset, [[ap.ap[0][0], P], [0, n]])

    def strided_half(t, base_off, blocks, blk_stride, width):
        ap = t[:]
        return bass.AP(ap.tensor, ap.offset + base_off,
                       [[ap.ap[0][0], P], [blk_stride, blocks], [1, width]])

    def rope_bc(base, c):
        ap = base[:]
        return bass.AP(ap.tensor, ap.offset + c * 64,
                       [[ap.ap[0][0], P], [0, 8], [1, 64]])

    def crep_src(c, which):
        base = cosT if which == 0 else sinT
        ap = base[:]
        return bass.AP(ap.tensor, ap.offset + c * 64,
                       [[NCH * 64, P], [0, 8], [1, 64]])

    with contextlib.ExitStack() as ps:
        th = ps.enter_context(nc.sbuf_tensor("p0_th", [P, NCH * 64], F32))
        uu = ps.enter_context(nc.sbuf_tensor("p0_u", [P, NCH * 64], F32))
        ki = ps.enter_context(nc.sbuf_tensor("p0_ki", [P, NCH * 64], I32))
        kw = ps.enter_context(nc.sbuf_tensor("p0_kw", [P, NCH * 64], F32))
        prp = ps.enter_context(nc.sbuf_tensor("p0_prp", [P, NCH * 64], F32))
        ivr = ps.enter_context(nc.sbuf_tensor("p0_ivr", [P, 2 * NCH * 64], F32))
        with nc.Block() as blk:
            @blk.sync
            def _(s):
                for t, d in [(ivh, d_ivh), (ivl, d_ivl), (posS, d_pos), (maskS, d_mask),
                             (idf, d_idf), (idb, d_idb), (cmask4, d_cmask),
                             (onesr, d_onesr), (onesc, d_onesc), (onescb, d_onescb)]:
                    ld.dma(s.dma_start(t[:], d.ap()))
                ld.dma(s.dma_start(prp[:], d_posr.ap()))
                ld.wait(s)
                ivh_src = bass.AP(ivh[:].tensor, ivh[:].offset,
                                  [[64, P], [0, NCH], [1, 64]])
                ivl_src = bass.AP(ivl[:].tensor, ivl[:].offset,
                                  [[64, P], [0, NCH], [1, 64]])
                ld.dma(s.dma_start(ivr[:, 0:NCH * 64].rearrange("p (c j) -> p c j", c=NCH), ivh_src))
                ld.dma(s.dma_start(ivr[:, NCH * 64:].rearrange("p (c j) -> p c j", c=NCH), ivl_src))
                ld.wait(s)

            ld_all = ld.v

            @blk.gpsimd
            def _(g):
                for c in range(NOC):
                    gq.dma(g.dma_start(
                        xob_tile(c).rearrange("p (f t) -> p f t", f=FI),
                        d_xt.ap()[NPC + c]))
                gq.wait(g)

            @blk.vector
            def _(v):
                ld.wait(v, ld_all)
                NW = NCH * 64
                dv.inc(v.tensor_tensor(th[:], prp[:], ivr[:, NW:], OP.mult))
                dv.wait(v)
                dv.inc(v.scalar_tensor_tensor(uu[:], prp[:], 1.0, ivr[:, 0:NW], OP.mult, OP.mult))
                dv.wait(v)
                dv.inc(v.tensor_tensor(th[:], uu[:], th[:], OP.add))
                dv.wait(v)
                dv.inc(v.tensor_scalar(uu[:], th[:], 1.0 / TWO_PI, None, OP.mult))
                dv.wait(v)
                dv.inc(v.tensor_copy(ki[:], uu[:]))
                dv.wait(v)
                dv.inc(v.tensor_copy(kw[:], ki[:]))
                dv.wait(v)
                dv.inc(v.scalar_tensor_tensor(th[:], kw[:], -CW1, th[:], OP.mult, OP.add))
                dv.wait(v)
                dv.inc(v.scalar_tensor_tensor(th[:], kw[:], -CW2, th[:], OP.mult, OP.add))
                dv.wait(v)
                dv.inc(v.tensor_scalar(uu[:], th[:], PB, None, OP.is_gt))
                dv.wait(v)
                dv.inc(v.scalar_tensor_tensor(th[:], uu[:], -TWO_PI, th[:], OP.mult, OP.add))
                dv.wait(v)
                dv.inc(v.tensor_scalar(uu[:], th[:], -PB, None, OP.is_lt))
                dv.wait(v)
                dv.inc(v.scalar_tensor_tensor(th[:], uu[:], TWO_PI, th[:], OP.mult, OP.add))
                dv.wait(v)
                dv.inc(v.tensor_scalar(uu[:], th[:], PB, -PB, OP.min, OP.max))
                dv.inc(v.tensor_scalar(kw[:], th[:], math.pi / 2, None, OP.add))
                dv.wait(v)
                dv.inc(v.tensor_scalar(th[:], kw[:], PB, None, OP.is_gt))
                dv.wait(v)
                dv.inc(v.scalar_tensor_tensor(kw[:], th[:], -TWO_PI, kw[:], OP.mult, OP.add))
                dv.wait(v)
                dv.inc(v.tensor_scalar(kw[:], kw[:], PB, -PB, OP.min, OP.max))

            dv_tab = dv.v

            @blk.scalar
            def _(s):
                dv.wait(s, dv_tab)
                ac.inc(s.activation(sinT[:], uu[:], AF.Sin))
                ac.inc(s.activation(cosT[:], kw[:], AF.Sin))

    with contextlib.ExitStack() as ps:
        NPB = 8
        xpb = [ps.enter_context(nc.sbuf_tensor(f"xp{i}", [P, DM], BF16)) for i in range(NPB)]
        wkb = ps.enter_context(nc.sbuf_tensor("wkb", [P, FI * 512], BF16))
        wvb = ps.enter_context(nc.sbuf_tensor("wvb", [P, FI * 512], BF16))
        krop = [ps.enter_context(nc.sbuf_tensor(f"krop{i}", [P, 512], F32)) for i in range(2)]
        kaux = [ps.enter_context(nc.sbuf_tensor(f"kaux{i}", [P, 1024], F32)) for i in range(2)]
        kfpb = [ps.enter_context(nc.sbuf_tensor(f"kfpb{i}", [P, 512], BF16)) for i in range(2)]
        vpb = [ps.enter_context(nc.sbuf_tensor(f"vpb{i}", [P, 512], BF16)) for i in range(2)]
        sqS = [ps.enter_context(nc.sbuf_tensor("sqa0", [P, DM], BF16))] * 2
        rPS = [ps.enter_context(nc.sbuf_tensor("rPS0", [1, 1024], F32R))] * 2
        rSg = [ps.enter_context(nc.sbuf_tensor(f"rSg{i}", [1, 1024], F32R)) for i in range(2)]
        rQg = [ps.enter_context(nc.sbuf_tensor(f"rQg{i}", [1, 1024], F32R)) for i in range(2)]
        gtmp = ps.enter_context(nc.sbuf_tensor("gtmp", [1, 1024], F32R))

        pkv = [ps.enter_context(nc.psum_tensor(f"pkv{i}", [P, 1024], F32)) for i in range(2)]
        pstate = ps.enter_context(nc.psum_tensor("pstate", [P, 512], F32))
        pzm = ps.enter_context(nc.psum_tensor("pzm", [P, 512], F32))
        pstA = ps.enter_context(nc.psum_tensor("pstA", [P, 1024], F32))
        pzp = pzm[:, 0:4]
        pmT = pzm[:, 256:512].bitcast(BF16)

        ck_xbld = [0] * NCH
        ck_sq = [0] * NCH
        ck_smm = [0] * NCH
        ck_sev = [0] * NCH
        ck_fold = [0] * NCH
        ck_grp = [0] * (NCH // 8)
        ck_rows_g = [0] * (NCH // 8)
        ck_k1use = [0] * (NCH // 8)
        ck_kv = [0] * NCH
        ck_aff = [0] * NCH
        ck_rope = [0] * NCH
        ck_emin = [0] * NCH
        ck_exp = [0] * NCH
        ck_elu = [0] * NCH
        ck_kft = [0] * NCH
        ck_kfte = [0] * NCH
        ck_state = [0] * NCH

        def xb(c):
            return xpb[c % NPB][:] if c < NPC else xob_tile(c - NPC)

        def kf_dst(c):
            if c < NPC:
                return kfpb[c % 2][:]
            return kf_tm[:, (c - NPC) * 512:(c - NPC + 1) * 512]

        def v_dst(c):
            if c < NPC:
                return vpb[c % 2][:]
            return v_tm[:, (c - NPC) * 512:(c - NPC + 1) * 512]

        with nc.Block() as blk:
            @blk.gpsimd
            def _(g):
                gq.dma(g.dma_start(wkb[:].rearrange("p (f n) -> p f n", f=FI), d_wk.ap()))
                gq.dma(g.dma_start(wvb[:].rearrange("p (f n) -> p f n", f=FI), d_wv.ap()))
                gq.wait(g)

            ck_wkv = gq.v

            @blk.tensor
            def _(t):
                gq.wait(t, ck_wkv)
                for fi in range(FI):
                    pe.inc(t.matmul(pkv[0][0:1, 0:512], onescb[:],
                                    wkb[:, fi * 512:(fi + 1) * 512],
                                    start=(fi == 0), stop=(fi == FI - 1)))
                    pe.inc(t.matmul(pkv[0][0:1, 512:1024], onescb[:],
                                    wvb[:, fi * 512:(fi + 1) * 512],
                                    start=(fi == 0), stop=(fi == FI - 1)))

            ck_one_mm = pe.v

            @blk.scalar
            def _(s):
                pe.wait(s, ck_one_mm)
                ac.inc(s.activation(konerow[:], pkv[0][0:1, :], AF.Copy))

            ck_kvone = ac.v

            pend = [None]
            pend_tail = [None]
            for grp in range(NCH // 8):
                gpar = grp % 2
                for c in range(grp * 8, grp * 8 + 8):
                    own = c >= NPC
                    spar = c % 2

                    if not own:
                        @blk.gpsimd
                        def _(g, c=c):
                            if c >= NPB:
                                pe.wait(g, ck_kv[c - NPB])
                                dv.wait(g, ck_sq[c - NPB])
                            ck_xbld[c] = gqp[c % NPB].dma(g.dma_start(
                                xpb[c % NPB][:].rearrange("p (f t) -> p f t", f=FI),
                                d_xt.ap()[c]))

                    @blk.vector
                    def _(v, c=c, spar=spar, own=own):
                        if not own:
                            gqp[c % NPB].wait(v, ck_xbld[c])
                        if c >= 1:
                            pe.wait(v, ck_smm[c - 1])
                        xbc = xb(c)
                        ck_sq[c] = dv.inc(v.tensor_tensor(sqS[spar][:], xbc, xbc,
                                                          OP.mult))

                    @blk.tensor
                    def _(t, c=c, spar=spar, own=own, grp=grp):
                        if not own:
                            gqp[c % NPB].wait(t, ck_xbld[c])
                        dv.wait(t, ck_sq[c])
                        if c >= 1:
                            ac.wait(t, ck_sev[c - 1])
                        if c % 8 == 0 and grp >= 1:
                            ac.wait(t, ck_grp[grp - 1])
                        for f4 in range(4):
                            pe.inc(t.matmul(pstA[0:1, 0:512], onescb[:],
                                            xb(c)[:, f4 * 512:(f4 + 1) * 512],
                                            start=(f4 == 0), stop=(f4 == 3)))
                            pe.inc(t.matmul(pstA[0:1, 512:1024], onescb[:],
                                            sqS[spar][:, f4 * 512:(f4 + 1) * 512],
                                            start=(f4 == 0), stop=(f4 == 3)))
                        ck_smm[c] = pe.v

                    @blk.scalar
                    def _(s, c=c, spar=spar):
                        pe.wait(s, ck_smm[c])
                        if c >= 1:
                            dv.wait(s, ck_fold[c - 1])
                        ck_sev[c] = ac.inc(s.activation(rPS[spar][:], pstA[0:1, :],
                                                        AF.Copy))

                    @blk.vector
                    def _(v, c=c, spar=spar, grp=grp, gpar=gpar):
                        ac.wait(v, ck_sev[c])
                        if grp >= 2 and c % 8 == 0:
                            pe.wait(v, ck_k1use[grp - 2])
                        ps_ = rPS[spar]
                        k = (c % 8) * P
                        sl = slice(k, k + P)
                        dv.inc(v.tensor_tensor(rSg[gpar][0:1, sl], ps_[0:1, 0:128],
                                               ps_[0:1, 128:256], OP.add))
                        dv.inc(v.tensor_tensor(rQg[gpar][0:1, sl], ps_[0:1, 512:640],
                                               ps_[0:1, 640:768], OP.add))
                        dv.wait(v)
                        dv.inc(v.scalar_tensor_tensor(rSg[gpar][0:1, sl],
                                                      ps_[0:1, 256:384], 1.0,
                                                      rSg[gpar][0:1, sl],
                                                      OP.mult, OP.add))
                        dv.inc(v.scalar_tensor_tensor(rQg[gpar][0:1, sl],
                                                      ps_[0:1, 768:896], 1.0,
                                                      rQg[gpar][0:1, sl],
                                                      OP.mult, OP.add))
                        dv.wait(v)
                        dv.inc(v.scalar_tensor_tensor(rSg[gpar][0:1, sl],
                                                      ps_[0:1, 384:512], 1.0,
                                                      rSg[gpar][0:1, sl],
                                                      OP.mult, OP.add))
                        dv.inc(v.scalar_tensor_tensor(rQg[gpar][0:1, sl],
                                                      ps_[0:1, 896:1024], 1.0,
                                                      rQg[gpar][0:1, sl],
                                                      OP.mult, OP.add))
                        ck_fold[c] = dv.v

                @blk.vector
                def _(v, grp=grp, gpar=gpar):
                    dv.wait(v, ck_fold[grp * 8 + 7])
                    dv.inc(v.tensor_scalar(rSg[gpar][:], rSg[gpar][:], -INV_DM,
                                           None, OP.mult))
                    dv.wait(v)
                    dv.inc(v.tensor_tensor(gtmp[:], rSg[gpar][:], rSg[gpar][:],
                                           OP.mult))
                    dv.wait(v)
                    dv.inc(v.scalar_tensor_tensor(rQg[gpar][:], rQg[gpar][:], INV_DM,
                                                  gtmp[:], OP.mult, OP.subtract))
                    dv.wait(v)
                    dv.inc(v.tensor_scalar(rQg[gpar][:], rQg[gpar][:], EPS_LN,
                                           None, OP.add))

                ck_gvar = dv.v
                ck_rows_g[grp] = ck_gvar

                @blk.scalar
                def _(s, gpar=gpar, ck_gvar=ck_gvar):
                    dv.wait(s, ck_gvar)
                    ac.inc(s.activation(gtmp[:], rQg[gpar][:], AF.Ln))
                    ac.wait(s)
                    ac.inc(s.activation(rQg[gpar][:], gtmp[:], AF.Exp, scale=-0.5))

                ck_grstd = ac.v

                def mk_rt(grp=grp, gpar=gpar, ck_grstd=ck_grstd):
                    def emit():
                        @blk.tensor
                        def _(t, grp=grp, gpar=gpar, ck_grstd=ck_grstd):
                            ac.wait(t, ck_grstd)
                            if grp >= 1:
                                ac.wait(t, ck_grp[grp - 1])
                            for k in range(8):
                                pe.inc(t.transpose(
                                    pstA[:, k:k + 1],
                                    rQg[gpar][0:1, k * P:(k + 1) * P].bitcast(F32),
                                    idf[0:1, 0:1]))

                        ck_rt = pe.v

                        @blk.scalar
                        def _(s, grp=grp, ck_rt=ck_rt):
                            pe.wait(s, ck_rt)
                            ck_grp[grp] = ac.inc(s.activation(
                                rcolA[:, grp * 8:(grp + 1) * 8], pstA[:, 0:8],
                                AF.Copy))
                    return emit

                pend_rt = mk_rt()

                if grp == 3:
                    @blk.vector
                    def _(v, gpar=gpar, ck_grstd=ck_grstd):
                        ac.wait(v, ck_grstd)
                        dv.inc(v.tensor_copy(rowsNm8[:], rSg[gpar][:]))

                for c in range(grp * 8, grp * 8 + 8):
                    par = c % 2
                    own = c >= NPC
                    oc = c - NPC

                    if pend_tail[0] is not None and c % 8 == 1:
                        pend_tail[0]()
                        pend_tail[0] = None

                    @blk.tensor
                    def _(t, c=c, par=par, own=own, grp=grp, gpar=gpar):
                        if not own:
                            gqp[c % NPB].wait(t, ck_xbld[c])
                        gq.wait(t, ck_wkv)
                        dv.wait(t, ck_rows_g[grp])
                        if c < 2:
                            ac.wait(t, ck_kvone)
                        if c >= 2:
                            ac.wait(t, ck_aff[c - 2])
                        xbc = xb(c)
                        for fi in range(FI):
                            pe.inc(t.matmul(pkv[par][:, 0:512],
                                            xbc[:, fi * P:(fi + 1) * P],
                                            wkb[:, fi * 512:(fi + 1) * 512],
                                            start=(fi == 0), stop=False))
                            pe.inc(t.matmul(pkv[par][:, 512:1024],
                                            xbc[:, fi * P:(fi + 1) * P],
                                            wvb[:, fi * 512:(fi + 1) * 512],
                                            start=(fi == 0), stop=False))
                        nr = rSg[gpar][0:1, (c % 8) * P:(c % 8 + 1) * P]
                        pe.inc(t.matmul(pkv[par][:, 0:512], nr, konerow[0:1, 0:512],
                                        start=False, stop=True))
                        pe.inc(t.matmul(pkv[par][:, 512:1024], nr,
                                        konerow[0:1, 512:1024],
                                        start=False, stop=True))
                        ck_kv[c] = pe.v
                        if c % 8 == 7:
                            ck_k1use[grp] = pe.v

                    if c % 8 == 0 and pend_rt is not None:
                        pend_rt()
                        pend_rt = None

                    @blk.scalar
                    def _(s, c=c, par=par, own=own, grp=grp):
                        pe.wait(s, ck_kv[c])
                        ac.wait(s, ck_grp[grp])
                        if c >= 2:
                            dv.wait(s, ck_rope[c - 2])
                            if c - 2 < NPC:
                                pe.wait(s, ck_state[c - 2])
                            else:
                                pe.wait(s, ck_kft[c - 2])
                        rc = rcolA[:, c:c + 1].bitcast(F32)
                        ac.inc(s.activation(krop[par][:], pkv[par][:, 0:512],
                                            AF.Copy, scale=rc))
                        ck_aff[c] = ac.inc(s.activation(v_dst(c),
                                                        pkv[par][:, 512:1024],
                                                        AF.Copy, scale=rc))

                    @blk.vector
                    def _(v, c=c, par=par):
                        ac.wait(v, ck_aff[c])
                        if c >= 2:
                            dv.wait(v, ck_emin[c - 2])
                            ac.wait(v, ck_exp[c - 2])
                        kr = krop[par]
                        ka = kaux[par]
                        kr3 = kr[:].rearrange("p (r t) -> p r t", r=8)
                        dv.inc(v.tensor_tensor(
                            ka[:, 0:512].rearrange("p (r t) -> p r t", r=8),
                            kr3, rope_bc(cosT, c), OP.mult))
                        dv.inc(v.tensor_tensor(
                            ka[:, 512:1024].rearrange("p (r t) -> p r t", r=8),
                            kr3, rope_bc(sinT, c), OP.mult))
                        dv.wait(v)
                        alo = strided_half(ka, 0, 4, 128, 64)
                        ahi = strided_half(ka, 64, 4, 128, 64)
                        plo = strided_half(ka, 512, 4, 128, 64)
                        phi = strided_half(ka, 576, 4, 128, 64)
                        olo = strided_half(kr, 0, 4, 128, 64)
                        ohi = strided_half(kr, 64, 4, 128, 64)
                        dv.inc(v.tensor_tensor(olo, alo, phi, OP.subtract))
                        dv.inc(v.tensor_tensor(ohi, ahi, plo, OP.add))
                        ck_rope[c] = dv.v
                        dv.wait(v)
                        dv.inc(v.tensor_scalar(ka[:, 0:512], kr[:], 0.0, None,
                                               OP.min))
                        ck_emin[c] = dv.v

                    @blk.scalar
                    def _(s, c=c, par=par):
                        dv.wait(s, ck_emin[c])
                        ck_exp[c] = ac.inc(s.activation(kaux[par][:, 512:1024],
                                                        kaux[par][:, 0:512], AF.Exp))

                    @blk.vector
                    def _(v, c=c, par=par, own=own):
                        ac.wait(v, ck_exp[c])
                        if (not own) and c >= 2:
                            pe.wait(v, ck_state[c - 2])
                        dv.inc(v.scalar_tensor_tensor(kf_dst(c), krop[par][:], 0.0,
                                                      kaux[par][:, 512:1024],
                                                      OP.max, OP.add))
                        ck_elu[c] = dv.v

                    def mk_tail(c=c, oc=oc, own=own):
                        def emit():
                            if own:
                                @blk.tensor
                                def _(t, c=c, oc=oc):
                                    dv.wait(t, ck_elu[c])
                                    if c >= NPC + 1:
                                        ac.wait(t, ck_kfte[c - 1])
                                    src2 = kf_tm[:, oc * 512:(oc + 1) * 512]
                                    for h in range(KVH):
                                        pe.inc(t.transpose(
                                            pmT[:, h * 128:(h + 1) * 128],
                                            src2[:, h * 128:(h + 1) * 128],
                                            idb[:]))
                                    ck_kft[c] = pe.v

                                @blk.scalar
                                def _(s, c=c, oc=oc):
                                    pe.wait(s, ck_kft[c])
                                    for h in range(KVH):
                                        ac.inc(s.activation(
                                            kfT[:, h * TOWN + oc * P:
                                                h * TOWN + (oc + 1) * P],
                                            pmT[:, h * 128:(h + 1) * 128], AF.Copy))
                                    ck_kfte[c] = ac.v
                            else:
                                @blk.tensor
                                def _(t, c=c):
                                    dv.wait(t, ck_elu[c])
                                    kfb = kf_dst(c)
                                    vb = v_dst(c)
                                    last = (c == NPC - 1)
                                    for h in range(KVH):
                                        pe.inc(t.matmul(
                                            pstate[:, h * 128:(h + 1) * 128],
                                            kfb[:, h * 128:(h + 1) * 128],
                                            vb[:, h * 128:(h + 1) * 128],
                                            start=(c == 0 and h == 0),
                                            stop=(last and h == KVH - 1)))
                                        pe.inc(t.matmul(
                                            pzp[:, h:h + 1],
                                            kfb[:, h * 128:(h + 1) * 128],
                                            maskS[:, c:c + 1],
                                            start=(c == 0 and h == 0),
                                            stop=(last and h == KVH - 1)))
                                    ck_state[c] = pe.v
                        return emit

                    if pend[0] is not None:
                        pend[0]()
                    if c % 8 == 7:
                        pend[0] = None
                        pend_tail[0] = mk_tail()
                    else:
                        pend[0] = mk_tail()

            if pend_tail[0] is not None:
                pend_tail[0]()
                pend_tail[0] = None

            @blk.scalar
            def _(s):
                pe.wait(s, ck_state[NPC - 1])
                ac.inc(s.activation(stp[:], pstate[:], AF.Copy))
                ac.inc(s.activation(zpS[:], pzp, AF.Copy))

    qoT = L_A.enter_context(nc.sbuf_tensor("qoT", [P, H * TOWN], BF16))

    with contextlib.ExitStack() as ps:
        wqb = [ps.enter_context(nc.sbuf_tensor(f"wqb{i}", [P, 512], BF16)) for i in range(4)]
        qful = [ps.enter_context(nc.sbuf_tensor(f"qful{i}", [P, 512], F32)) for i in range(3)]
        qaux = [ps.enter_context(nc.sbuf_tensor(f"qaux{i}", [P, 1024], F32)) for i in range(3)]
        qbf = [ps.enter_context(nc.sbuf_tensor(f"qbf{i}", [P, 512], BF16)) for i in range(3)]
        qonerow = ps.enter_context(nc.sbuf_tensor("qonerow", [1, 512], F32R))
        pq = [ps.enter_context(nc.psum_tensor(f"pq{i}", [P, 2048], F32)) for i in range(2)]

        lastu = {"rope": [0, 0, 0], "emin": [0, 0, 0], "exp": [0, 0, 0], "ft": [0, 0, 0],
                 "crep": [0, 0], "wq": [0, 0, 0, 0], "qelu": [0, 0, 0]}
        pq_free = [[0] * 4, [0] * 4]
        with nc.Block() as blk:
            chains_seq = [0]
            for eq in range(4):
                half_k1 = [None, None]
                chains = [None, None]
                for half in range(2):
                    tcs = list(range(half * 4, half * 4 + 4))
                    pqh = pq[half]
                    ck_wld = [0] * FI
                    ck_use = [0] * FI

                    for fi in range(FI):
                        @blk.gpsimd
                        def _(g, eq=eq, fi=fi, ck_wld=ck_wld, ck_use=ck_use):
                            if fi >= 4:
                                pe.wait(g, ck_use[fi - 4])
                            elif eq + half > 0:
                                pe.wait(g, lastu["wq"][fi % 4])
                            ck_wld[fi] = gqw[fi % 4].dma(g.dma_start(wqb[fi % 4][:],
                                                                     d_wq.ap()[eq][fi]))

                        @blk.tensor
                        def _(t, eq=eq, half=half, fi=fi, tcs=tcs, pqh=pqh,
                              ck_wld=ck_wld, ck_use=ck_use):
                            gqw[fi % 4].wait(t, ck_wld[fi])
                            if fi == 0:
                                for j in range(4):
                                    ac.wait(t, pq_free[half][j])
                            if half == 0:
                                if eq > 0 and fi == 0:
                                    ac.wait(t, pq_free[1][0])
                                pe.inc(t.matmul(pq[1][0:1, 0:512], onescb[:],
                                                wqb[fi % 4][:],
                                                start=(fi == 0), stop=(fi == FI - 1)))
                            for j, tc in enumerate(tcs):
                                pe.inc(t.matmul(pqh[:, j * 512:(j + 1) * 512],
                                                xob_tile(tc)[:, fi * P:(fi + 1) * P],
                                                wqb[fi % 4][:],
                                                start=(fi == 0), stop=False))
                            ck_use[fi] = pe.v
                            lastu["wq"][fi % 4] = pe.v

                    ck_qmm = pe.v

                    if half == 0:
                        @blk.scalar
                        def _(s, ck_qmm=ck_qmm):
                            pe.wait(s, ck_qmm)
                            ac.inc(s.activation(qonerow[:], pq[1][0:1, 0:512],
                                                AF.Copy))

                        ck_qone = ac.v
                        pq_free[1][0] = ck_qone

                    ck_k1s = [0] * 4
                    for j, tc in enumerate(tcs):
                        @blk.tensor
                        def _(t, j=j, tc=tc, pqh=pqh, ck_qone=ck_qone):
                            ac.wait(t, ck_qone)
                            pe.inc(t.matmul(pqh[:, j * 512:(j + 1) * 512],
                                            rowsNm8[0:1, tc * P:(tc + 1) * P],
                                            qonerow[:], start=False, stop=True))
                        ck_k1s[j] = pe.v

                    def mk_chains(eq=eq, half=half, tcs=tcs, pqh=pqh, ck_k1s=ck_k1s):
                        def emit():
                            nonlocal_seq = [None]
                            for j, tc in enumerate(tcs):
                                par = chains_seq[0] % 3
                                chains_seq[0] += 1

                                @blk.scalar
                                def _(s, j=j, tc=tc, par=par, pqh=pqh):
                                    pe.wait(s, ck_k1s[j])
                                    dv.wait(s, lastu["rope"][par])
                                    ac.inc(s.activation(
                                        qful[par][:],
                                        pqh[:, j * 512:(j + 1) * 512], AF.Copy,
                                        scale=rcolA[:, NPC + tc:NPC + tc + 1].bitcast(F32)))

                                ck_qaff = ac.v

                                @blk.vector
                                def _(v, tc=tc, par=par, ck_qaff=ck_qaff):
                                    ac.wait(v, ck_qaff)
                                    dv.wait(v, lastu["emin"][par])
                                    ac.wait(v, lastu["exp"][par])
                                    ka = qaux[par]
                                    kr = qful[par]
                                    kr3 = kr[:].rearrange("p (r t) -> p r t", r=8)
                                    dv.inc(v.tensor_tensor(
                                        ka[:, 0:512].rearrange("p (r t) -> p r t", r=8),
                                        kr3, rope_bc(cosT, NPC + tc), OP.mult))
                                    dv.inc(v.tensor_tensor(
                                        ka[:, 512:1024].rearrange("p (r t) -> p r t", r=8),
                                        kr3, rope_bc(sinT, NPC + tc), OP.mult))
                                    dv.wait(v)
                                    alo = strided_half(ka, 0, 4, 128, 64)
                                    ahi = strided_half(ka, 64, 4, 128, 64)
                                    plo = strided_half(ka, 512, 4, 128, 64)
                                    phi = strided_half(ka, 576, 4, 128, 64)
                                    olo = strided_half(kr, 0, 4, 128, 64)
                                    ohi = strided_half(kr, 64, 4, 128, 64)
                                    dv.inc(v.tensor_tensor(olo, alo, phi, OP.subtract))
                                    dv.inc(v.tensor_tensor(ohi, ahi, plo, OP.add))
                                    lastu["rope"][par] = dv.v
                                    dv.wait(v)
                                    dv.inc(v.tensor_scalar(ka[:, 0:512], kr[:], 0.0,
                                                           None, OP.min))
                                    lastu["emin"][par] = dv.v

                                ck_emin = dv.v

                                @blk.scalar
                                def _(s, par=par, ck_emin=ck_emin):
                                    dv.wait(s, ck_emin)
                                    ac.inc(s.activation(qaux[par][:, 512:1024],
                                                        qaux[par][:, 0:512], AF.Exp))
                                    lastu["exp"][par] = ac.v

                                ck_exp = ac.v

                                @blk.vector
                                def _(v, par=par, ck_exp=ck_exp):
                                    ac.wait(v, ck_exp)
                                    pe.wait(v, lastu["ft"][par])
                                    dv.inc(v.scalar_tensor_tensor(
                                        qbf[par][:], qful[par][:], 0.0,
                                        qaux[par][:, 512:1024], OP.max, OP.add))
                                    lastu["qelu"][par] = dv.v

                                ck_qelu = dv.v

                                @blk.tensor
                                def _(t, j=j, par=par, pqh=pqh, ck_qelu=ck_qelu,
                                      ck_qaff=ck_qaff):
                                    dv.wait(t, ck_qelu)
                                    ac.wait(t, ck_qaff)
                                    pm = pqh[:, j * 512:j * 512 + 256].bitcast(BF16)
                                    for h in range(4):
                                        pe.inc(t.transpose(
                                            pm[:, h * 128:(h + 1) * 128],
                                            qbf[par][:, h * 128:(h + 1) * 128],
                                            idb[:]))
                                    lastu["ft"][par] = pe.v

                                ck_ft = pe.v

                                @blk.scalar
                                def _(s, j=j, tc=tc, eq=eq, half=half, pqh=pqh,
                                      ck_ft=ck_ft):
                                    pe.wait(s, ck_ft)
                                    pm = pqh[:, j * 512:j * 512 + 256].bitcast(BF16)
                                    for h in range(4):
                                        hg = eq * 4 + h
                                        ac.inc(s.activation(
                                            qoT[:, hg * TOWN + tc * P:
                                                hg * TOWN + (tc + 1) * P],
                                            pm[:, h * 128:(h + 1) * 128], AF.Copy))
                                    pq_free[half][j] = ac.v
                        return emit

                    chains[half] = mk_chains()

                chains[0]()
                chains[1]()

            if "tap_qfT" in d_tap:
                @blk.vector
                def _(v):
                    for j in range(4):
                        ac.wait(v, pq_free[0][j])
                        ac.wait(v, pq_free[1][j])
                    dv.inc(v.tensor_copy(qaux[0][:, 0:512], qoT[:, 0:512]))

                @blk.sync
                def _(s):
                    dv.wait(s)
                    pass

    with contextlib.ExitStack() as ps:
        stsb = ps.enter_context(nc.sbuf_tensor("stsb", [P, 512], BF16))
        zsb = ps.enter_context(nc.sbuf_tensor("zsb", [P, 4], BF16))
        smb = [ps.enter_context(nc.sbuf_tensor(f"smb{i}", [P, 512], BF16)) for i in range(2)]
        denS = ps.enter_context(nc.sbuf_tensor("denS", [1, 512], F32))
        recS = [ps.enter_context(nc.sbuf_tensor(f"recS{i}", [1, 512], F32R)) for i in range(2)]
        rbS = ps.enter_context(nc.sbuf_tensor("rbS", [P, 512], F32))

        pstate3 = ps.enter_context(nc.psum_tensor("pstate3", [P, 512], F32))
        pz3 = ps.enter_context(nc.psum_tensor("pz3", [P, 4], F32))
        psS = [ps.enter_context(nc.psum_tensor(f"psS{i}", [P, 512], F32)) for i in range(2)]
        pnum = [ps.enter_context(nc.psum_tensor(f"pnum{i}", [P, 512], F32)) for i in range(2)]
        pden = ps.enter_context(nc.psum_tensor("pden", [1, 512], F32))
        prb = ps.enter_context(nc.psum_tensor("prb", [P, 512], F32))

        with nc.Block() as blk:
            ck_stev = [0] * NOC
            ck_pedone = [0] * NOC
            ck_supd = [0] * NOC
            ck_smev = [0, 0]
            ck_smuse = [0, 0]
            ck_div_last = [0, 0]
            ck_rec_last = 0
            ck_rb_last = 0
            ck_rb_par = [0, 0]
            ck_rbev_last = 0
            ck_div2_last = [0]
            pend5 = [None]
            for oc in range(NOC):
                @blk.vector
                def _(v, oc=oc):
                    if oc == 0:
                        dv.inc(v.tensor_copy(stsb[:], stp[:]))
                        dv.inc(v.tensor_copy(zsb[:], zpS[:]))
                    else:
                        pe.wait(v, ck_supd[oc - 1])
                        pe.wait(v, ck_pedone[oc - 1])
                        dv.inc(v.tensor_tensor(stsb[:], pstate3[:], stp[:], OP.add))
                        dv.inc(v.tensor_tensor(zsb[:], pz3[:], zpS[:], OP.add))
                    ck_stev[oc] = dv.v

                @blk.tensor
                def _(t, oc=oc):
                    dv.wait(t, ck_stev[oc])
                    kfb = kf_tm[:, oc * 512:(oc + 1) * 512]
                    vb = v_tm[:, oc * 512:(oc + 1) * 512]
                    for h in range(KVH):
                        pe.inc(t.matmul(pstate3[:, h * 128:(h + 1) * 128],
                                        kfb[:, h * 128:(h + 1) * 128],
                                        vb[:, h * 128:(h + 1) * 128],
                                        start=(oc == 0 and h == 0),
                                        stop=(h == KVH - 1),
                                        skip_group_check=(oc > 0)))
                        pe.inc(t.matmul(pz3[:, h:h + 1],
                                        kfb[:, h * 128:(h + 1) * 128],
                                        maskS[:, NPC + oc:NPC + oc + 1],
                                        start=(oc == 0 and h == 0),
                                        stop=(h == KVH - 1),
                                        skip_group_check=(oc > 0)))
                    ck_supd[oc] = pe.v

                for kv in range(KVH):
                    par = (oc * KVH + kv) % 2

                    @blk.tensor
                    def _(t, oc=oc, kv=kv, par=par):
                        dv.wait(t, ck_smev[par])
                        for h in range(4):
                            hg = kv * 4 + h
                            pe.inc(t.matmul(
                                psS[par][:, h * 128:(h + 1) * 128],
                                kfT[:, kv * TOWN + oc * P:kv * TOWN + (oc + 1) * P],
                                qoT[:, hg * TOWN + oc * P:hg * TOWN + (oc + 1) * P],
                                start=(h == 0), stop=(h == 3)))

                    ck_smm = pe.v

                    @blk.vector
                    def _(v, par=par, ck_smm=ck_smm):
                        pe.wait(v, ck_smm)
                        pe.wait(v, ck_smuse[par])
                        dv.inc(v.tensor_tensor(smb[par][:], psS[par][:], cmask4[:], OP.mult))
                        ck_smev[par] = dv.v

                    ck_sm = dv.v

                    @blk.tensor
                    def _(t, oc=oc, kv=kv, par=par, ck_sm=ck_sm):
                        dv.wait(t, ck_sm)
                        dv.wait(t, ck_stev[oc])
                        dv.wait(t, ck_div_last[par])
                        vb = v_tm[:, oc * 512 + kv * 128:oc * 512 + (kv + 1) * 128]
                        for h in range(4):
                            pe.inc(t.matmul(pnum[par][:, h * 128:(h + 1) * 128],
                                            vb, smb[par][:, h * 128:(h + 1) * 128],
                                            start=(h == 0), stop=False))
                        for h in range(4):
                            hg = kv * 4 + h
                            pe.inc(t.matmul(
                                pnum[par][:, h * 128:(h + 1) * 128],
                                stsb[:, kv * 128:(kv + 1) * 128],
                                qoT[:, hg * TOWN + oc * P:hg * TOWN + (oc + 1) * P],
                                start=False, stop=(h == 3)))

                    ck_num = pe.v

                    @blk.tensor
                    def _(t, oc=oc, kv=kv, par=par):
                        dv.wait(t, ck_rec_last)
                        for h in range(4):
                            pe.inc(t.matmul(pden[0:1, h * 128:(h + 1) * 128],
                                            onescb[:], smb[par][:, h * 128:(h + 1) * 128],
                                            start=(h == 0), stop=False))
                        for h in range(4):
                            hg = kv * 4 + h
                            pe.inc(t.matmul(
                                pden[0:1, h * 128:(h + 1) * 128],
                                zsb[:, kv:kv + 1],
                                qoT[:, hg * TOWN + oc * P:hg * TOWN + (oc + 1) * P],
                                start=False, stop=(h == 3)))

                    ck_denmm = pe.v
                    ck_smuse[par] = pe.v

                    sidx = oc * KVH + kv
                    rpar = sidx % 2

                    @blk.vector
                    def _(v, rpar=rpar, ck_denmm=ck_denmm):
                        pe.wait(v, ck_denmm)
                        pe.wait(v, ck_rb_par[rpar])
                        dv.inc(v.tensor_scalar(denS[:], pden[0:1, :], EPS_ATT, None, OP.add))
                        dv.wait(v)
                        with nc.allow_low_precision(reason="f32r view of fp32 recip"):
                            dv.inc(v.reciprocal(recS[rpar][:], denS[:]))

                    ck_rec_last = dv.v

                    def mk_tail5(oc=oc, kv=kv, par=par, rpar=rpar,
                                 ck_rec=ck_rec_last, ck_num=ck_num):
                        def emit():
                            global_ck = {}

                            @blk.tensor
                            def _(t, ck_rec=ck_rec, ck_rbev=ck_rbev_last):
                                dv.wait(t, ck_rec)
                                ac.wait(t, ck_rbev)
                                pe.inc(t.matmul(prb[:], onesr[:], recS[rpar][:],
                                                start=True, stop=True))

                            ck_rb = pe.v

                            @blk.scalar
                            def _(s, ck_rb=ck_rb):
                                pe.wait(s, ck_rb)
                                dv.wait(s, ck_div2_last[0])
                                ac.inc(s.activation(rbS[:], prb[:], AF.Copy))

                            ck_rbev = ac.v

                            @blk.vector
                            def _(v, oc=oc, kv=kv, par=par, ck_rbev=ck_rbev,
                                  ck_num=ck_num):
                                ac.wait(v, ck_rbev)
                                pe.wait(v, ck_num)
                                otap = qoT[:]
                                dst = bass.AP(otap.tensor,
                                              otap.offset + (kv * 4) * TOWN + oc * P,
                                              [[H * TOWN, P], [TOWN, 4], [1, P]])
                                dv.inc(v.tensor_tensor(dst, pnum[par][:], rbS[:],
                                                       OP.mult))
                                ck_div_last[par] = dv.v
                                ck_div2_last[0] = dv.v
                            return ck_rb, ck_rbev
                        return emit

                    if pend5[0] is not None:
                        ck_rb_par[1 - rpar], ck_rbev_last = pend5[0]()
                        ck_rb_last = ck_rb_par[1 - rpar]
                    pend5[0] = mk_tail5()

                ck_pedone[oc] = pe.v

            if pend5[0] is not None:
                rp_last = (NOC * KVH - 1) % 2
                ck_rb_par[rp_last], ck_rbev_last = pend5[0]()
                pend5[0] = None

    with contextlib.ExitStack() as ps:
        wob = [ps.enter_context(nc.sbuf_tensor(f"wob{i}", [P, FI * P], BF16)) for i in range(2)]
        aor = [ps.enter_context(nc.sbuf_tensor(f"aor{i}", [P, TOWN], F32R)) for i in range(2)]
        sqr = ps.enter_context(nc.sbuf_tensor("sqr", [P, TOWN], F32R))
        rws = ps.enter_context(nc.sbuf_tensor("rws", [1, 3 * TOWN], F32R))
        mrb = ps.enter_context(nc.sbuf_tensor("mrb", [P, 2 * TOWN], F32))
        x2w = [ps.enter_context(nc.sbuf_tensor(f"x2w{i}", [P, TOWN], F32R)) for i in range(2)]
        xre = [ps.enter_context(nc.sbuf_tensor(f"xre{i}", [P, TOWN], F32R)) for i in range(2)]

        pwo = [ps.enter_context(nc.psum_tensor(f"pwo{i}", [P, 1024], F32)) for i in range(2)]
        pls = ps.enter_context(nc.psum_tensor("pls", [1, 1024], F32))
        plq = ps.enter_context(nc.psum_tensor("plq", [1, 1024], F32))

        with nc.Block() as blk:
            ck_wld = [0] * FI
            ck_womm = [0] * FI
            ck_aoev = [0] * FI
            ck_aopark = [0] * FI
            ck_sqt = [0] * FI
            ck_sqmm = [0] * FI

            pend4 = [None]
            for fc in range(FI):
                par = fc % 2

                @blk.gpsimd
                def _(g, fc=fc, par=par):
                    if fc >= 2:
                        pe.wait(g, ck_womm[fc - 2])
                    ck_wld[fc] = gqw[par].dma(g.dma_start(
                        wob[par][:].rearrange("p (f n) -> p f n", f=FI),
                        d_wo.ap()[fc]))

                @blk.tensor
                def _(t, fc=fc, par=par):
                    gqw[par].wait(t, ck_wld[fc])
                    if fc >= 2:
                        ac.wait(t, ck_aoev[fc - 2])
                    for fo in range(FI):
                        wtile = wob[par][:, fo * P:(fo + 1) * P]
                        for th in range(2):
                            pe.inc(t.matmul(
                                pwo[par][:, th * 512:(th + 1) * 512],
                                wtile,
                                qoT[:, fo * TOWN + th * 512:fo * TOWN + (th + 1) * 512],
                                start=(fo == 0), stop=(fo == FI - 1)))
                    ck_womm[fc] = pe.v

                @blk.scalar
                def _(s, fc=fc, par=par):
                    pe.wait(s, ck_womm[fc])
                    if fc >= 2:
                        ldp[par].wait(s, ck_aopark[fc - 2])
                        pe.wait(s, ck_sqmm[fc - 2] if fc >= 2 else 0)
                    ck_aoev[fc] = ac.inc(s.activation(aor[par][:], pwo[par][:], AF.Copy))

                @blk.sync
                def _(s, fc=fc, par=par):
                    ac.wait(s, ck_aoev[fc])
                    ck_aopark[fc] = ldp[par].dma(s.dma_start(d_ao.ap()[fc], aor[par][:]))

                def mk_stats(fc=fc, par=par):
                    def emit():
                        @blk.tensor
                        def _(t, fc=fc, par=par):
                            ac.wait(t, ck_aoev[fc])
                            for th in range(2):
                                pe.inc(t.matmul(pls[0:1, th * 512:(th + 1) * 512],
                                                onesc[:],
                                                aor[par][:, th * 512:(th + 1) * 512],
                                                start=(fc == 0), stop=(fc == FI - 1)))
                            dv.wait(t, ck_sqt[fc])
                            for th in range(2):
                                pe.inc(t.matmul(plq[0:1, th * 512:(th + 1) * 512],
                                                onesc[:],
                                                sqr[:, th * 512:(th + 1) * 512],
                                                start=(fc == 0), stop=(fc == FI - 1)))
                            ck_sqmm[fc] = pe.v
                    return emit

                if pend4[0] is not None:
                    pend4[0]()
                pend4[0] = mk_stats()

                @blk.vector
                def _(v, fc=fc, par=par):
                    ac.wait(v, ck_aoev[fc])
                    if fc >= 1:
                        pe.wait(v, ck_sqmm[fc - 1])
                    ck_sqt[fc] = dv.inc(v.tensor_tensor(sqr[:], aor[par][:],
                                                        aor[par][:], OP.mult))

            pend4[0]()

            @blk.vector
            def _(v):
                pe.wait(v, ck_sqmm[FI - 1])
                dv.inc(v.tensor_scalar(rws[0:1, 0:1024], pls[0:1, :], INV_DM, None, OP.mult))
                dv.wait(v)
                dv.inc(v.tensor_tensor(rws[0:1, 1024:2048], rws[0:1, 0:1024],
                                       rws[0:1, 0:1024], OP.mult))
                dv.wait(v)
                dv.inc(v.scalar_tensor_tensor(rws[0:1, 2048:3072], plq[0:1, :], INV_DM,
                                              rws[0:1, 1024:2048], OP.mult, OP.subtract))
                dv.wait(v)
                dv.inc(v.tensor_scalar(rws[0:1, 2048:3072], rws[0:1, 2048:3072], EPS_LN,
                                       None, OP.add))

            ck_r1 = dv.v

            @blk.scalar
            def _(s):
                dv.wait(s, ck_r1)
                ac.inc(s.activation(rws[0:1, 1024:2048], rws[0:1, 2048:3072], AF.Ln))
                ac.wait(s)
                ac.inc(s.activation(rws[0:1, 2048:3072], rws[0:1, 1024:2048], AF.Exp, scale=-0.5))

            ck_r2 = ac.v

            @blk.tensor
            def _(t):
                ac.wait(t, ck_r2)
                ac.wait(t, ck_aoev[FI - 1])
                for th in range(2):
                    pe.inc(t.matmul(pwo[0][:, th * 512:(th + 1) * 512], onesr[:],
                                    rws[0:1, th * 512:(th + 1) * 512],
                                    start=True, stop=True))
                    pe.inc(t.matmul(pwo[1][:, th * 512:(th + 1) * 512], onesr[:],
                                    rws[0:1, 2048 + th * 512:2048 + (th + 1) * 512],
                                    start=True, stop=True))

            ck_bc = pe.v

            @blk.scalar
            def _(s):
                pe.wait(s, ck_bc)
                ac.inc(s.activation(mrb[:, 0:1024], pwo[0][:], AF.Copy))
                ac.inc(s.activation(mrb[:, 1024:2048], pwo[1][:], AF.Copy))

            ck_mrb = ac.v

            ck_aold = [0] * FI
            ck_xld2 = [0] * FI
            ck_t1 = [0] * FI
            ck_x2w = [0] * FI
            ck_x2pk = [0] * FI
            ck_x2sq = [0] * FI
            ck_x2smm = [0] * FI
            for fc in range(FI):
                par = fc % 2

                @blk.sync
                def _(s, fc=fc, par=par):
                    if fc == 0:
                        ldp[0].wait(s)
                        ldp[1].wait(s)
                    if fc < 2:
                        pe.wait(s, ck_sqmm[FI - 1])
                    if fc >= 2:
                        dv.wait(s, ck_t1[fc - 2])
                        dv.wait(s, ck_x2w[fc - 2])
                        pe.wait(s, ck_x2smm[fc - 2])
                    ck_aold[fc] = ldr[par].dma(s.dma_start(aor[par][:], d_ao.ap()[fc]))
                    ck_xld2[fc] = ldr[par].dma(s.dma_start(
                        xre[par][:],
                        bass.AP(d_xt, NPC * P * FI * P + fc * P,
                                [[FI * P, P], [P * FI * P, NOC], [1, P]])))

                @blk.vector
                def _(v, fc=fc, par=par):
                    ldr[par].wait(v, ck_xld2[fc])
                    ac.wait(v, ck_mrb)
                    if fc >= 2:
                        ldp[par].wait(v, ck_x2pk[fc - 2])
                    dv.inc(v.tensor_tensor(x2w[par][:], aor[par][:], mrb[:, 0:1024], OP.subtract))
                    dv.wait(v)
                    dv.inc(v.tensor_tensor(aor[par][:], x2w[par][:], mrb[:, 1024:2048], OP.mult))
                    ck_t1[fc] = dv.v
                    dv.wait(v)
                    ldr[par].wait(v, ck_xld2[fc])
                    dv.inc(v.tensor_tensor(x2w[par][:], aor[par][:], xre[par][:], OP.add))
                    ck_x2w[fc] = dv.v
                    dv.wait(v)
                    if fc == 0:
                        pe.wait(v, ck_sqmm[FI - 1])
                    else:
                        pe.wait(v, ck_x2smm[fc - 1])
                    dv.inc(v.tensor_tensor(sqr[:], x2w[par][:], x2w[par][:], OP.mult))
                    ck_x2sq[fc] = dv.v

                @blk.tensor
                def _(t, fc=fc, par=par):
                    dv.wait(t, ck_x2w[fc])
                    if fc == 0:
                        dv.wait(t, ck_r1)
                    for th in range(2):
                        pe.inc(t.matmul(pls[0:1, th * 512:(th + 1) * 512], onesc[:],
                                        x2w[par][:, th * 512:(th + 1) * 512],
                                        start=(fc == 0), stop=(fc == FI - 1)))
                    dv.wait(t, ck_x2sq[fc])
                    for th in range(2):
                        pe.inc(t.matmul(plq[0:1, th * 512:(th + 1) * 512], onesc[:],
                                        sqr[:, th * 512:(th + 1) * 512],
                                        start=(fc == 0), stop=(fc == FI - 1)))
                    ck_x2smm[fc] = pe.v

                @blk.sync
                def _(s, fc=fc, par=par):
                    dv.wait(s, ck_x2w[fc])
                    ck_x2pk[fc] = ldp[par].dma(s.dma_start(d_x2.ap()[fc], x2w[par][:]))

            @blk.scalar
            def _(s):
                pe.wait(s, ck_x2smm[FI - 1])
                ac.inc(s.activation(rws[0:1, 0:1024], pls[0:1, :], AF.Copy))
                ck_stev = ac.inc(s.activation(rws[0:1, 1024:2048], plq[0:1, :], AF.Copy))
                ac.wait(s, ck_stev)
                vq.dma(s.dma_start(d_st.ap(), rws[0:1, 0:2048]))

            ck_stpark = vq.v

            @blk.sync
            def _(s):
                ldp[0].wait(s)
                ldp[1].wait(s)
                vq.wait(s, ck_stpark)

    L_A.close()
    L_C.close()
    L_B.close()

    P56 = contextlib.ExitStack()
    acc = P56.enter_context(nc.sbuf_tensor("acc", [P, FI * TOWN], F32R))
    P5ab = contextlib.ExitStack()
    x3h = P5ab.enter_context(nc.sbuf_tensor("x3h", [P, FI * TOWN], BF16))
    m3b = P5ab.enter_context(nc.sbuf_tensor("m3b", [P, TOWN], F32))
    r3b = P5ab.enter_context(nc.sbuf_tensor("r3b", [P, TOWN], F32))

    with contextlib.ExitStack() as ps:
        x2r = [ps.enter_context(nc.sbuf_tensor(f"x2r{i}", [P, TOWN], F32R)) for i in range(2)]
        rw5 = ps.enter_context(nc.sbuf_tensor("rw5", [1, 3 * TOWN], F32R))
        scr5 = ps.enter_context(nc.sbuf_tensor("scr5", [P, TOWN], F32))
        pb5 = [ps.enter_context(nc.psum_tensor(f"pb5{i}", [P, 512], F32)) for i in range(4)]

        with nc.Block() as blk:
            ck_x2l = [0] * FI

            @blk.sync
            def _(s):
                ck_stld = vq.dma(s.dma_start(rw5[0:1, 0:2048], d_st.ap()))
                vq.wait(s, ck_stld)

            ck_stld = vq.v

            @blk.vector
            def _(v):
                vq.wait(v, ck_stld)
                dv.inc(v.tensor_scalar(rw5[0:1, 0:1024], rw5[0:1, 0:1024], INV_DM,
                                       None, OP.mult))
                dv.wait(v)
                dv.inc(v.tensor_tensor(rw5[0:1, 2048:3072], rw5[0:1, 0:1024],
                                       rw5[0:1, 0:1024], OP.mult))
                dv.wait(v)
                dv.inc(v.scalar_tensor_tensor(rw5[0:1, 1024:2048], rw5[0:1, 1024:2048],
                                              INV_DM, rw5[0:1, 2048:3072],
                                              OP.mult, OP.subtract))
                dv.wait(v)
                dv.inc(v.tensor_scalar(rw5[0:1, 2048:3072], rw5[0:1, 1024:2048], EPS_LN,
                                       None, OP.add))

            ck_r1 = dv.v

            @blk.scalar
            def _(s):
                dv.wait(s, ck_r1)
                ac.inc(s.activation(rw5[0:1, 1024:2048], rw5[0:1, 2048:3072], AF.Ln))
                ac.wait(s)
                ac.inc(s.activation(rw5[0:1, 2048:3072], rw5[0:1, 1024:2048], AF.Exp, scale=-0.5))

            ck_r2 = ac.v

            @blk.tensor
            def _(t):
                ac.wait(t, ck_r2)
                for th in range(2):
                    pe.inc(t.matmul(pb5[th][:], onesr[:],
                                    rw5[0:1, th * 512:(th + 1) * 512], start=True, stop=True))
                    pe.inc(t.matmul(pb5[2 + th][:], onesr[:],
                                    rw5[0:1, 2048 + th * 512:2048 + (th + 1) * 512],
                                    start=True, stop=True))

            ck_bc = pe.v

            @blk.scalar
            def _(s):
                pe.wait(s, ck_bc)
                for th in range(2):
                    ac.inc(s.activation(m3b[:, th * 512:(th + 1) * 512], pb5[th][:], AF.Copy))
                    ac.inc(s.activation(r3b[:, th * 512:(th + 1) * 512], pb5[2 + th][:], AF.Copy))

            ck_m3 = ac.v

            ck_x3 = [0] * FI
            for fc in range(FI):
                par = fc % 2

                @blk.sync
                def _(s, fc=fc, par=par):
                    if fc >= 2:
                        dv.wait(s, ck_x3[fc - 2])
                    ck_x2l[fc] = ldr[par].dma(s.dma_start(x2r[par][:], d_x2.ap()[fc]))

                @blk.vector
                def _(v, fc=fc, par=par):
                    ldr[par].wait(v, ck_x2l[fc])
                    ac.wait(v, ck_m3)
                    dv.wait(v)
                    dv.inc(v.tensor_tensor(scr5[:], x2r[par][:], m3b[:], OP.subtract))
                    dv.wait(v)
                    dv.inc(v.tensor_tensor(x3h[:, fc * TOWN:(fc + 1) * TOWN], scr5[:],
                                           r3b[:], OP.mult))
                    ck_x3[fc] = dv.v

            ck_x3_all = dv.v

    P5ab_keep = (x3h,)

    with contextlib.ExitStack() as ps:
        wgb = [ps.enter_context(nc.sbuf_tensor(f"wgb{i}", [P, FI * P], BF16)) for i in range(2)]
        wub = [ps.enter_context(nc.sbuf_tensor(f"wub{i}", [P, FI * P], BF16)) for i in range(2)]
        wdb = [ps.enter_context(nc.sbuf_tensor(f"wdb{i}", [P, 1024], BF16)) for i in range(2)]
        sgb = ps.enter_context(nc.sbuf_tensor("sgb", [P, 1024], F32))
        hb = ps.enter_context(nc.sbuf_tensor("hb", [P, 4 * 1024], F32))
        hq = ps.enter_context(nc.sbuf_tensor("hq", [P, 4 * 1024], F32))
        tr = [ps.enter_context(nc.sbuf_tensor(f"tr{i}", [P, 1024], F32)) for i in range(3)]
        hsb = ps.enter_context(nc.sbuf_tensor("hsb", [P, CBG * 4096], BF16))
        pgu = [ps.enter_context(nc.psum_tensor(f"pgu{i}", [P, 2048], F32)) for i in range(2)]
        pd = pgu[0]

        with nc.Block() as blk:
            ck_wg = {}
            ck_wd = {}
            ck_gu = {}
            ck_sil = {}
            ck_h = {}
            ck_hs = [0] * NCB
            ck_dmm = {}
            ck_fl = {}
            ck_hq = [0] * NCB
            last_flush = 0
            last_dmm_cbg = 0

            for cbg in range(NCB // CBG):
                for cbb in range(CBG):
                    cb = cbg * CBG + cbb
                    for j in range(4):
                        pgi = (cb * 4 + j) % 2

                        @blk.gpsimd
                        def _(g, cb=cb, j=j, pgi=pgi):
                            prev = ck_gu.get((cb * 4 + j - 2), None)
                            if prev is not None:
                                pe.wait(g, prev)
                            gqw[pgi].dma(g.dma_start(
                                wgb[pgi][:].rearrange("p (f n) -> p f n", f=FI),
                                d_wg.ap()[cb][j]))
                            ck_wg[cb * 4 + j] = gqw[pgi].dma(g.dma_start(
                                wub[pgi][:].rearrange("p (f n) -> p f n", f=FI),
                                d_wu.ap()[cb][j]))

                        @blk.tensor
                        def _(t, cb=cb, j=j, pgi=pgi, cbg=cbg):
                            gqw[pgi].wait(t, ck_wg[cb * 4 + j])
                            dv.wait(t, ck_x3_all)
                            prevh = ck_h.get(cb * 4 + j - 2)
                            if prevh is not None:
                                dv.wait(t, prevh)
                            if cbg > 0 and cb == cbg * CBG and j < 2:
                                dv.wait(t, last_flush)
                            for fi in range(FI):
                                wgt = wgb[pgi][:, fi * P:(fi + 1) * P]
                                wut = wub[pgi][:, fi * P:(fi + 1) * P]
                                for th in range(2):
                                    pe.inc(t.matmul(
                                        pgu[pgi][:, th * 512:(th + 1) * 512], wgt,
                                        x3h[:, fi * TOWN + th * 512:fi * TOWN + (th + 1) * 512],
                                        start=(fi == 0), stop=(fi == FI - 1)))
                                for th in range(2):
                                    pe.inc(t.matmul(
                                        pgu[pgi][:, 1024 + th * 512:1024 + (th + 1) * 512], wut,
                                        x3h[:, fi * TOWN + th * 512:fi * TOWN + (th + 1) * 512],
                                        start=(fi == 0), stop=(fi == FI - 1)))
                            ck_gu[cb * 4 + j] = pe.v

                        @blk.scalar
                        def _(s, cb=cb, j=j, pgi=pgi):
                            pe.wait(s, ck_gu[cb * 4 + j])
                            prevh = ck_h.get(cb * 4 + j - 1)
                            if prevh is not None:
                                dv.wait(s, prevh)
                            ck_sil[cb * 4 + j] = ac.inc(
                                s.activation(sgb[:], pgu[pgi][:, 0:1024], AF.Silu))

                        @blk.vector
                        def _(v, cb=cb, j=j, pgi=pgi):
                            ac.wait(v, ck_sil[cb * 4 + j])
                            if j == 0:
                                dv.wait(v, ck_hq[cb - 1] if cb >= 1 else 0)
                            dv.inc(v.tensor_tensor(hb[:, j * 1024:(j + 1) * 1024], sgb[:],
                                                   pgu[pgi][:, 1024:2048], OP.mult))
                            ck_h[cb * 4 + j] = dv.v

                    @blk.vector
                    def _(v, cb=cb, cbb=cbb):
                        dv.wait(v, ck_h[cb * 4 + 3])
                        ck_hq[cb] = dv.inc(v.tensor_tensor(hq[:], hb[:], hb[:], OP.mult))

                    @blk.vector
                    def _(v, cb=cb, cbb=cbb, cbg=cbg):
                        dv.wait(v, ck_hq[cb])
                        if cbg > 0 and cbb == 0:
                            pe.wait(v, last_dmm_cbg)
                        q = [hq[:, j * 1024:(j + 1) * 1024] for j in range(4)]
                        dv.inc(v.tensor_tensor(tr[0][:], q[0], q[1], OP.min))
                        dv.inc(v.tensor_tensor(tr[1][:], q[0], q[1], OP.max))
                        dv.inc(v.tensor_tensor(tr[2][:], q[2], q[3], OP.min))
                        dv.wait(v)
                        dv.inc(v.tensor_tensor(tr[0][:], tr[0][:], tr[2][:], OP.max))
                        dv.wait(v)
                        dv.inc(v.tensor_tensor(tr[2][:], q[2], q[3], OP.max))
                        dv.wait(v)
                        dv.inc(v.tensor_tensor(tr[1][:], tr[1][:], tr[2][:], OP.min))
                        dv.wait(v)
                        dv.inc(v.tensor_tensor(tr[0][:], tr[0][:], tr[1][:], OP.max))
                        dv.wait(v)
                        for j in range(4):
                            dv.inc(v.tensor_tensor(tr[1][:], q[j], tr[0][:], OP.is_ge))
                            dv.wait(v)
                            dv.inc(v.tensor_tensor(
                                hsb[:, cbb * 4096 + j * 1024:(cbb * 4096 + (j + 1) * 1024)],
                                hb[:, j * 1024:(j + 1) * 1024], tr[1][:], OP.mult))
                            dv.wait(v)
                        ck_hs[cb] = dv.v

                for fcp in range(8):
                    pdl = pgu[fcp % 2]
                    for cbb in range(CBG):
                        idx = fcp * CBG + cbb

                        @blk.gpsimd
                        def _(g, cbg=cbg, fcp=fcp, cbb=cbb, idx=idx):
                            prev = ck_dmm.get((cbg, idx - 2))
                            if prev is not None:
                                pe.wait(g, prev)
                            elif cbg > 0:
                                pe.wait(g, last_dmm_cbg)
                            ck_wd[(cbg, idx)] = gqw[2 + idx % 2].dma(g.dma_start(
                                wdb[idx % 2][:].rearrange("p (f n) -> p f n", f=4),
                                d_wd.ap()[fcp][cbg * CBG + cbb]))

                        @blk.tensor
                        def _(t, cbg=cbg, fcp=fcp, cbb=cbb, idx=idx, pdl=pdl):
                            if fcp == 0:
                                dv.wait(t, ck_hs[cbg * CBG + cbb])
                                if cbb == 0:
                                    dv.wait(t, ck_h[(cbg * CBG + 3) * 4 + 3])
                            if cbb == 0:
                                prevfl = ck_fl.get((cbg, fcp - 2))
                                if prevfl is not None:
                                    dv.wait(t, prevfl)
                                elif cbg > 0:
                                    dv.wait(t, ck_fl[(cbg - 1, 7 - fcp % 2)])
                            gqw[2 + idx % 2].wait(t, ck_wd[(cbg, idx)])
                            for j in range(4):
                                for f01 in range(2):
                                    wdt = wdb[idx % 2][:, j * 256 + f01 * 128:
                                                       j * 256 + (f01 + 1) * 128]
                                    for th in range(2):
                                        pe.inc(t.matmul(
                                            pdl[:, f01 * 1024 + th * 512:
                                                f01 * 1024 + (th + 1) * 512],
                                            wdt,
                                            hsb[:, cbb * 4096 + j * 1024 + th * 512:
                                                cbb * 4096 + j * 1024 + (th + 1) * 512],
                                            start=(cbb == 0 and j == 0),
                                            stop=(cbb == CBG - 1 and j == 3)))
                            ck_dmm[(cbg, idx)] = pe.v
                            if cbb == CBG - 1:
                                ck_dmm[(cbg, "fcp%d" % fcp)] = pe.v

                    @blk.vector
                    def _(v, cbg=cbg, fcp=fcp, pdl=pdl):
                        pe.wait(v, ck_dmm[(cbg, "fcp%d" % fcp)])
                        for f01 in range(2):
                            for th in range(2):
                                fcn = fcp * 2 + f01
                                dst = acc[:, fcn * TOWN + th * 512:fcn * TOWN + (th + 1) * 512]
                                srcp = pdl[:, f01 * 1024 + th * 512:f01 * 1024 + (th + 1) * 512]
                                if cbg == 0:
                                    dv.inc(v.tensor_copy(dst, srcp))
                                else:
                                    dv.inc(v.tensor_tensor(dst, dst, srcp, OP.add))
                        ck_fl[(cbg, fcp)] = dv.v

                last_flush = dv.v
                last_dmm_cbg = pe.v

    P5ab.close()

    with contextlib.ExitStack() as ps:
        x2r = [ps.enter_context(nc.sbuf_tensor(f"x6r{i}", [P, TOWN], F32R)) for i in range(2)]
        sq6 = ps.enter_context(nc.sbuf_tensor("sq6", [P, TOWN], F32R))
        rw6 = ps.enter_context(nc.sbuf_tensor("rw6", [1, 3 * TOWN], F32R))
        m4b = ps.enter_context(nc.sbuf_tensor("m4b", [P, TOWN], F32))
        r4b = ps.enter_context(nc.sbuf_tensor("r4b", [P, TOWN], F32))
        scr6 = ps.enter_context(nc.sbuf_tensor("scr6", [P, TOWN], F32))
        ob = [ps.enter_context(nc.sbuf_tensor(f"ob{i}", [P, TOWN], F32)) for i in range(2)]
        pl6 = ps.enter_context(nc.psum_tensor("pl6", [1, 1024], F32))
        pq6 = ps.enter_context(nc.psum_tensor("pq6", [1, 1024], F32))
        pb6 = [ps.enter_context(nc.psum_tensor(f"pb6{i}", [P, 512], F32)) for i in range(4)]

        with nc.Block() as blk:
            ck_sqt = [0] * FI
            ck_smm = [0] * FI

            pend6 = [None]
            for fc in range(FI):
                @blk.tensor
                def _(t, fc=fc):
                    for th in range(2):
                        pe.inc(t.matmul(pl6[0:1, th * 512:(th + 1) * 512], onesc[:],
                                        acc[:, fc * TOWN + th * 512:fc * TOWN + (th + 1) * 512],
                                        start=(fc == 0), stop=(fc == FI - 1)))

                def mk6(fc=fc):
                    def emit():
                        @blk.tensor
                        def _(t, fc=fc):
                            dv.wait(t, ck_sqt[fc])
                            for th in range(2):
                                pe.inc(t.matmul(pq6[0:1, th * 512:(th + 1) * 512],
                                                onesc[:],
                                                sq6[:, th * 512:(th + 1) * 512],
                                                start=(fc == 0), stop=(fc == FI - 1)))
                            ck_smm[fc] = pe.v
                    return emit

                if pend6[0] is not None:
                    pend6[0]()
                pend6[0] = mk6()

                @blk.vector
                def _(v, fc=fc):
                    if fc >= 1:
                        pe.wait(v, ck_smm[fc - 1])
                    a_ = acc[:, fc * TOWN:(fc + 1) * TOWN]
                    ck_sqt[fc] = dv.inc(v.tensor_tensor(sq6[:], a_, a_, OP.mult))

            pend6[0]()

            @blk.vector
            def _(v):
                pe.wait(v, ck_smm[FI - 1])
                dv.inc(v.tensor_scalar(rw6[0:1, 0:1024], pl6[0:1, :], INV_DM, None, OP.mult))
                dv.wait(v)
                dv.inc(v.tensor_tensor(rw6[0:1, 1024:2048], rw6[0:1, 0:1024],
                                       rw6[0:1, 0:1024], OP.mult))
                dv.wait(v)
                dv.inc(v.scalar_tensor_tensor(rw6[0:1, 2048:3072], pq6[0:1, :], INV_DM,
                                              rw6[0:1, 1024:2048], OP.mult, OP.subtract))
                dv.wait(v)
                dv.inc(v.tensor_scalar(rw6[0:1, 2048:3072], rw6[0:1, 2048:3072], EPS_LN,
                                       None, OP.add))

            ck_r1 = dv.v

            @blk.scalar
            def _(s):
                dv.wait(s, ck_r1)
                ac.inc(s.activation(rw6[0:1, 1024:2048], rw6[0:1, 2048:3072], AF.Ln))
                ac.wait(s)
                ac.inc(s.activation(rw6[0:1, 2048:3072], rw6[0:1, 1024:2048], AF.Exp, scale=-0.5))

            ck_r2 = ac.v

            @blk.tensor
            def _(t):
                ac.wait(t, ck_r2)
                for th in range(2):
                    pe.inc(t.matmul(pb6[th][:], onesr[:],
                                    rw6[0:1, th * 512:(th + 1) * 512], start=True, stop=True))
                    pe.inc(t.matmul(pb6[2 + th][:], onesr[:],
                                    rw6[0:1, 2048 + th * 512:2048 + (th + 1) * 512],
                                    start=True, stop=True))

            ck_bc = pe.v

            @blk.scalar
            def _(s):
                pe.wait(s, ck_bc)
                for th in range(2):
                    ac.inc(s.activation(m4b[:, th * 512:(th + 1) * 512], pb6[th][:], AF.Copy))
                    ac.inc(s.activation(r4b[:, th * 512:(th + 1) * 512], pb6[2 + th][:], AF.Copy))

            ck_m4 = ac.v

            ck_x2l = [0] * FI
            ck_ow = [0] * FI
            ck_ost = [0] * FI
            for fc in range(FI):
                par = fc % 2

                @blk.sync
                def _(s, fc=fc, par=par):
                    if fc >= 2:
                        dv.wait(s, ck_ow[fc - 2])
                        ldp[par].wait(s, ck_ost[fc - 2])
                    ck_x2l[fc] = ldr[par].dma(s.dma_start(x2r[par][:], d_x2.ap()[fc]))

                @blk.vector
                def _(v, fc=fc, par=par):
                    ac.wait(v, ck_m4)
                    dv.wait(v)
                    dv.inc(v.tensor_tensor(scr6[:], acc[:, fc * TOWN:(fc + 1) * TOWN],
                                           m4b[:], OP.subtract))
                    dv.wait(v)
                    dv.inc(v.tensor_tensor(scr6[:], scr6[:], r4b[:], OP.mult))
                    dv.wait(v)
                    ldr[par].wait(v, ck_x2l[fc])
                    dv.inc(v.tensor_tensor(ob[par][:], scr6[:], x2r[par][:], OP.add))
                    ck_ow[fc] = dv.v

                @blk.sync
                def _(s, fc=fc, par=par):
                    dv.wait(s, ck_ow[fc])
                    ck_ost[fc] = ldp[par].dma(s.dma_start(d_out.ap()[fc], ob[par][:]))

            @blk.sync
            def _(s):
                ldp[0].wait(s)
                ldp[1].wait(s)

    P56.close()
    es.close()
    _regroup_blocks(nc)
    return nc


def _regroup_blocks(nc):
    ORDER = {"EngineType.SP": 0, "EngineType.Pool": 1, "EngineType.PE": 2,
             "EngineType.DVE": 3, "EngineType.Activation": 4}
    f = nc.m.functions[0]
    out, seg = [], []

    def flush():
        seg.sort(key=lambda pair: pair[0])
        out.extend(b for _, b in seg)
        seg.clear()

    for b in f.blocks:
        engs = {str(i.engine) for i in b.instructions}
        if len(engs) == 1:
            seg.append((ORDER.get(next(iter(engs)), 99), b))
        else:
            flush()
            out.append(b)
    flush()
    f.blocks = out


def _prep_shared(Wq, Wk, Wv, Wo, Wg, Wu, Wd):
    sh = {}
    sh["wq"] = np.ascontiguousarray(
        Wq.T.reshape(FI, P, 4, 512).transpose(2, 0, 1, 3))
    sh["wk"] = np.ascontiguousarray(Wk.T.reshape(FI, P, 512).transpose(1, 0, 2))
    sh["wv"] = np.ascontiguousarray(Wv.T.reshape(FI, P, 512).transpose(1, 0, 2))
    sh["wo"] = np.ascontiguousarray(
        Wo.T.reshape(FI, P, FI, P).transpose(2, 1, 0, 3))
    cb = np.arange(NCB)[:, None, None]
    jj = np.arange(4)[None, :, None]
    pp = np.arange(P)[None, None, :]
    idx = (4 * (P * cb + pp) + jj).reshape(-1)
    wgP = Wg[:, idx]
    wuP = Wu[:, idx]
    wdP = Wd[idx, :]
    sh["wg"] = np.ascontiguousarray(
        wgP.reshape(FI, P, NCB, 4, P).transpose(2, 3, 1, 0, 4))
    sh["wu"] = np.ascontiguousarray(
        wuP.reshape(FI, P, NCB, 4, P).transpose(2, 3, 1, 0, 4))
    sh["wd"] = np.ascontiguousarray(
        wdP.reshape(NCB, 4, P, 8, 256).transpose(3, 0, 2, 1, 4))
    return sh


def _prep_consts():
    import ml_dtypes
    c = {}
    iv64 = np.exp(np.arange(64, dtype=np.float64) * (-math.log(10000.0) / 64.0))
    hi = iv64.astype(np.float32)
    lo = (iv64 - hi.astype(np.float64)).astype(np.float32)
    c["ivh"] = np.ascontiguousarray(np.broadcast_to(hi[None, :], (P, 64)))
    c["ivl"] = np.ascontiguousarray(np.broadcast_to(lo[None, :], (P, 64)))
    c["identf"] = np.eye(P, dtype=np.float32)
    c["identb"] = np.eye(P, dtype=np.float32).astype(ml_dtypes.bfloat16)
    cm = (np.arange(P)[:, None] <= np.arange(P)[None, :]).astype(np.float32)
    c["cmask4"] = np.ascontiguousarray(np.tile(cm, (1, 4)))
    c["onesr"] = np.ones((1, P), dtype=np.float32)
    c["onescol"] = np.ones((P, 1), dtype=np.float32)
    c["onescolb"] = np.ones((P, 1), dtype=np.float32).astype(ml_dtypes.bfloat16)
    return c


def _prep_core(x, core, consts):
    import ml_dtypes
    b, q = core // 4, core % 4
    start = q * TOWN
    xt = np.zeros((NCH, P, FI, P), dtype=np.float32)
    pos = np.zeros((P, NCH), dtype=np.float32)
    mask = np.zeros((P, NCH), dtype=np.float32)
    for c in range(NCH):
        t0 = start - TPREF + c * P
        pos[:, c] = np.maximum(t0 + np.arange(P), 0)
        if t0 + P <= 0:
            continue
        lo = max(t0, 0)
        hit = t0 + P
        seg = x[b, lo:hit, :]
        n = seg.shape[0]
        tile = np.zeros((DM, P), dtype=np.float32)
        tile[:, P - n:] = seg.T
        xt[c] = tile.reshape(FI, P, P).transpose(1, 0, 2)
        mask[:, c][P - n:] = 1.0
    m = dict(consts)
    m["xt"] = xt
    m["pos"] = pos
    m["posrep"] = np.ascontiguousarray(np.repeat(pos, 64, axis=1))
    m["maskc"] = mask.astype(ml_dtypes.bfloat16)
    return m


def kernel(L0_attn_x, L0_attn_Wq, L0_attn_Wk, L0_attn_Wv, L0_attn_Wo,
           L0_ffn_w_gate, L0_ffn_w_up, L0_ffn_w_down):
    x = np.ascontiguousarray(np.asarray(L0_attn_x, dtype=np.float32))
    sh = _prep_shared(np.asarray(L0_attn_Wq, dtype=np.float32),
                      np.asarray(L0_attn_Wk, dtype=np.float32),
                      np.asarray(L0_attn_Wv, dtype=np.float32),
                      np.asarray(L0_attn_Wo, dtype=np.float32),
                      np.asarray(L0_ffn_w_gate, dtype=np.float32),
                      np.asarray(L0_ffn_w_up, dtype=np.float32),
                      np.asarray(L0_ffn_w_down, dtype=np.float32))
    consts = _prep_consts()
    consts.update(sh)
    in_maps = [_prep_core(x, c, consts) for c in range(8)]
    nc = build_program()
    res = run_bass_kernel_spmd(nc, in_maps, core_ids=list(range(8)))
    out = np.empty((2, 4096, DM), dtype=np.float32)
    for c in range(8):
        b, q = c // 4, c % 4
        o = np.asarray(res.results[c]["outT"])
        out[b, q * TOWN:(q + 1) * TOWN, :] = o.reshape(DM, TOWN).T
    return out
